# version 05
# baseline (speedup 1.0000x reference)
import numpy as np
import ml_dtypes
import concourse.bass as bass
import concourse.mybir as mybir
from concourse.bass_utils import run_bass_kernel_spmd
from contextlib import ExitStack

F32 = mybir.dt.float32
BF16 = mybir.dt.bfloat16
I32 = mybir.dt.int32
AF = mybir.ActivationFunctionType
ALU = mybir.AluOpType
AX = mybir.AxisListType
bfdt = ml_dtypes.bfloat16

B, T, D = 4, 2048, 4096
H, HD = 32, 128
HH = 11008
M = 1024
NSLOT = 8
C_ATT = 1.0 / float(np.sqrt(128.0))
DB = 32
HB = 86
NT = 24

DEBUG_TAPS = False
BUILD_UPTO = "all"
DETECT_RACES = True


def cdv(a, b):
    return (a + b - 1) // b


class Builder:
    def __init__(self, events=None):
        self.nc = nc = bass.Bass("TRN2", target_bir_lowering=False,
                                 detect_race_conditions=DETECT_RACES)
        self.events = events
        self.rec = {}
        self.sem = {}
        self.cnt = {}
        for nm in ("t", "v", "s", "g", "q"):
            self.sem[nm] = nc.alloc_semaphore("sem_" + nm)
            self.cnt[nm] = 0

    def mark(self, instr, key, name=None, k=1):
        instr.then_inc(self.sem[key], k)
        self.cnt[key] += k
        if name is not None:
            assert name not in self.rec, "dup event " + name
            self.rec[name] = (key, self.cnt[key])

    def evhere(self, name, key):
        if name not in self.rec:
            self.rec[name] = (key, self.cnt[key])

    def wait(self, engine, name):
        if self.events is None:
            return
        key, n = self.events[name]
        if n > 0:
            engine.wait_ge(self.sem[key], n)

    def gate(self, engine, names):
        for nm in names:
            self.wait(engine, nm)

    def vsp(self, v, n=2):
        for _ in range(n):
            v.tensor_scalar(out=self.junk[:, 0:8], in0=self.junk[:, 0:8],
                            scalar1=1.0, scalar2=None, op0=ALU.mult)

    def build(self):
        nc = self.nc
        p = self

        d_xkv = nc.dram_tensor("xkv", [T, D], F32, kind="ExternalInput").ap()
        d_xown = nc.dram_tensor("xown", [M, D], F32, kind="ExternalInput").ap()
        d_wqT = nc.dram_tensor("wqT", [D, D], F32, kind="ExternalInput").ap()
        d_wkT = nc.dram_tensor("wkT", [D, HD], F32, kind="ExternalInput").ap()
        d_wvT = nc.dram_tensor("wvT", [D, HD], F32, kind="ExternalInput").ap()
        d_woT = nc.dram_tensor("woT", [D, D], F32, kind="ExternalInput").ap()
        d_wqb = nc.dram_tensor("wqb", [128, 32], F32, kind="ExternalInput").ap()
        d_wob = nc.dram_tensor("wob", [128, 32], F32, kind="ExternalInput").ap()
        d_wkb = nc.dram_tensor("wkb", [128, 1], F32, kind="ExternalInput").ap()
        d_wvb = nc.dram_tensor("wvb", [128, 1], F32, kind="ExternalInput").ap()
        d_w1 = nc.dram_tensor("w1p", [352256, 64], I32, kind="ExternalInput").ap()
        d_w2 = nc.dram_tensor("w2p", [352256, 64], I32, kind="ExternalInput").ap()
        d_w3 = nc.dram_tensor("w3p", [352256, 64], I32, kind="ExternalInput").ap()
        d_s1 = nc.dram_tensor("s1", [704512], F32, kind="ExternalInput").ap()
        d_s2 = nc.dram_tensor("s2", [704512], F32, kind="ExternalInput").ap()
        d_s3 = nc.dram_tensor("s3", [704512], F32, kind="ExternalInput").ap()
        d_mask = nc.dram_tensor("mask", [128, 256], F32, kind="ExternalInput").ap()
        d_out = nc.dram_tensor("out", [M, D], F32, kind="ExternalOutput").ap()
        d_ident = nc.inline_tensor(np.eye(128, dtype=bfdt), "ident_const").ap()

        d_xnT = nc.dram_tensor("xnT_sc", [128, DB * M], BF16, kind="Internal").ap()
        d_sT = nc.dram_tensor("sT_sc", [128, DB * M], BF16, kind="Internal").ap()
        d_w1t = nc.dram_tensor("w1t_sc", [HB, 128, D], BF16, kind="Internal").ap()
        d_w3t = nc.dram_tensor("w3t_sc", [HB, 128, D], BF16, kind="Internal").ap()
        d_w2t = nc.dram_tensor("w2t_sc", [DB, 128, HH], BF16, kind="Internal").ap()

        taps = {}
        if DEBUG_TAPS:
            taps["kT"] = nc.dram_tensor("tap_kT", [128, T], BF16, kind="ExternalOutput").ap()
            taps["vtok"] = nc.dram_tensor("tap_vtok", [128, T], BF16, kind="ExternalOutput").ap()
            taps["xnT"] = nc.dram_tensor("tap_xnT", [128, DB * M], BF16, kind="ExternalOutput").ap()
            taps["attnT"] = nc.dram_tensor("tap_attnT", [128, H * M], BF16, kind="ExternalOutput").ap()
            taps["sT"] = nc.dram_tensor("tap_sT", [128, DB * M], BF16, kind="ExternalOutput").ap()
            taps["hT"] = nc.dram_tensor("tap_hT", [128, HB * 512], BF16, kind="ExternalOutput").ap()
            taps["gT"] = nc.dram_tensor("tap_gT", [128, DB * 512], BF16, kind="ExternalOutput").ap()
            taps["st"] = nc.dram_tensor("tap_st", [128, 160], F32, kind="ExternalOutput").ap()
            taps["QT"] = nc.dram_tensor("tap_QT", [128, 2048], BF16, kind="ExternalOutput").ap()
            taps["probs"] = nc.dram_tensor("tap_probs", [128, 2048], BF16, kind="ExternalOutput").ap()
            taps["pT"] = nc.dram_tensor("tap_pT", [128, 2048], BF16, kind="ExternalOutput").ap()
            taps["sm"] = nc.dram_tensor("tap_sm", [128, 64], F32, kind="ExternalOutput").ap()
        self.taps = taps

        ctx = ExitStack()
        with ctx:
            S = lambda name, shape, dt: ctx.enter_context(nc.sbuf_tensor(name, shape, dt))
            ident = S("c_ident", [128, 128], BF16)
            mask = S("c_mask", [128, 256], F32)
            wqb = S("c_wqb", [128, 32], F32)
            wob = S("c_wob", [128, 32], F32)
            wkb = S("c_wkb", [128, 1], F32)
            wvb = S("c_wvb", [128, 1], F32)
            self.junk = S("c_junk", [128, 8], F32)
            s1w = S("c_s1w", [128, 32], F32)
            s2w = S("c_s2w", [128, 32], F32)
            mnw = S("c_mnw", [128, 32], F32)
            vrw = S("c_vrw", [128, 32], F32)
            rsw = S("c_rsw", [128, 32], F32)
            ssum = S("c_ssum", [128, 16], F32)
            srec = S("c_srec", [128, 16], F32)
            nmaxb = S("c_nmax", [128, 16], F32)
            nbias = S("c_nbias", [128, 16], F32)

            psA = ctx.enter_context(nc.psum_tensor("psA", [128, 2048], F32))
            psB = ctx.enter_context(nc.psum_tensor("psB", [128, 2048], BF16))
            psC = ctx.enter_context(nc.psum_tensor("psC", [128, 1024], F32))

            blk = ctx.enter_context(nc.Block())
            self.blk = blk

            with ExitStack() as cAB:
                kT = cAB.enter_context(nc.sbuf_tensor("c_kT", [128, T], BF16))
                vtok = cAB.enter_context(nc.sbuf_tensor("c_vtok", [128, 16 * 128], BF16))
                xnownT = cAB.enter_context(nc.sbuf_tensor("b_xnownT", [128, DB, M], BF16))

                with ExitStack() as c1:
                    S1 = lambda n_, sh, dt: c1.enter_context(nc.sbuf_tensor(n_, sh, dt))
                    xf = S1("p1_xf", [128, 3, D], F32)
                    wkT = S1("c_wkT", [128, DB * 128], BF16)
                    wvT = S1("c_wvT", [128, DB * 128], BF16)
                    xnb = S1("p1_xnb", [128, 2, D], BF16)
                    xnkv = S1("p1_xnkv", [128, DB, 512], BF16)
                    vT = S1("p1_vT", [128, T], BF16)
                    self.emit_p1(d_xkv, d_xown, d_wkT, d_wvT, d_ident, d_mask,
                                 d_wqb, d_wob, d_wkb, d_wvb,
                                 ident, mask, wqb, wob, wkb, wvb, wkT, wvT, kT,
                                 vtok, s1w, s2w, mnw, vrw, rsw,
                                 xnownT, xf, xnb, xnkv, vT, psA, psB)

                P1_FIN = ["vtrr_1", "vtev_1", "vev_3", "ld_x_23"]
                if DEBUG_TAPS:
                    def p1_taps(gp):
                        p.gate(gp, P1_FIN)
                        p.mark(gp.dma_start(out=taps["st"][:, 0:32], in_=s1w[:]), "g", None, 16)
                        p.mark(gp.dma_start(out=taps["st"][:, 32:64], in_=s2w[:]), "g", None, 16)
                        p.mark(gp.dma_start(out=taps["st"][:, 64:96], in_=mnw[:]), "g", None, 16)
                        p.mark(gp.dma_start(out=taps["st"][:, 96:128], in_=vrw[:]), "g", None, 16)
                        p.mark(gp.dma_start(out=taps["st"][:, 128:160], in_=rsw[:]), "g", None, 16)
                        p.mark(gp.dma_start(out=taps["kT"], in_=kT[:]), "g", None, 16)
                        p.mark(gp.dma_start(out=taps["vtok"], in_=vtok[:]), "g", None, 16)
                        p.mark(gp.dma_start(out=taps["xnT"],
                                            in_=xnownT[:].rearrange("p b m -> p (b m)")),
                               "g", None, 16)
                    blk.gpsimd(p1_taps)
                if BUILD_UPTO == "p1":
                    def p_final0(gp):
                        gp.wait_ge(self.sem["g"], self.cnt["g"])
                    blk.gpsimd(p_final0)
                    return self.nc

                attnT = cAB.enter_context(nc.sbuf_tensor("b_attnT", [128, H * M], BF16))

                with ExitStack() as c2:
                    S2 = lambda n_, sh, dt: c2.enter_context(nc.sbuf_tensor(n_, sh, dt))
                    wq_f = S2("p2_wqf", [128, D], F32)
                    wq_b = S2("p2_wqb", [128, 2, D], BF16)
                    QT = S2("p2_QT", [128, 2, M], BF16)
                    probs = S2("p2_probs", [128, 2, 2048], BF16)
                    probsT = S2("p2_probsT", [128, 2, 2048], BF16)
                    self.emit_p2(P1_FIN, d_wqT, d_xnT, ident, mask, wqb,
                                 kT, vtok, xnownT, attnT,
                                 ssum, srec, nmaxb, nbias,
                                 wq_f, wq_b, QT, probs, probsT, psA, psB, psC)

                P2_FIN = ["avmm_31_7", "pev_31_7_1", "avev_31_7", "ldq_31", "gxn"]
                if DEBUG_TAPS:
                    def p2_taps(gp):
                        p.gate(gp, P2_FIN)
                        p.mark(gp.dma_start(out=taps["QT"][:, 0:1024], in_=QT[:, 1, :]), "g", None, 16)
                        p.mark(gp.dma_start(out=taps["probs"], in_=probs[:, 1, :]), "g", None, 16)
                        p.mark(gp.dma_start(out=taps["pT"], in_=probsT[:, 1, :]), "g", None, 16)
                        p.mark(gp.dma_start(out=taps["sm"][:, 0:16], in_=ssum[:]), "g", None, 16)
                        p.mark(gp.dma_start(out=taps["sm"][:, 16:32], in_=srec[:]), "g", None, 16)
                        p.mark(gp.dma_start(out=taps["sm"][:, 32:48], in_=nmaxb[:]), "g", None, 16)
                        p.mark(gp.dma_start(out=taps["sm"][:, 48:64], in_=nbias[:]), "g", None, 16)
                        p.mark(gp.dma_start(out=taps["attnT"], in_=attnT[:]), "g", None, 16)
                    blk.gpsimd(p2_taps)
                if BUILD_UPTO == "p2":
                    def p_final0(gp):
                        gp.wait_ge(self.sem["g"], self.cnt["g"])
                    blk.gpsimd(p_final0)
                    return self.nc

                with ExitStack() as c3:
                    S3 = lambda n_, sh, dt: c3.enter_context(nc.sbuf_tensor(n_, sh, dt))
                    wo_f = S3("p3_wof", [128, D], F32)
                    wo_b = S3("p3_wob", [128, 2, D], BF16)
                    sst = S3("p3_sst", [128, 2, M], BF16)
                    self.emit_p3(P2_FIN, d_woT, d_sT, wob, attnT,
                                 wo_f, wo_b, sst, psA)

            P3_FIN = ["womm_31_1", "cvo_31", "sev_31_1", "ldo_31", "gsT_31"]
            if BUILD_UPTO == "p3":
                def p_final0(gp):
                    gp.wait_ge(self.sem["g"], self.cnt["g"])
                blk.gpsimd(p_final0)
                return self.nc

            with ExitStack() as c4:
                hT = c4.enter_context(nc.sbuf_tensor("b_hT", [128, HB, 512], BF16))
                prev = P3_FIN
                for mc in range(2):
                    with ExitStack() as cm1:
                        Sm = lambda n_, sh, dt: cm1.enter_context(
                            nc.sbuf_tensor(n_ + "_%d" % mc, sh, dt))
                        xnsl = Sm("p4_xnsl", [128, DB, 512], BF16)
                        pkr = Sm("p4_pkr", [128, 2, 2048], I32)
                        sci1 = Sm("p4_sci1", [128, 2048], I32)
                        sci3 = Sm("p4_sci3", [128, 2048], I32)
                        nat1 = Sm("p4_nat1", [128, D], BF16)
                        nat3 = Sm("p4_nat3", [128, D], BF16)
                        w1t = Sm("p4_w1t", [128, 2, D], BF16)
                        w3t = Sm("p4_w3t", [128, 2, D], BF16)
                        sc1 = Sm("p4_sc1", [128, 2, 64], F32)
                        sc3 = Sm("p4_sc3", [128, 2, 64], F32)
                        asil = Sm("p4_asil", [128, 2, 512], BF16)
                        bsb = Sm("p4_bsb", [128, 2, 512], BF16)
                        prev = self.emit_mm1(mc, prev, d_w1, d_w3, d_s1, d_s3,
                                             d_w1t, d_w3t, d_xnT, hT, xnsl, pkr,
                                             sci1, sci3, nat1, nat3, w1t, w3t,
                                             sc1, sc3, asil, bsb, psA, psB, ident)
                    if BUILD_UPTO == "mm1":
                        break
                    with ExitStack() as cm2:
                        gT = cm2.enter_context(
                            nc.sbuf_tensor("p5_gT_%d" % mc, [128, DB, 512], BF16))
                        with ExitStack() as cm2i:
                            Si = lambda n_, sh, dt: cm2i.enter_context(
                                nc.sbuf_tensor(n_ + "_%d" % mc, sh, dt))
                            pk2 = Si("p5_pk2", [128, 2, 2752], I32)
                            sci2 = Si("p5_sci2", [128, 2752], I32)
                            nat2 = Si("p5_nat2", [128, 5504], BF16)
                            w2t = Si("p5_w2t", [128, 2, HH], BF16)
                            sc2 = Si("p5_sc2", [128, 2, 172], F32)
                            prev = self.emit_mm2(mc, prev, d_w2, d_s2, d_w2t, hT,
                                                 pk2, sci2, nat2, w2t, sc2, gT,
                                                 psA, psB, ident)
                        if BUILD_UPTO == "mm2":
                            break
                        with ExitStack() as cm3:
                            Sc = lambda n_, sh, dt: cm3.enter_context(
                                nc.sbuf_tensor(n_ + "_%d" % mc, sh, dt))
                            ssl = Sc("p6_ssl", [128, DB, 512], BF16)
                            xcf = Sc("p6_xcf", [128, 2, D], F32)
                            outst = Sc("p6_outst", [128, 2, 512], F32)
                            prev = self.emit_combine(mc, prev, d_sT, d_xown,
                                                     d_out, gT, ssl, xcf,
                                                     outst, psB, ident)
                    if BUILD_UPTO in ("mm1", "mm2", "cb0"):
                        break

                def p_final(gp):
                    gp.wait_ge(self.sem["g"], self.cnt["g"])
                blk.gpsimd(p_final)

        return self.nc

    def emit_p1(self, d_xkv, d_xown, d_wkT, d_wvT, d_ident, d_mask,
                d_wqb, d_wob, d_wkb, d_wvb,
                ident, mask, wqb, wob, wkb, wvb, wkT, wvT, kT,
                vtok, s1w, s2w, mnw, vrw, rsw,
                xnownT, xf, xnb, xnkv, vT, psA, psB):
        p = self
        blk = self.blk

        def sync_f(sync):
            for dst, src in ((ident, d_ident), (mask, d_mask), (wqb, d_wqb),
                             (wob, d_wob), (wkb, d_wkb), (wvb, d_wvb)):
                p.mark(sync.dma_start(out=dst[:], in_=src), "q", None, 16)
            p.mark(sync.dma_start(
                out=xf[:, 0, :].rearrange("p (b f) -> p b f", b=DB),
                in_=d_wkT.rearrange("(b p) f -> p b f", p=128)),
                "q", "ld_wk", 16)
            p.mark(sync.dma_start(
                out=xf[:, 1, :].rearrange("p (b f) -> p b f", b=DB),
                in_=d_wvT.rearrange("(b p) f -> p b f", p=128)),
                "q", "ld_wv", 16)
            for t in range(NT):
                if t == 0:
                    p.wait(sync, "cv_wk")
                elif t == 1:
                    p.wait(sync, "cv_wv")
                else:
                    if t >= 3:
                        p.wait(sync, "stt_%d" % (t - 3))
                        p.wait(sync, "sq_%d" % (t - 3))
                src = (d_xkv[t * 128:(t + 1) * 128, :] if t < 16
                       else d_xown[(t - 16) * 128:(t - 15) * 128, :])
                p.mark(sync.dma_start(out=xf[:, t % 3, :], in_=src),
                       "q", "ld_x_%d" % t, 16)

        def scalar_f(scalar):
            for t in range(NT):
                p.wait(scalar, "ld_x_%d" % t)
                if t >= 2:
                    p.wait(scalar, "xtr_%d" % (t - 2))
                p.mark(scalar.activation(out=xnb[:, t % 2, :], in_=xf[:, t % 3, :],
                                         func=AF.Square, bias=0.0, scale=1.0,
                                         accum_out=s2w[:, t:t + 1]),
                       "s", "sq_%d" % t)
                if t % 2 == 1:
                    g = t // 2
                    c0 = g * 2
                    p.wait(scalar, "var_%d" % g)
                    p.mark(scalar.activation(out=rsw[:, c0:c0 + 2],
                                             in_=vrw[:, c0:c0 + 2],
                                             func=AF.Sqrt, bias=0.0, scale=1.0),
                           "s", "sqrt_%d" % g)
                if t < 16 and t % 4 == 3:
                    mc4 = t // 4
                    p.wait(scalar, "kmm_%d" % mc4)
                    p.mark(scalar.activation(out=kT[:, mc4 * 512:(mc4 + 1) * 512],
                                             in_=psA[:, 0:512], func=AF.Identity,
                                             bias=wkb[:], scale=1.0),
                           "s", "kev_%d" % mc4)
                    p.wait(scalar, "vmm_%d" % mc4)
                    p.mark(scalar.activation(out=vT[:, mc4 * 512:(mc4 + 1) * 512],
                                             in_=psA[:, 512:1024], func=AF.Identity,
                                             bias=wvb[:], scale=1.0),
                           "s", "vev_%d" % mc4)

        def vector_f(vector):
            vector.memset(self.junk[:], 0.0)
            for buf in (s1w, s2w, mnw, vrw, rsw):
                vector.memset(buf[:], 0.0)
            p.wait(vector, "ld_wk")
            p.mark(vector.tensor_copy(out=wkT[:], in_=xf[:, 0, :]), "v", "cv_wk")
            p.wait(vector, "ld_wv")
            p.mark(vector.tensor_copy(out=wvT[:], in_=xf[:, 1, :]), "v", "cv_wv")
            for t in range(NT):
                p.wait(vector, "ld_x_%d" % t)
                p.mark(vector.reduce_sum(out=s1w[:, t:t + 1], in_=xf[:, t % 3, :],
                                         axis=AX.X), "v", "rsum_%d" % t)
                if t % 2 != 1:
                    continue
                g = t // 2
                c0 = g * 2
                p.wait(vector, "sq_%d" % t)
                self.vsp(vector)
                vector.tensor_scalar(out=mnw[:, c0:c0 + 2], in0=s1w[:, c0:c0 + 2],
                                     scalar1=1.0 / D, scalar2=None, op0=ALU.mult)
                vector.tensor_scalar(out=vrw[:, c0:c0 + 2], in0=s2w[:, c0:c0 + 2],
                                     scalar1=1.0 / D, scalar2=None, op0=ALU.mult)
                vector.tensor_mul(out=s2w[:, c0:c0 + 2], in0=mnw[:, c0:c0 + 2],
                                  in1=mnw[:, c0:c0 + 2])
                self.vsp(vector)
                p.mark(vector.scalar_tensor_tensor(
                    out=vrw[:, c0:c0 + 2], in0=vrw[:, c0:c0 + 2], scalar=1e-5,
                    in1=s2w[:, c0:c0 + 2], op0=ALU.add, op1=ALU.subtract),
                    "v", "var_%d" % g)
                p.wait(vector, "sqrt_%d" % g)
                vector.reciprocal(out=rsw[:, c0:c0 + 2], in_=rsw[:, c0:c0 + 2])
                self.vsp(vector)
                for tt in (t - 1, t):
                    p.mark(vector.scalar_tensor_tensor(
                        out=xnb[:, tt % 2, :], in0=xf[:, tt % 3, :],
                        scalar=mnw[:, tt:tt + 1],
                        in1=rsw[:, tt:tt + 1].broadcast_to([128, D]),
                        op0=ALU.subtract, op1=ALU.mult), "v", "stt_%d" % tt)
                    for r in range(4):
                        p.wait(vector, "xtrr_%d_%d" % (tt, r))
                        if tt < 16:
                            if tt >= 4:
                                p.wait(vector, "vmm_%d" % (tt // 4 - 1))
                            dst = xnkv[:, r * 8:(r + 1) * 8,
                                       (tt % 4) * 128:(tt % 4) * 128 + 128]
                        else:
                            dst = xnownT[:, r * 8:(r + 1) * 8,
                                         (tt - 16) * 128:(tt - 15) * 128]
                        p.mark(vector.tensor_copy(
                            out=dst,
                            in_=psB[:, (r % 2) * 1024:(r % 2) * 1024 + 1024]
                            .rearrange("p (b f) -> p b f", b=8)),
                            "v", "xev_%d_%d" % (tt, r))
            for r in range(2):
                p.wait(vector, "vtrr_%d" % r)
                p.mark(vector.tensor_copy(
                    out=vtok[:, r * 1024:(r + 1) * 1024],
                    in_=psB[:, (r % 2) * 1024:(r % 2) * 1024 + 1024]),
                    "v", "vtev_%d" % r)

        def tensor_f(tensor):
            for t in range(NT):
                p.wait(tensor, "stt_%d" % t)
                for r in range(4):
                    rid = t * 4 + r
                    if rid >= 2:
                        p.wait(tensor, "xev_%d_%d" % ((rid - 2) // 4, (rid - 2) % 4))
                    last = None
                    for qq in range(8):
                        q = r * 8 + qq
                        last = tensor.transpose(
                            psB[:, (r % 2) * 1024 + qq * 128:
                                (r % 2) * 1024 + (qq + 1) * 128],
                            xnb[:, t % 2, q * 128:(q + 1) * 128], ident[:])
                    p.mark(last, "t", "xtrr_%d_%d" % (t, r))
                self.evhere("xtr_%d" % t, "t")
                if t < 16 and t % 4 == 3:
                    mc4 = t // 4
                    p.wait(tensor, "cv_wk")
                    p.wait(tensor, "cv_wv")
                    p.wait(tensor, "xev_%d_3" % t)
                    if mc4 >= 1:
                        p.wait(tensor, "kev_%d" % (mc4 - 1))
                        p.wait(tensor, "vev_%d" % (mc4 - 1))
                    last = None
                    for dd in range(DB):
                        last = tensor.matmul(psA[:, 0:512],
                                             wkT[:, dd * 128:(dd + 1) * 128],
                                             xnkv[:, dd, :],
                                             start=(dd == 0), stop=(dd == DB - 1))
                    p.mark(last, "t", "kmm_%d" % mc4)
                    last = None
                    for dd in range(DB):
                        last = tensor.matmul(psA[:, 512:1024],
                                             wvT[:, dd * 128:(dd + 1) * 128],
                                             xnkv[:, dd, :],
                                             start=(dd == 0), stop=(dd == DB - 1))
                    p.mark(last, "t", "vmm_%d" % mc4)
            for r in range(2):
                p.wait(tensor, "vev_3")
                p.wait(tensor, "xev_23_3")
                if r == 1:
                    p.wait(tensor, "vtev_0")
                last = None
                for b8 in range(8):
                    kb = r * 8 + b8
                    last = tensor.transpose(
                        psB[:, (r % 2) * 1024 + b8 * 128:(r % 2) * 1024 + (b8 + 1) * 128],
                        vT[:, kb * 128:(kb + 1) * 128], ident[:])
                p.mark(last, "t", "vtrr_%d" % r)

        blk.sync(sync_f)
        blk.scalar(scalar_f)
        blk.vector(vector_f)
        blk.tensor(tensor_f)

    def emit_p2(self, FIN, d_wqT, d_xnT, ident, mask, wqb, kT, vtok, xnownT,
                attnT, ssum, srec, nmaxb, nbias, wq_f, wq_b, QT, probs,
                probsT, psA, psB, psC):
        p = self
        blk = self.blk
        taps = self.taps

        def nrounds(i):
            return cdv(2 * i + 2, 8)

        def gp_f(gp):
            p.gate(gp, FIN)
            p.mark(gp.dma_start(out=d_xnT,
                                in_=xnownT[:].rearrange("p b m -> p (b m)")),
                   "g", "gxn", 16)

        def sync_f(sync):
            p.gate(sync, FIN)
            for h in range(H):
                if h >= 1:
                    p.wait(sync, "cvq_%d" % (h - 1))
                p.mark(sync.dma_start(
                    out=wq_f[:].rearrange("p (b f) -> p b f", b=DB),
                    in_=d_wqT[:, h * 128:(h + 1) * 128]
                    .rearrange("(b p) f -> p b f", p=128)),
                    "q", "ldq_%d" % h, 16)

        def vector_f(vector):
            p.gate(vector, FIN)
            for h in range(H):
                p.wait(vector, "ldq_%d" % h)
                if h >= 2:
                    p.wait(vector, "qmm_%d_1" % (h - 2))
                p.mark(vector.tensor_copy(out=wq_b[:, h % 2, :], in_=wq_f[:]),
                       "v", "cvq_%d" % h)
                for i in range(NSLOT):
                    ii = (h * NSLOT + i) % 16
                    Ki = 256 * (i + 1)
                    p.wait(vector, "smm_%d_%d" % (h, i))
                    vector.tensor_add(out=psA[:, Ki - 256:Ki],
                                      in0=psA[:, Ki - 256:Ki], in1=mask[:])
                    if i >= 1:
                        p.wait(vector, "exp_%d_%d" % (h, i - 1))
                        vector.reciprocal(out=srec[:, ii - 1:ii],
                                          in_=ssum[:, ii - 1:ii])
                    else:
                        self.vsp(vector)
                    vector.reduce_max(out=nmaxb[:, ii:ii + 1], in_=psA[:, 0:Ki],
                                      axis=AX.X, negate=True)
                    if i >= 1:
                        Kp = 256 * i
                        p.mark(vector.tensor_scalar(
                            out=probs[:, (i - 1) % 2, 0:Kp],
                            in0=probs[:, (i - 1) % 2, 0:Kp],
                            scalar1=srec[:, ii - 1:ii], scalar2=None,
                            op0=ALU.mult), "v", "vnorm_%d_%d" % (h, i - 1))
                    else:
                        self.vsp(vector)
                    p.mark(vector.tensor_scalar(
                        out=nbias[:, ii:ii + 1], in0=nmaxb[:, ii:ii + 1],
                        scalar1=C_ATT, scalar2=None, op0=ALU.mult),
                        "v", "vnb_%d_%d" % (h, i))
                    if i >= 1:
                        if i >= 3:
                            p.wait(vector, "avmm_%d_%d" % (h, i - 3))
                        elif h >= 1:
                            p.wait(vector, "avmm_%d_%d" % (h - 1, 5 + i))
                        for r in range(nrounds(i - 1)):
                            nb_r = min(8, 2 * i - r * 8)
                            p.wait(vector, "ttr_%d_%d_%d" % (h, i - 1, r))
                            p.mark(vector.tensor_copy(
                                out=probsT[:, (i - 1) % 2,
                                           r * 1024:r * 1024 + nb_r * 128],
                                in_=psB[:, (r % 2) * 1024:
                                        (r % 2) * 1024 + nb_r * 128]),
                                "v", "pev_%d_%d_%d" % (h, i - 1, r))
                ii7 = (h * NSLOT + 7) % 16
                p.wait(vector, "exp_%d_7" % h)
                vector.reciprocal(out=srec[:, ii7:ii7 + 1], in_=ssum[:, ii7:ii7 + 1])
                self.vsp(vector)
                p.mark(vector.tensor_scalar(
                    out=probs[:, 1, 0:2048], in0=probs[:, 1, 0:2048],
                    scalar1=srec[:, ii7:ii7 + 1], scalar2=None, op0=ALU.mult),
                    "v", "vnorm_%d_7" % h)
                p.wait(vector, "avmm_%d_5" % h)
                for r in range(2):
                    p.wait(vector, "ttr_%d_7_%d" % (h, r))
                    p.mark(vector.tensor_copy(
                        out=probsT[:, 1, r * 1024:(r + 1) * 1024],
                        in_=psB[:, (r % 2) * 1024:(r % 2) * 1024 + 1024]),
                        "v", "pev_%d_7_%d" % (h, r))

        def scalar_f(scalar):
            p.gate(scalar, FIN)
            for h in range(H):
                for mc in range(2):
                    p.wait(scalar, "qmm_%d_%d" % (h, mc))
                    if h >= 2:
                        p.wait(scalar, "smm_%d_7" % (h - 2))
                    p.mark(scalar.activation(
                        out=QT[:, h % 2, mc * 512:(mc + 1) * 512],
                        in_=psC[:, 512:1024], func=AF.Identity,
                        bias=wqb[:, h:h + 1], scale=1.0),
                        "s", "qev_%d_%d" % (h, mc))
                for i in range(NSLOT):
                    ii = (h * NSLOT + i) % 16
                    Ki = 256 * (i + 1)
                    p.wait(scalar, "vnb_%d_%d" % (h, i))
                    if i >= 2:
                        p.wait(scalar, "ttr_last_%d_%d" % (h, i - 2))
                    elif h >= 1:
                        p.wait(scalar, "ttr_last_%d_%d" % (h - 1, 6 + i))
                    if h >= 2:
                        p.wait(scalar, "vnorm_%d_%d" % (h - 2, i))
                    p.mark(scalar.activation(
                        out=probs[:, i % 2, 0:Ki], in_=psA[:, 0:Ki],
                        func=AF.Exp, bias=nbias[:, ii:ii + 1], scale=C_ATT,
                        accum_out=ssum[:, ii:ii + 1]),
                        "s", "exp_%d_%d" % (h, i))
                    if i >= 1:
                        p.wait(scalar, "avmm_%d_%d" % (h, i - 1))
                        p.mark(scalar.activation(
                            out=attnT[:, h * M + (i - 1) * 128: h * M + i * 128],
                            in_=psC[:, ((i - 1) % 2) * 128:((i - 1) % 2) * 128 + 128],
                            func=AF.Identity, bias=0.0, scale=1.0),
                            "s", "avev_%d_%d" % (h, i - 1))
                p.wait(scalar, "avmm_%d_7" % h)
                p.mark(scalar.activation(
                    out=attnT[:, h * M + 7 * 128: h * M + 8 * 128],
                    in_=psC[:, 128:256], func=AF.Identity, bias=0.0, scale=1.0),
                    "s", "avev_%d_7" % h)

        def tensor_f(tensor):
            p.gate(tensor, FIN)

            def ptr_attnv(h, i):
                nblk = 2 * i + 2
                p.wait(tensor, "vnorm_%d_%d" % (h, i))
                for r in range(nrounds(i)):
                    nb_r = min(8, nblk - r * 8)
                    rid = self.p2_rounds
                    self.p2_rounds += 1
                    prev = self.p2_round_ev.get(rid - 2)
                    if prev is not None:
                        p.wait(tensor, prev)
                    self.p2_round_ev[rid] = "pev_%d_%d_%d" % (h, i, r)
                    last = None
                    for u in range(nb_r):
                        tb = r * 8 + u
                        last = tensor.transpose(
                            psB[:, (r % 2) * 1024 + u * 128:
                                (r % 2) * 1024 + (u + 1) * 128],
                            probs[:, i % 2, tb * 128:(tb + 1) * 128], ident[:])
                    p.mark(last, "t", "ttr_%d_%d_%d" % (h, i, r))
                self.evhere("ttr_last_%d_%d" % (h, i), "t")
                if i >= 2:
                    p.wait(tensor, "avev_%d_%d" % (h, i - 2))
                elif h >= 1:
                    p.wait(tensor, "avev_%d_%d" % (h - 1, 6 + i))
                last = None
                for tb in range(nblk):
                    if tb % 8 == 0:
                        p.wait(tensor, "pev_%d_%d_%d" % (h, i, tb // 8))
                    last = tensor.matmul(
                        psC[:, (i % 2) * 128:(i % 2) * 128 + 128],
                        vtok[:, tb * 128:(tb + 1) * 128],
                        probsT[:, i % 2, tb * 128:(tb + 1) * 128],
                        start=(tb == 0), stop=(tb == nblk - 1))
                p.mark(last, "t", "avmm_%d_%d" % (h, i))

            for h in range(H):
                p.wait(tensor, "cvq_%d" % h)
                for mc in range(2):
                    if h == 0 and mc == 0:
                        pass
                    elif mc == 1:
                        p.wait(tensor, "qev_%d_0" % h)
                    else:
                        p.wait(tensor, "qev_%d_1" % (h - 1))
                    last = None
                    for dd in range(DB):
                        last = tensor.matmul(
                            psC[:, 512:1024],
                            wq_b[:, h % 2, dd * 128:(dd + 1) * 128],
                            xnownT[:, dd, mc * 512:(mc + 1) * 512],
                            start=(dd == 0), stop=(dd == DB - 1))
                    p.mark(last, "t", "qmm_%d_%d" % (h, mc))
                for i in range(NSLOT):
                    Ki = 256 * (i + 1)
                    p.wait(tensor, "qev_%d_1" % h)
                    if i >= 1:
                        p.wait(tensor, "exp_%d_%d" % (h, i - 1))
                        p.wait(tensor, "vnb_%d_%d" % (h, i - 1))
                    elif h >= 1:
                        p.wait(tensor, "exp_%d_7" % (h - 1))
                        p.wait(tensor, "vnb_%d_7" % (h - 1))
                    last = None
                    for c in range(cdv(Ki, 512)):
                        nw = min(512, Ki - c * 512)
                        last = tensor.matmul(
                            psA[:, c * 512:c * 512 + nw],
                            QT[:, h % 2, i * 128:(i + 1) * 128],
                            kT[:, c * 512:c * 512 + nw], start=True, stop=True)
                    p.mark(last, "t", "smm_%d_%d" % (h, i))
                    if i >= 1:
                        ptr_attnv(h, i - 1)
                ptr_attnv(h, 7)

        self.p2_rounds = 0
        self.p2_round_ev = {}
        blk.gpsimd(gp_f)
        blk.sync(sync_f)
        blk.vector(vector_f)
        blk.scalar(scalar_f)
        blk.tensor(tensor_f)

    def emit_p3(self, FIN, d_woT, d_sT, wob, attnT, wo_f, wo_b, sst, psA):
        p = self
        blk = self.blk
        taps = self.taps

        def sync_f(sync):
            p.gate(sync, FIN)
            for j in range(DB):
                if j >= 1:
                    p.wait(sync, "cvo_%d" % (j - 1))
                p.mark(sync.dma_start(
                    out=wo_f[:].rearrange("p (b f) -> p b f", b=DB)[:, 0:16, :],
                    in_=d_woT[0:2048, j * 128:(j + 1) * 128]
                    .rearrange("(b p) f -> p b f", p=128)),
                    "q", "ldo_%d" % j, 16)

        def vector_f(vector):
            p.gate(vector, FIN)
            for j in range(DB):
                p.wait(vector, "ldo_%d" % j)
                p.wait(vector, "ldo2_%d" % j)
                if j >= 2:
                    p.wait(vector, "womm_%d_1" % (j - 2))
                p.mark(vector.tensor_copy(out=wo_b[:, j % 2, :], in_=wo_f[:]),
                       "v", "cvo_%d" % j)

        def tensor_f(tensor):
            p.gate(tensor, FIN)
            for j in range(DB):
                p.wait(tensor, "cvo_%d" % j)
                for mc in range(2):
                    if j >= 2:
                        p.wait(tensor, "sev_%d_%d" % (j - 2, mc))
                    reg = (j % 2) * 1024 + mc * 512
                    last = None
                    for a in range(H):
                        last = tensor.matmul(
                            psA[:, reg:reg + 512],
                            wo_b[:, j % 2, a * 128:(a + 1) * 128],
                            attnT[:, a * M + mc * 512: a * M + (mc + 1) * 512],
                            start=(a == 0), stop=(a == H - 1))
                    p.mark(last, "t", "womm_%d_%d" % (j, mc))

        def scalar_f(scalar):
            p.gate(scalar, FIN)
            for j in range(DB):
                for mc in range(2):
                    p.wait(scalar, "womm_%d_%d" % (j, mc))
                    if j >= 2:
                        p.wait(scalar, "gsT_%d" % (j - 2))
                    reg = (j % 2) * 1024 + mc * 512
                    p.mark(scalar.activation(
                        out=sst[:, j % 2, mc * 512:(mc + 1) * 512],
                        in_=psA[:, reg:reg + 512], func=AF.Identity,
                        bias=wob[:, j:j + 1], scale=1.0),
                        "s", "sev_%d_%d" % (j, mc))

        def gp_f(gp):
            p.gate(gp, FIN)
            for j in range(DB):
                if j >= 1:
                    p.wait(gp, "cvo_%d" % (j - 1))
                p.mark(gp.dma_start(
                    out=wo_f[:].rearrange("p (b f) -> p b f", b=DB)[:, 16:32, :],
                    in_=d_woT[2048:4096, j * 128:(j + 1) * 128]
                    .rearrange("(b p) f -> p b f", p=128)),
                    "g", "ldo2_%d" % j, 16)
                if j >= 1:
                    p.wait(gp, "sev_%d_0" % (j - 1))
                    p.wait(gp, "sev_%d_1" % (j - 1))
                    p.mark(gp.dma_start(out=d_sT[:, (j - 1) * M:j * M],
                                        in_=sst[:, (j - 1) % 2, :]),
                           "g", "gsT_%d" % (j - 1), 16)
            p.wait(gp, "sev_31_0")
            p.wait(gp, "sev_31_1")
            p.mark(gp.dma_start(out=d_sT[:, 31 * M:32 * M],
                                in_=sst[:, 1, :]),
                   "g", "gsT_31", 16)
            if DEBUG_TAPS:
                p.wait(gp, "avev_31_7")
                p.mark(gp.dma_start(out=taps["attnT"], in_=attnT[:]), "g", None, 16)
                p.wait(gp, "gsT_31")
                p.mark(gp.dma_start(out=taps["sT"], in_=d_sT), "g", None, 16)

        blk.sync(sync_f)
        blk.vector(vector_f)
        blk.tensor(tensor_f)
        blk.scalar(scalar_f)
        blk.gpsimd(gp_f)

    def emit_mm1(self, mc, FIN, d_w1, d_w3, d_s1, d_s3, d_w1t, d_w3t,
                 d_xnT, hT, xnsl, pkr, sci1, sci3, nat1, nat3, w1t, w3t,
                 sc1, sc3, asil, bsb, psA, psB, ident):
        p = self
        blk = self.blk
        P = "m1%d_" % mc

        def sync_f(sync):
            p.gate(sync, FIN)
            if mc == 0:
                p.wait(sync, "gxn")
            else:
                p.wait(sync, "m10_mm_85")
            p.mark(sync.dma_start(
                out=xnsl[:],
                in_=d_xnT.rearrange("p (b m) -> p b m", b=DB)[:, :, mc * 512:(mc + 1) * 512]),
                "q", P + "ldxn", 16)
            for b in range(HB):
                if mc == 0:
                    if b >= 1:
                        p.wait(sync, P + "dqv1_%d" % (b - 1))
                    p.mark(sync.dma_start(
                        out=pkr[:, 0, :],
                        in_=d_w1[b * 4096:(b + 1) * 4096, :]
                        .rearrange("(p r) k -> p (r k)", p=128)),
                        "q", P + "ldw1_%d" % b, 16)
                    if b >= 2:
                        p.wait(sync, P + "dq1_%d" % (b - 2))
                    p.mark(sync.dma_start(
                        out=sc1[:, b % 2, :],
                        in_=d_s1[b * 8192:(b + 1) * 8192]
                        .rearrange("(p g) -> p g", p=128)),
                        "q", P + "ldsc1_%d" % b, 16)
                else:
                    if b >= 2:
                        p.wait(sync, P + "mm_%d" % (b - 2))
                    p.mark(sync.dma_start(out=w1t[:, b % 2, :], in_=d_w1t[b]),
                           "q", P + "ldt1_%d" % b, 16)

        def gpl_f(gp):
            p.gate(gp, FIN)

            def gws(b):
                for which in (1, 3):
                    wdst = w1t if which == 1 else w3t
                    wdram = d_w1t if which == 1 else d_w3t
                    p.wait(gp, P + "wev%d_%d_3" % (which, b))
                    p.mark(gp.dma_start(out=wdram[b], in_=wdst[:, b % 2, :]),
                           "g", P + "gw%d_%d" % (which, b), 16)

            for b in range(HB):
                if mc == 0:
                    if b >= 1:
                        p.wait(gp, P + "dqv3_%d" % (b - 1))
                    p.mark(gp.dma_start(
                        out=pkr[:, 1, :],
                        in_=d_w3[b * 4096:(b + 1) * 4096, :]
                        .rearrange("(p r) k -> p (r k)", p=128)),
                        "g", P + "ldw3_%d" % b, 16)
                    if b >= 2:
                        p.wait(gp, P + "dq3_%d" % (b - 2))
                    p.mark(gp.dma_start(
                        out=sc3[:, b % 2, :],
                        in_=d_s3[b * 8192:(b + 1) * 8192]
                        .rearrange("(p g) -> p g", p=128)),
                        "g", P + "ldsc3_%d" % b, 16)
                    if b >= 2:
                        gws(b - 2)
                else:
                    if b == 0:
                        p.wait(gp, P + "ldxn")
                    if b >= 2:
                        p.wait(gp, P + "mm_%d" % (b - 2))
                    p.mark(gp.dma_start(out=w3t[:, b % 2, :], in_=d_w3t[b]),
                           "g", P + "ldt3_%d" % b, 16)
            if mc == 0:
                gws(HB - 2)
                gws(HB - 1)

        def vec_f(vector):
            p.gate(vector, FIN)
            for b in range(HB):
                if mc == 0:
                    p.wait(vector, P + "ldw1_%d" % b)
                    p.wait(vector, P + "ldsc1_%d" % b)
                    if b >= 1:
                        p.wait(vector, P + "tr1_%d" % (b - 1))
                    pk1v = pkr[:, 0, :].rearrange("p (q j) -> p q j", q=32)
                    m1 = nat1[:].rearrange("p (q n j) -> p q n j", q=32, n=2)
                    s1v = sc1[:, b % 2, :].rearrange("p (q n) -> p q n", n=2)
                    p.wait(vector, P + "ldw3_%d" % b)
                    p.wait(vector, P + "ldsc3_%d" % b)
                    if b >= 1:
                        p.wait(vector, P + "tr3_%d" % (b - 1))
                    pk3v = pkr[:, 1, :].rearrange("p (q j) -> p q j", q=32)
                    m3 = nat3[:].rearrange("p (q n j) -> p q n j", q=32, n=2)
                    s3v = sc3[:, b % 2, :].rearrange("p (q n) -> p q n", n=2)
                    sc1v = sci1[:].rearrange("p (q j) -> p q j", q=32)
                    sc3v = sci3[:].rearrange("p (q j) -> p q j", q=32)
                    vector.tensor_scalar(out=sci1[:], in0=pkr[:, 0, :], scalar1=4,
                                         scalar2=None, op0=ALU.arith_shift_right)
                    vector.tensor_scalar(out=sci3[:], in0=pkr[:, 1, :], scalar1=4,
                                         scalar2=None, op0=ALU.arith_shift_right)
                    vector.tensor_mul(out=m1[:, :, 0, :], in0=sc1v,
                                      in1=s1v[:, :, 0:1].broadcast_to([128, 32, 64]))
                    vector.tensor_mul(out=m3[:, :, 0, :], in0=sc3v,
                                      in1=s3v[:, :, 0:1].broadcast_to([128, 32, 64]))
                    p.mark(vector.tensor_scalar(
                        out=sci1[:], in0=pkr[:, 0, :], scalar1=15, scalar2=8,
                        op0=ALU.bitwise_and, op1=ALU.bitwise_xor),
                        "v", P + "dqv1_%d" % b)
                    p.mark(vector.tensor_scalar(
                        out=sci3[:], in0=pkr[:, 1, :], scalar1=15, scalar2=8,
                        op0=ALU.bitwise_and, op1=ALU.bitwise_xor),
                        "v", P + "dqv3_%d" % b)
                    p.mark(vector.scalar_tensor_tensor(
                        out=m1[:, :, 1, :], in0=sc1v, scalar=-8.0,
                        in1=s1v[:, :, 1:2].broadcast_to([128, 32, 64]),
                        op0=ALU.add, op1=ALU.mult),
                        "v", P + "dq1_%d" % b)
                    p.mark(vector.scalar_tensor_tensor(
                        out=m3[:, :, 1, :], in0=sc3v, scalar=-8.0,
                        in1=s3v[:, :, 1:2].broadcast_to([128, 32, 64]),
                        op0=ALU.add, op1=ALU.mult),
                        "v", P + "dq3_%d" % b)
                if b >= 1:
                    p.wait(vector, P + "abev_%d" % (b - 1))
                    p.mark(vector.tensor_mul(
                        out=hT[:, b - 1, :], in0=asil[:, (b - 1) % 2, :],
                        in1=bsb[:, (b - 1) % 2, :]),
                        "v", P + "hTT_%d" % (b - 1))
            p.wait(vector, P + "abev_85")
            p.mark(vector.tensor_mul(
                out=hT[:, 85, :], in0=asil[:, 1, :], in1=bsb[:, 1, :]),
                "v", P + "hTT_85")

        def ten_f(tensor):
            p.gate(tensor, FIN)
            p.wait(tensor, P + "ldxn")
            rid = [0]
            rev = {}

            def tr_mat(b, which):
                nat = nat1 if which == 1 else nat3
                p.wait(tensor, P + "dq%d_%d" % (which, b))
                for r in range(4):
                    k = rid[0]
                    rid[0] += 1
                    if k >= 2:
                        p.wait(tensor, rev[k - 2])
                    rev[k] = P + "wev%d_%d_%d" % (which, b, r)
                    last = None
                    for u in range(8):
                        q = r * 8 + u
                        last = tensor.transpose(
                            psB[:, (k % 2) * 1024 + u * 128:(k % 2) * 1024 + (u + 1) * 128],
                            nat[:, q * 128:(q + 1) * 128], ident[:])
                    p.mark(last, "t", P + "wtr%d_%d_%d" % (which, b, r))
                self.evhere(P + "tr%d_%d" % (which, b), "t")

            def mms(b):
                if mc == 0:
                    p.wait(tensor, P + "wev1_%d_3" % b)
                    p.wait(tensor, P + "wev3_%d_3" % b)
                else:
                    p.wait(tensor, P + "ldt1_%d" % b)
                    p.wait(tensor, P + "ldt3_%d" % b)
                if b >= 2:
                    p.wait(tensor, P + "abev_%d" % (b - 2))
                reg_a = (b % 2) * 1024
                reg_b = (b % 2) * 1024 + 512
                last = None
                for dd in range(DB):
                    last = tensor.matmul(psA[:, reg_a:reg_a + 512],
                                         w1t[:, b % 2, dd * 128:(dd + 1) * 128],
                                         xnsl[:, dd, :],
                                         start=(dd == 0), stop=(dd == DB - 1))
                for dd in range(DB):
                    last = tensor.matmul(psA[:, reg_b:reg_b + 512],
                                         w3t[:, b % 2, dd * 128:(dd + 1) * 128],
                                         xnsl[:, dd, :],
                                         start=(dd == 0), stop=(dd == DB - 1))
                p.mark(last, "t", P + "mm_%d" % b)

            if mc == 0:
                for b in range(HB):
                    tr_mat(b, 1)
                    tr_mat(b, 3)
                    if b >= 1:
                        mms(b - 1)
                mms(85)
            else:
                for b in range(HB):
                    mms(b)

        def sca_f(scalar):
            p.gate(scalar, FIN)
            for b in range(HB):
                if mc == 0:
                    for which in (1, 3):
                        wdst = w1t if which == 1 else w3t
                        for r in range(4):
                            p.wait(scalar, P + "wtr%d_%d_%d" % (which, b, r))
                            if b >= 2 and r == 0:
                                p.wait(scalar, P + "mm_%d" % (b - 2))
                                p.wait(scalar, P + "gw%d_%d" % (which, b - 2))
                            k = b * 8 + (0 if which == 1 else 4) + r
                            p.mark(scalar.activation(
                                out=wdst[:, b % 2, r * 1024:(r + 1) * 1024],
                                in_=psB[:, (k % 2) * 1024:(k % 2) * 1024 + 1024],
                                func=AF.Identity, bias=0.0, scale=1.0),
                                "s", P + "wev%d_%d_%d" % (which, b, r))
                if b >= 1:
                    bb = b - 1
                    p.wait(scalar, P + "mm_%d" % bb)
                    if bb >= 2:
                        p.wait(scalar, P + "hTT_%d" % (bb - 2))
                    reg_a = (bb % 2) * 1024
                    reg_b = (bb % 2) * 1024 + 512
                    scalar.activation(out=asil[:, bb % 2, :],
                                      in_=psA[:, reg_a:reg_a + 512],
                                      func=AF.Silu, bias=0.0, scale=1.0)
                    p.mark(scalar.activation(out=bsb[:, bb % 2, :],
                                             in_=psA[:, reg_b:reg_b + 512],
                                             func=AF.Identity, bias=0.0, scale=1.0),
                           "s", P + "abev_%d" % bb)
            p.wait(scalar, P + "mm_85")
            p.wait(scalar, P + "hTT_83")
            scalar.activation(out=asil[:, 1, :], in_=psA[:, 1024:1536],
                              func=AF.Silu, bias=0.0, scale=1.0)
            p.mark(scalar.activation(out=bsb[:, 1, :], in_=psA[:, 1536:2048],
                                     func=AF.Identity, bias=0.0, scale=1.0),
                   "s", P + "abev_85")

        blk.sync(sync_f)
        blk.gpsimd(gpl_f)
        blk.vector(vec_f)
        blk.tensor(ten_f)
        blk.scalar(sca_f)

        fin = [P + "mm_85", P + "hTT_85", P + "abev_85", P + "ldxn"]
        if mc == 0:
            fin += [P + "gw1_85", P + "gw3_85", P + "ldw3_85"]
        else:
            fin += [P + "ldt3_85"]
        return fin

    def emit_mm2(self, mc, FIN, d_w2, d_s2, d_w2t, hT, pk2, sci2, nat2, w2t,
                 sc2, gT, psA, psB, ident):
        p = self
        blk = self.blk
        taps = self.taps
        P = "m2%d_" % mc
        d_w2v = d_w2.rearrange("(a q) k -> a q k", q=HB)

        def sync_f(sync):
            p.gate(sync, FIN)
            for k in range(DB):
                if mc == 0:
                    if k >= 1:
                        p.wait(sync, P + "dqv_%d_0" % (k - 1))
                    p.mark(sync.dma_start(
                        out=pk2[:, 0, :].rearrange("p (q j) -> p q j", q=43),
                        in_=d_w2v[k * 128:(k + 1) * 128, 0:43, :]),
                        "q", P + "ldp_%d_0" % k, 16)
                    if k >= 2:
                        p.wait(sync, P + "dq_%d_1" % (k - 2))
                    p.mark(sync.dma_start(
                        out=sc2[:, k % 2, :],
                        in_=d_s2[k * 128 * 172:(k + 1) * 128 * 172]
                        .rearrange("(p g) -> p g", p=128)),
                        "q", P + "lds_%d" % k, 16)
                else:
                    if k >= 2:
                        p.wait(sync, P + "mm_%d" % (k - 2))
                    p.mark(sync.dma_start(out=w2t[:, k % 2, 0:5504],
                                          in_=d_w2t[k][:, 0:5504]),
                           "q", P + "ldt_%d_0" % k, 16)

        def gpl_f(gp):
            p.gate(gp, FIN)

            def gws(k):
                p.wait(gp, P + "wev_%d_1_5" % k)
                p.mark(gp.dma_start(out=d_w2t[k], in_=w2t[:, k % 2, :]),
                       "g", P + "gw_%d" % k, 16)

            for k in range(DB):
                if mc == 0:
                    if k >= 1:
                        p.wait(gp, P + "dqv_%d_1" % (k - 1))
                    p.mark(gp.dma_start(
                        out=pk2[:, 1, :].rearrange("p (q j) -> p q j", q=43),
                        in_=d_w2v[k * 128:(k + 1) * 128, 43:86, :]),
                        "g", P + "ldp_%d_1" % k, 16)
                    if k >= 2:
                        gws(k - 2)
                else:
                    if k == 0:
                        p.wait(gp, P + "ldt_0_0")
                    if k >= 2:
                        p.wait(gp, P + "mm_%d" % (k - 2))
                    p.mark(gp.dma_start(out=w2t[:, k % 2, 5504:11008],
                                        in_=d_w2t[k][:, 5504:11008]),
                           "g", P + "ldt_%d_1" % k, 16)
            if mc == 0:
                gws(DB - 2)
                gws(DB - 1)
            if DEBUG_TAPS and mc == 0:
                p.wait(gp, "m10_hTT_85")
                p.mark(gp.dma_start(out=taps["hT"],
                                    in_=hT[:].rearrange("p b m -> p (b m)")),
                       "g", None, 16)

        def vec_f(vector):
            p.gate(vector, FIN)
            if mc != 0:
                return
            for k in range(DB):
                for cc in range(2):
                    p.wait(vector, P + "ldp_%d_%d" % (k, cc))
                    p.wait(vector, P + "lds_%d" % k)
                    if k > 0 or cc > 0:
                        pk_, pc_ = (k, 0) if cc == 1 else (k - 1, 1)
                        p.wait(vector, P + "tr_%d_%d" % (pk_, pc_))
                    mo = nat2[:].rearrange("p (q n j) -> p q n j", q=43, n=2)
                    sv = sc2[:, k % 2, 86 * cc:86 * (cc + 1)].rearrange(
                        "p (q n) -> p q n", n=2)
                    sciv = sci2[:].rearrange("p (q j) -> p q j", q=43)
                    vector.tensor_scalar(out=sci2[:], in0=pk2[:, cc, :],
                                         scalar1=4, scalar2=None,
                                         op0=ALU.arith_shift_right)
                    self.vsp(vector)
                    vector.tensor_mul(out=mo[:, :, 0, :], in0=sciv,
                                      in1=sv[:, :, 0:1].broadcast_to([128, 43, 64]))
                    p.mark(vector.tensor_scalar(
                        out=sci2[:], in0=pk2[:, cc, :],
                        scalar1=15, scalar2=8,
                        op0=ALU.bitwise_and, op1=ALU.bitwise_xor),
                        "v", P + "dqv_%d_%d" % (k, cc))
                    self.vsp(vector)
                    p.mark(vector.scalar_tensor_tensor(
                        out=mo[:, :, 1, :], in0=sciv, scalar=-8.0,
                        in1=sv[:, :, 1:2].broadcast_to([128, 43, 64]),
                        op0=ALU.add, op1=ALU.mult),
                        "v", P + "dq_%d_%d" % (k, cc))

        def ten_f(tensor):
            p.gate(tensor, FIN)
            p.wait(tensor, "m1%d_hTT_85" % mc)
            rid = [0]
            rev = {}

            def trs(k):
                for cc in range(2):
                    p.wait(tensor, P + "dq_%d_%d" % (k, cc))
                    for r in range(6):
                        nb_r = min(8, 43 - r * 8)
                        kk = rid[0]
                        rid[0] += 1
                        if kk >= 2:
                            p.wait(tensor, rev[kk - 2])
                        rev[kk] = P + "wev_%d_%d_%d" % (k, cc, r)
                        last = None
                        for u in range(nb_r):
                            q = r * 8 + u
                            last = tensor.transpose(
                                psB[:, (kk % 2) * 1024 + u * 128:
                                    (kk % 2) * 1024 + (u + 1) * 128],
                                nat2[:, q * 128:(q + 1) * 128], ident[:])
                        p.mark(last, "t", P + "wtr_%d_%d_%d" % (k, cc, r))
                    self.evhere(P + "tr_%d_%d" % (k, cc), "t")

            def mms(k):
                if mc == 0:
                    p.wait(tensor, P + "wev_%d_1_5" % k)
                else:
                    p.wait(tensor, P + "ldt_%d_0" % k)
                    p.wait(tensor, P + "ldt_%d_1" % k)
                if k >= 2:
                    p.wait(tensor, P + "gev_%d" % (k - 2))
                reg = (k % 2) * 512
                last = None
                for c in range(HB):
                    last = tensor.matmul(psA[:, reg:reg + 512],
                                         w2t[:, k % 2, c * 128:(c + 1) * 128],
                                         hT[:, c, :],
                                         start=(c == 0), stop=(c == HB - 1))
                p.mark(last, "t", P + "mm_%d" % k)

            if mc == 0:
                for k in range(DB):
                    trs(k)
                    if k >= 1:
                        mms(k - 1)
                mms(DB - 1)
            else:
                for k in range(DB):
                    mms(k)

        def sca_f(scalar):
            p.gate(scalar, FIN)

            def gev(k):
                p.wait(scalar, P + "mm_%d" % k)
                reg = (k % 2) * 512
                p.mark(scalar.activation(out=gT[:, k, :], in_=psA[:, reg:reg + 512],
                                         func=AF.Silu, bias=0.0, scale=1.0),
                       "s", P + "gev_%d" % k)

            for k in range(DB):
                if mc == 0:
                    for cc in range(2):
                        for r in range(6):
                            nb_r = min(8, 43 - r * 8)
                            p.wait(scalar, P + "wtr_%d_%d_%d" % (k, cc, r))
                            if k >= 2 and cc == 0 and r == 0:
                                p.wait(scalar, P + "mm_%d" % (k - 2))
                                p.wait(scalar, P + "gw_%d" % (k - 2))
                            kk = k * 12 + cc * 6 + r
                            p.mark(scalar.activation(
                                out=w2t[:, k % 2,
                                        (cc * 43 + r * 8) * 128:
                                        (cc * 43 + r * 8 + nb_r) * 128],
                                in_=psB[:, (kk % 2) * 1024:(kk % 2) * 1024 + nb_r * 128],
                                func=AF.Identity, bias=0.0, scale=1.0),
                                "s", P + "wev_%d_%d_%d" % (k, cc, r))
                    if k >= 1:
                        gev(k - 1)
                else:
                    gev(k)
            if mc == 0:
                gev(DB - 1)

        blk.sync(sync_f)
        blk.gpsimd(gpl_f)
        blk.vector(vec_f)
        blk.tensor(ten_f)
        blk.scalar(sca_f)

        fin = [P + "mm_31", P + "gev_31"]
        if mc == 0:
            fin += [P + "gw_31", P + "dq_31_1", P + "lds_31"]
        else:
            fin += [P + "ldt_31_0", P + "ldt_31_1"]
        return fin

    def emit_combine(self, mc, FIN, d_sT, d_xown, d_out, gT, ssl, xcf,
                     outst, psB, ident):
        p = self
        blk = self.blk
        taps = self.taps
        P = "cb%d_" % mc

        def sync_f(sync):
            p.gate(sync, FIN)
            p.mark(sync.dma_start(
                out=ssl[:],
                in_=d_sT.rearrange("p (b m) -> p b m", b=DB)[:, :, mc * 512:(mc + 1) * 512]),
                "q", P + "lds", 16)
            for t in range(4):
                if t >= 2:
                    p.wait(sync, P + "add_%d_7" % (t - 2))
                p.mark(sync.dma_start(
                    out=xcf[:, t % 2, :],
                    in_=d_xown[(mc * 4 + t) * 128:(mc * 4 + t + 1) * 128, :]),
                    "q", P + "ldx_%d" % t, 16)

        def vec_f(vector):
            p.gate(vector, FIN)
            p.wait(vector, P + "lds")
            for c in range(8):
                p.mark(vector.tensor_mul(out=gT[:, c * 4:(c + 1) * 4, :],
                                         in0=gT[:, c * 4:(c + 1) * 4, :],
                                         in1=ssl[:, c * 4:(c + 1) * 4, :]),
                       "v", P + "ptt_%d" % c)
            for t in range(4):
                p.wait(vector, P + "ldx_%d" % t)
                for rr in range(8):
                    p.wait(vector, P + "ctr_%d_%d" % (t, rr))
                    if rr >= 2:
                        p.wait(vector, P + "out_%d_%d" % (t, rr - 2))
                    elif t >= 1:
                        p.wait(vector, P + "out_%d_%d" % (t - 1, 6 + rr))
                    p.mark(vector.tensor_add(
                        out=outst[:, rr % 2, :],
                        in0=psB[:, (rr % 2) * 1024:(rr % 2) * 1024 + 512],
                        in1=xcf[:, t % 2, rr * 512:(rr + 1) * 512]),
                        "v", P + "add_%d_%d" % (t, rr))

        def ten_f(tensor):
            p.gate(tensor, FIN)
            for t in range(4):
                for rr in range(8):
                    p.wait(tensor, P + "ptt_%d" % rr)
                    rid = t * 8 + rr
                    if rid >= 2:
                        p.wait(tensor, P + "add_%d_%d" % ((rid - 2) // 8, (rid - 2) % 8))
                    last = None
                    for u in range(4):
                        k = rr * 4 + u
                        last = tensor.transpose(
                            psB[:, (rr % 2) * 1024 + u * 128:(rr % 2) * 1024 + (u + 1) * 128],
                            gT[:, k, t * 128:(t + 1) * 128], ident[:])
                    p.mark(last, "t", P + "ctr_%d_%d" % (t, rr))

        def gp_f(gp):
            p.gate(gp, FIN)
            for t in range(4):
                for rr in range(8):
                    p.wait(gp, P + "add_%d_%d" % (t, rr))
                    p.mark(gp.dma_start(
                        out=d_out[(mc * 4 + t) * 128:(mc * 4 + t + 1) * 128,
                                  rr * 512:(rr + 1) * 512],
                        in_=outst[:, rr % 2, :]),
                        "g", P + "out_%d_%d" % (t, rr), 16)
            if DEBUG_TAPS and mc == 0:
                p.wait(gp, "m20_gev_31")
                p.mark(gp.dma_start(out=taps["gT"],
                                    in_=gT[:].rearrange("p b m -> p (b m)")),
                       "g", None, 16)

        blk.sync(sync_f)
        blk.vector(vec_f)
        blk.tensor(ten_f)
        blk.gpsimd(gp_f)

        return [P + "ctr_3_7", P + "add_3_7", P + "out_3_7", P + "ldx_3"]


def build_program():
    b1 = Builder(events=None)
    b1.build()
    b2 = Builder(events=b1.rec)
    nc = b2.build()
    assert b2.cnt == b1.cnt, (b1.cnt, b2.cnt)
    return nc


_NC_CACHE = {}
_LAST_RES = {}


def _get_nc():
    if "nc" not in _NC_CACHE:
        _NC_CACHE["nc"] = build_program()
    return _NC_CACHE["nc"]


def _make_mask(par):
    q = np.arange(128)[:, None]
    k = np.arange(128)[None, :]
    tri = np.where(k <= q, 0.0, -30000.0).astype(np.float32)
    m = np.zeros((128, 256), np.float32)
    if par == 0:
        m[:, 0:128] = tri
        m[:, 128:256] = -30000.0
    else:
        m[:, 0:128] = 0.0
        m[:, 128:256] = tri
    return np.ascontiguousarray(m)


def kernel(L0_attn_x, L0_attn_wq_w, L0_attn_wq_b, L0_attn_wk_w, L0_attn_wk_b,
           L0_attn_wv_w, L0_attn_wv_b, L0_attn_wo_w, L0_attn_wo_b,
           L0_ffn_w1, L0_ffn_s1, L0_ffn_w2, L0_ffn_s2, L0_ffn_w3, L0_ffn_s3):
    x = np.ascontiguousarray(np.asarray(L0_attn_x, dtype=np.float32))
    wqT = np.ascontiguousarray(np.asarray(L0_attn_wq_w, np.float32).T)
    wkT = np.ascontiguousarray(np.asarray(L0_attn_wk_w, np.float32).T)
    wvT = np.ascontiguousarray(np.asarray(L0_attn_wv_w, np.float32).T)
    woT = np.ascontiguousarray(np.asarray(L0_attn_wo_w, np.float32).T)
    wqb = np.ascontiguousarray(np.asarray(L0_attn_wq_b, np.float32).reshape(32, 128).T)
    wob = np.ascontiguousarray(np.asarray(L0_attn_wo_b, np.float32).reshape(32, 128).T)
    wkb = np.ascontiguousarray(np.asarray(L0_attn_wk_b, np.float32).reshape(128, 1))
    wvb = np.ascontiguousarray(np.asarray(L0_attn_wv_b, np.float32).reshape(128, 1))
    w1 = np.ascontiguousarray(np.asarray(L0_ffn_w1, np.int32))
    w2 = np.ascontiguousarray(np.asarray(L0_ffn_w2, np.int32))
    w3 = np.ascontiguousarray(np.asarray(L0_ffn_w3, np.int32))
    s1 = np.ascontiguousarray(np.asarray(L0_ffn_s1, np.float32))
    s2 = np.ascontiguousarray(np.asarray(L0_ffn_s2, np.float32))
    s3 = np.ascontiguousarray(np.asarray(L0_ffn_s3, np.float32))

    masks = [_make_mask(0), _make_mask(1)]
    in_maps = []
    for c in range(8):
        b, par = c // 2, c % 2
        xb = np.ascontiguousarray(x[b])
        xown = np.ascontiguousarray(xb.reshape(16, 128, D)[par::2].reshape(M, D))
        in_maps.append(dict(
            xkv=xb, xown=xown, wqT=wqT, wkT=wkT, wvT=wvT, woT=woT,
            wqb=wqb, wob=wob, wkb=wkb, wvb=wvb,
            w1p=w1, w2p=w2, w3p=w3, s1=s1, s2=s2, s3=s3,
            mask=masks[par]))

    nc = _get_nc()
    res = run_bass_kernel_spmd(nc, in_maps, core_ids=list(range(8)))
    _LAST_RES["res"] = res

    out = np.empty((B, T, D), np.float32)
    for c in range(8):
        b, par = c // 2, c % 2
        oc = np.asarray(res.results[c]["out"]).reshape(NSLOT, 128, D)
        out.reshape(B, 16, 128, D)[b, par::2] = oc
    return out


# version 07
# speedup vs baseline: 1.0106x; 1.0106x over previous
import numpy as np
import ml_dtypes
import concourse.bass as bass
import concourse.mybir as mybir
from concourse.bass_utils import run_bass_kernel_spmd
from contextlib import ExitStack

F32 = mybir.dt.float32
BF16 = mybir.dt.bfloat16
I32 = mybir.dt.int32
AF = mybir.ActivationFunctionType
ALU = mybir.AluOpType
AX = mybir.AxisListType
bfdt = ml_dtypes.bfloat16

B, T, D = 4, 2048, 4096
H, HD = 32, 128
HH = 11008
M = 1024
NSLOT = 8
C_ATT = 1.0 / float(np.sqrt(128.0))
DB = 32
HB = 86
NT = 24

DEBUG_TAPS = False
BUILD_UPTO = "all"
DETECT_RACES = True


def cdv(a, b):
    return (a + b - 1) // b


class Builder:
    def __init__(self, events=None):
        self.nc = nc = bass.Bass("TRN2", target_bir_lowering=False,
                                 detect_race_conditions=DETECT_RACES)
        self.events = events
        self.rec = {}
        self.sem = {}
        self.cnt = {}
        for nm in ("t", "v", "s", "g", "q"):
            self.sem[nm] = nc.alloc_semaphore("sem_" + nm)
            self.cnt[nm] = 0

    def mark(self, instr, key, name=None, k=1):
        instr.then_inc(self.sem[key], k)
        self.cnt[key] += k
        if name is not None:
            assert name not in self.rec, "dup event " + name
            self.rec[name] = (key, self.cnt[key])

    def evhere(self, name, key):
        if name not in self.rec:
            self.rec[name] = (key, self.cnt[key])

    def wait(self, engine, name):
        if self.events is None:
            return
        key, n = self.events[name]
        if n > 0:
            engine.wait_ge(self.sem[key], n)

    def gate(self, engine, names):
        for nm in names:
            self.wait(engine, nm)

    def vsp(self, v, n=2):
        for _ in range(n):
            v.tensor_scalar(out=self.junk[:, 0:8], in0=self.junk[:, 0:8],
                            scalar1=1.0, scalar2=None, op0=ALU.mult)

    def build(self):
        nc = self.nc
        p = self

        d_xkv = nc.dram_tensor("xkv", [T, D], F32, kind="ExternalInput").ap()
        d_xown = nc.dram_tensor("xown", [M, D], F32, kind="ExternalInput").ap()
        d_wqT = nc.dram_tensor("wqT", [D, D], F32, kind="ExternalInput").ap()
        d_wkT = nc.dram_tensor("wkT", [D, HD], F32, kind="ExternalInput").ap()
        d_wvT = nc.dram_tensor("wvT", [D, HD], F32, kind="ExternalInput").ap()
        d_woT = nc.dram_tensor("woT", [D, D], F32, kind="ExternalInput").ap()
        d_wqb = nc.dram_tensor("wqb", [128, 32], F32, kind="ExternalInput").ap()
        d_wob = nc.dram_tensor("wob", [128, 32], F32, kind="ExternalInput").ap()
        d_wkb = nc.dram_tensor("wkb", [128, 1], F32, kind="ExternalInput").ap()
        d_wvb = nc.dram_tensor("wvb", [128, 1], F32, kind="ExternalInput").ap()
        d_w1 = nc.dram_tensor("w1p", [352256, 64], I32, kind="ExternalInput").ap()
        d_w2 = nc.dram_tensor("w2p", [352256, 64], I32, kind="ExternalInput").ap()
        d_w3 = nc.dram_tensor("w3p", [352256, 64], I32, kind="ExternalInput").ap()
        d_s1 = nc.dram_tensor("s1", [704512], F32, kind="ExternalInput").ap()
        d_s2 = nc.dram_tensor("s2", [704512], F32, kind="ExternalInput").ap()
        d_s3 = nc.dram_tensor("s3", [704512], F32, kind="ExternalInput").ap()
        d_mask = nc.dram_tensor("mask", [128, 512], BF16, kind="ExternalInput").ap()
        d_out = nc.dram_tensor("out", [M, D], F32, kind="ExternalOutput").ap()
        d_ident = nc.inline_tensor(np.eye(128, dtype=bfdt), "ident_const").ap()

        d_xnT = nc.dram_tensor("xnT_sc", [128, DB * M], BF16, kind="Internal").ap()
        d_sT = nc.dram_tensor("sT_sc", [128, DB * M], BF16, kind="Internal").ap()
        d_w1t = nc.dram_tensor("w1t_sc", [HB, 128, D], BF16, kind="Internal").ap()
        d_w3t = nc.dram_tensor("w3t_sc", [HB, 128, D], BF16, kind="Internal").ap()
        d_w2t = nc.dram_tensor("w2t_sc", [DB, 128, HH], BF16, kind="Internal").ap()

        taps = {}
        if DEBUG_TAPS:
            taps["kT"] = nc.dram_tensor("tap_kT", [128, T], BF16, kind="ExternalOutput").ap()
            taps["vtok"] = nc.dram_tensor("tap_vtok", [128, T], BF16, kind="ExternalOutput").ap()
            taps["xnT"] = nc.dram_tensor("tap_xnT", [128, DB * M], BF16, kind="ExternalOutput").ap()
            taps["attnT"] = nc.dram_tensor("tap_attnT", [128, H * M], BF16, kind="ExternalOutput").ap()
            taps["sT"] = nc.dram_tensor("tap_sT", [128, DB * M], BF16, kind="ExternalOutput").ap()
            taps["hT"] = nc.dram_tensor("tap_hT", [128, HB * 512], BF16, kind="ExternalOutput").ap()
            taps["gT"] = nc.dram_tensor("tap_gT", [128, DB * 512], BF16, kind="ExternalOutput").ap()
            taps["st"] = nc.dram_tensor("tap_st", [128, 160], F32, kind="ExternalOutput").ap()
            taps["QT"] = nc.dram_tensor("tap_QT", [128, 2048], BF16, kind="ExternalOutput").ap()
            taps["probs"] = nc.dram_tensor("tap_probs", [128, 2048], BF16, kind="ExternalOutput").ap()
            taps["pT"] = nc.dram_tensor("tap_pT", [128, 2048], BF16, kind="ExternalOutput").ap()
            taps["sm"] = nc.dram_tensor("tap_sm", [128, 64], F32, kind="ExternalOutput").ap()
        self.taps = taps

        ctx = ExitStack()
        with ctx:
            S = lambda name, shape, dt: ctx.enter_context(nc.sbuf_tensor(name, shape, dt))
            ident = S("c_ident", [128, 128], BF16)
            mask = S("c_mask", [128, 512], BF16)
            wqbs = S("c_wqbs", [128, 32], F32)
            wqb = S("c_wqb", [128, 32], F32)
            wob = S("c_wob", [128, 32], F32)
            wkb = S("c_wkb", [128, 1], F32)
            wvb = S("c_wvb", [128, 1], F32)
            self.junk = S("c_junk", [128, 8], F32)
            s1w = S("c_s1w", [128, 32], F32)
            s2w = S("c_s2w", [128, 32], F32)
            mnw = S("c_mnw", [128, 32], F32)
            vrw = S("c_vrw", [128, 32], F32)
            rsw = S("c_rsw", [128, 32], F32)
            ssum = S("c_ssum", [128, 16], F32)
            srec = S("c_srec", [128, 16], F32)
            nmaxb = S("c_nmax", [128, 16], F32)

            psA = ctx.enter_context(nc.psum_tensor("psA", [128, 2048], F32))
            psB = ctx.enter_context(nc.psum_tensor("psB", [128, 2048], BF16))
            psC = ctx.enter_context(nc.psum_tensor("psC", [128, 1024], F32))

            blk = ctx.enter_context(nc.Block())
            self.blk = blk

            with ExitStack() as cAB:
                kT = cAB.enter_context(nc.sbuf_tensor("c_kT", [128, T], BF16))
                vtok = cAB.enter_context(nc.sbuf_tensor("c_vtok", [128, 16 * 128], BF16))
                xnownT = cAB.enter_context(nc.sbuf_tensor("b_xnownT", [128, DB, M], BF16))

                with ExitStack() as c1:
                    S1 = lambda n_, sh, dt: c1.enter_context(nc.sbuf_tensor(n_, sh, dt))
                    xf = S1("p1_xf", [128, 3, D], F32)
                    wkT = S1("c_wkT", [128, DB * 128], BF16)
                    wvT = S1("c_wvT", [128, DB * 128], BF16)
                    xnb = S1("p1_xnb", [128, 2, D], BF16)
                    xnkv = S1("p1_xnkv", [128, DB, 512], BF16)
                    vT = S1("p1_vT", [128, T], BF16)
                    self.emit_p1(d_xkv, d_xown, d_wkT, d_wvT, d_ident, d_mask,
                                 d_wqb, d_wob, d_wkb, d_wvb,
                                 ident, mask, wqb, wqbs, wob, wkb, wvb, wkT, wvT, kT,
                                 vtok, s1w, s2w, mnw, vrw, rsw,
                                 xnownT, xf, xnb, xnkv, vT, psA, psB)

                P1_FIN = ["vtrr_1", "vtev_1", "vev_3", "ld_x_23"]
                if DEBUG_TAPS:
                    def p1_taps(gp):
                        p.gate(gp, P1_FIN)
                        p.mark(gp.dma_start(out=taps["st"][:, 0:32], in_=s1w[:]), "g", None, 16)
                        p.mark(gp.dma_start(out=taps["st"][:, 32:64], in_=s2w[:]), "g", None, 16)
                        p.mark(gp.dma_start(out=taps["st"][:, 64:96], in_=mnw[:]), "g", None, 16)
                        p.mark(gp.dma_start(out=taps["st"][:, 96:128], in_=vrw[:]), "g", None, 16)
                        p.mark(gp.dma_start(out=taps["st"][:, 128:160], in_=rsw[:]), "g", None, 16)
                        p.mark(gp.dma_start(out=taps["kT"], in_=kT[:]), "g", None, 16)
                        p.mark(gp.dma_start(out=taps["vtok"], in_=vtok[:]), "g", None, 16)
                        p.mark(gp.dma_start(out=taps["xnT"],
                                            in_=xnownT[:].rearrange("p b m -> p (b m)")),
                               "g", None, 16)
                    blk.gpsimd(p1_taps)
                if BUILD_UPTO == "p1":
                    def p_final0(gp):
                        gp.wait_ge(self.sem["g"], self.cnt["g"])
                    blk.gpsimd(p_final0)
                    return self.nc

                attnT = cAB.enter_context(nc.sbuf_tensor("b_attnT", [128, H * M], BF16))

                with ExitStack() as c2:
                    S2 = lambda n_, sh, dt: c2.enter_context(nc.sbuf_tensor(n_, sh, dt))
                    wq_f = S2("p2_wqf", [128, D], F32)
                    wq_b = S2("p2_wqb", [128, 2, D], BF16)
                    QT = S2("p2_QT", [128, 2, M], BF16)
                    probs = S2("p2_probs", [128, 2, 2048], BF16)
                    probsT = S2("p2_probsT", [128, 2, 2048], BF16)
                    self.emit_p2(P1_FIN, d_wqT, d_xnT, ident, mask, wqbs,
                                 kT, vtok, xnownT, attnT,
                                 ssum, srec, nmaxb,
                                 wq_f, wq_b, QT, probs, probsT, psA, psB, psC)

                P2_FIN = ["avmm_31_7", "pev_31_7_1", "avev_31_7", "ldq_31", "gxn"]
                if DEBUG_TAPS:
                    def p2_taps(gp):
                        p.gate(gp, P2_FIN)
                        p.mark(gp.dma_start(out=taps["QT"][:, 0:1024], in_=QT[:, 1, :]), "g", None, 16)
                        p.mark(gp.dma_start(out=taps["probs"], in_=probs[:, 1, :]), "g", None, 16)
                        p.mark(gp.dma_start(out=taps["pT"], in_=probsT[:, 1, :]), "g", None, 16)
                        p.mark(gp.dma_start(out=taps["sm"][:, 0:16], in_=ssum[:]), "g", None, 16)
                        p.mark(gp.dma_start(out=taps["sm"][:, 16:32], in_=srec[:]), "g", None, 16)
                        p.mark(gp.dma_start(out=taps["sm"][:, 32:48], in_=nmaxb[:]), "g", None, 16)
                        p.mark(gp.dma_start(out=taps["attnT"], in_=attnT[:]), "g", None, 16)
                    blk.gpsimd(p2_taps)
                if BUILD_UPTO == "p2":
                    def p_final0(gp):
                        gp.wait_ge(self.sem["g"], self.cnt["g"])
                    blk.gpsimd(p_final0)
                    return self.nc

                with ExitStack() as c3:
                    S3 = lambda n_, sh, dt: c3.enter_context(nc.sbuf_tensor(n_, sh, dt))
                    wo_f = S3("p3_wof", [128, D], F32)
                    wo_b = S3("p3_wob", [128, 2, D], BF16)
                    sst = S3("p3_sst", [128, 2, M], BF16)
                    self.emit_p3(P2_FIN, d_woT, d_sT, wob, attnT,
                                 wo_f, wo_b, sst, psA)

            P3_FIN = ["womm_31_1", "cvo_31", "sev_31_1", "ldo_31", "gsT_31"]
            if BUILD_UPTO == "p3":
                def p_final0(gp):
                    gp.wait_ge(self.sem["g"], self.cnt["g"])
                blk.gpsimd(p_final0)
                return self.nc

            with ExitStack() as c4:
                hT = c4.enter_context(nc.sbuf_tensor("b_hT", [128, HB, 512], BF16))
                prev = P3_FIN
                for mc in range(2):
                    with ExitStack() as cm1:
                        Sm = lambda n_, sh, dt: cm1.enter_context(
                            nc.sbuf_tensor(n_ + "_%d" % mc, sh, dt))
                        xnsl = Sm("p4_xnsl", [128, DB, 512], BF16)
                        pkr = Sm("p4_pkr", [128, 2, 2048], I32)
                        sci1 = Sm("p4_sci1", [128, 1024], I32)
                        sci3 = Sm("p4_sci3", [128, 1024], I32)
                        nat1 = Sm("p4_nat1", [128, D], BF16)
                        nat3 = Sm("p4_nat3", [128, D], BF16)
                        w1t = Sm("p4_w1t", [128, 2, D], BF16)
                        w3t = Sm("p4_w3t", [128, 2, D], BF16)
                        sc1 = Sm("p4_sc1", [128, 2, 64], F32)
                        sc3 = Sm("p4_sc3", [128, 2, 64], F32)
                        asil = Sm("p4_asil", [128, 2, 512], BF16)
                        bsb = Sm("p4_bsb", [128, 2, 512], BF16)
                        prev = self.emit_mm1(mc, prev, d_w1, d_w3, d_s1, d_s3,
                                             d_w1t, d_w3t, d_xnT, hT, xnsl, pkr,
                                             sci1, sci3, nat1, nat3, w1t, w3t,
                                             sc1, sc3, asil, bsb, psA, psB, ident)
                    if BUILD_UPTO == "mm1":
                        break
                    with ExitStack() as cm2:
                        gT = cm2.enter_context(
                            nc.sbuf_tensor("p5_gT_%d" % mc, [128, DB, 512], BF16))
                        with ExitStack() as cm2i:
                            Si = lambda n_, sh, dt: cm2i.enter_context(
                                nc.sbuf_tensor(n_ + "_%d" % mc, sh, dt))
                            pk2 = Si("p5_pk2", [128, 2, 2752], I32)
                            sci2 = Si("p5_sci2", [128, 1536], I32)
                            nat2 = Si("p5_nat2", [128, 5504], BF16)
                            w2t = Si("p5_w2t", [128, 2, HH], BF16)
                            sc2 = Si("p5_sc2", [128, 2, 172], F32)
                            prev = self.emit_mm2(mc, prev, d_w2, d_s2, d_w2t, hT,
                                                 pk2, sci2, nat2, w2t, sc2, gT,
                                                 psA, psB, ident)
                        if BUILD_UPTO == "mm2":
                            break
                        with ExitStack() as cm3:
                            Sc = lambda n_, sh, dt: cm3.enter_context(
                                nc.sbuf_tensor(n_ + "_%d" % mc, sh, dt))
                            ssl = Sc("p6_ssl", [128, DB, 512], BF16)
                            xcf = Sc("p6_xcf", [128, 2, D], F32)
                            outst = Sc("p6_outst", [128, 2, 512], F32)
                            prev = self.emit_combine(mc, prev, d_sT, d_xown,
                                                     d_out, gT, ssl, xcf,
                                                     outst, psB, ident)
                    if BUILD_UPTO in ("mm1", "mm2", "cb0"):
                        break

                def p_final(gp):
                    gp.wait_ge(self.sem["g"], self.cnt["g"])
                blk.gpsimd(p_final)

        return self.nc

    def emit_p1(self, d_xkv, d_xown, d_wkT, d_wvT, d_ident, d_mask,
                d_wqb, d_wob, d_wkb, d_wvb,
                ident, mask, wqb, wqbs, wob, wkb, wvb, wkT, wvT, kT,
                vtok, s1w, s2w, mnw, vrw, rsw,
                xnownT, xf, xnb, xnkv, vT, psA, psB):
        p = self
        blk = self.blk

        def sync_f(sync):
            for dst, src in ((ident, d_ident), (mask, d_mask), (wqb, d_wqb),
                             (wob, d_wob), (wkb, d_wkb), (wvb, d_wvb)):
                p.mark(sync.dma_start(out=dst[:], in_=src), "q",
                       "ld_wqb" if dst is wqb else None, 16)
            p.mark(sync.dma_start(
                out=xf[:, 0, :].rearrange("p (b f) -> p b f", b=DB),
                in_=d_wkT.rearrange("(b p) f -> p b f", p=128)),
                "q", "ld_wk", 16)
            p.mark(sync.dma_start(
                out=xf[:, 1, :].rearrange("p (b f) -> p b f", b=DB),
                in_=d_wvT.rearrange("(b p) f -> p b f", p=128)),
                "q", "ld_wv", 16)
            for t in range(NT):
                if t == 0:
                    p.wait(sync, "cv_wk")
                elif t == 1:
                    p.wait(sync, "cv_wv")
                else:
                    if t >= 3:
                        p.wait(sync, "stt_%d" % (t - 3))
                        p.wait(sync, "sq_%d" % (t - 3))
                src = (d_xkv[t * 128:(t + 1) * 128, :] if t < 16
                       else d_xown[(t - 16) * 128:(t - 15) * 128, :])
                p.mark(sync.dma_start(out=xf[:, t % 3, :], in_=src),
                       "q", "ld_x_%d" % t, 16)

        def scalar_f(scalar):
            for t in range(NT):
                p.wait(scalar, "ld_x_%d" % t)
                if t >= 2:
                    p.wait(scalar, "xtr_%d" % (t - 2))
                p.mark(scalar.activation(out=xnb[:, t % 2, :], in_=xf[:, t % 3, :],
                                         func=AF.Square, bias=0.0, scale=1.0,
                                         accum_out=s2w[:, t:t + 1]),
                       "s", "sq_%d" % t)
                if t % 2 == 1:
                    g = t // 2
                    c0 = g * 2
                    p.wait(scalar, "var_%d" % g)
                    p.mark(scalar.activation(out=rsw[:, c0:c0 + 2],
                                             in_=vrw[:, c0:c0 + 2],
                                             func=AF.Sqrt, bias=0.0, scale=1.0),
                           "s", "sqrt_%d" % g)
                if t < 16 and t % 4 == 3:
                    mc4 = t // 4
                    p.wait(scalar, "kmm_%d" % mc4)
                    p.mark(scalar.activation(out=kT[:, mc4 * 512:(mc4 + 1) * 512],
                                             in_=psA[:, 0:512], func=AF.Identity,
                                             bias=wkb[:], scale=1.0),
                           "s", "kev_%d" % mc4)
                    p.wait(scalar, "vmm_%d" % mc4)
                    p.mark(scalar.activation(out=vT[:, mc4 * 512:(mc4 + 1) * 512],
                                             in_=psA[:, 512:1024], func=AF.Identity,
                                             bias=wvb[:], scale=1.0),
                           "s", "vev_%d" % mc4)

        def vector_f(vector):
            vector.memset(self.junk[:], 0.0)
            for buf in (s1w, s2w, mnw, vrw, rsw):
                vector.memset(buf[:], 0.0)
            p.wait(vector, "ld_wqb")
            vector.tensor_scalar(out=wqbs[:], in0=wqb[:], scalar1=C_ATT,
                                 scalar2=None, op0=ALU.mult)
            p.wait(vector, "ld_wk")
            p.mark(vector.tensor_copy(out=wkT[:], in_=xf[:, 0, :]), "v", "cv_wk")
            p.wait(vector, "ld_wv")
            p.mark(vector.tensor_copy(out=wvT[:], in_=xf[:, 1, :]), "v", "cv_wv")
            for t in range(NT):
                p.wait(vector, "ld_x_%d" % t)
                p.mark(vector.reduce_sum(out=s1w[:, t:t + 1], in_=xf[:, t % 3, :],
                                         axis=AX.X), "v", "rsum_%d" % t)
                if t % 2 != 1:
                    continue
                g = t // 2
                c0 = g * 2
                p.wait(vector, "sq_%d" % t)
                self.vsp(vector)
                vector.tensor_scalar(out=mnw[:, c0:c0 + 2], in0=s1w[:, c0:c0 + 2],
                                     scalar1=1.0 / D, scalar2=None, op0=ALU.mult)
                vector.tensor_scalar(out=vrw[:, c0:c0 + 2], in0=s2w[:, c0:c0 + 2],
                                     scalar1=1.0 / D, scalar2=None, op0=ALU.mult)
                vector.tensor_mul(out=s2w[:, c0:c0 + 2], in0=mnw[:, c0:c0 + 2],
                                  in1=mnw[:, c0:c0 + 2])
                self.vsp(vector)
                p.mark(vector.scalar_tensor_tensor(
                    out=vrw[:, c0:c0 + 2], in0=vrw[:, c0:c0 + 2], scalar=1e-5,
                    in1=s2w[:, c0:c0 + 2], op0=ALU.add, op1=ALU.subtract),
                    "v", "var_%d" % g)
                p.wait(vector, "sqrt_%d" % g)
                vector.reciprocal(out=rsw[:, c0:c0 + 2], in_=rsw[:, c0:c0 + 2])
                self.vsp(vector)
                for tt in (t - 1, t):
                    p.mark(vector.scalar_tensor_tensor(
                        out=xnb[:, tt % 2, :], in0=xf[:, tt % 3, :],
                        scalar=mnw[:, tt:tt + 1],
                        in1=rsw[:, tt:tt + 1].broadcast_to([128, D]),
                        op0=ALU.subtract, op1=ALU.mult), "v", "stt_%d" % tt)
                    for r in range(4):
                        p.wait(vector, "xtrr_%d_%d" % (tt, r))
                        if tt < 16:
                            if tt >= 4:
                                p.wait(vector, "vmm_%d" % (tt // 4 - 1))
                            dst = xnkv[:, r * 8:(r + 1) * 8,
                                       (tt % 4) * 128:(tt % 4) * 128 + 128]
                        else:
                            dst = xnownT[:, r * 8:(r + 1) * 8,
                                         (tt - 16) * 128:(tt - 15) * 128]
                        p.mark(vector.tensor_copy(
                            out=dst,
                            in_=psB[:, (r % 2) * 1024:(r % 2) * 1024 + 1024]
                            .rearrange("p (b f) -> p b f", b=8)),
                            "v", "xev_%d_%d" % (tt, r))
            for r in range(2):
                p.wait(vector, "vtrr_%d" % r)
                p.mark(vector.tensor_copy(
                    out=vtok[:, r * 1024:(r + 1) * 1024],
                    in_=psB[:, (r % 2) * 1024:(r % 2) * 1024 + 1024]),
                    "v", "vtev_%d" % r)

        def tensor_f(tensor):
            for t in range(NT):
                p.wait(tensor, "stt_%d" % t)
                for r in range(4):
                    rid = t * 4 + r
                    if rid >= 2:
                        p.wait(tensor, "xev_%d_%d" % ((rid - 2) // 4, (rid - 2) % 4))
                    last = None
                    for qq in range(8):
                        q = r * 8 + qq
                        last = tensor.transpose(
                            psB[:, (r % 2) * 1024 + qq * 128:
                                (r % 2) * 1024 + (qq + 1) * 128],
                            xnb[:, t % 2, q * 128:(q + 1) * 128], ident[:])
                    p.mark(last, "t", "xtrr_%d_%d" % (t, r))
                self.evhere("xtr_%d" % t, "t")
                if t < 16 and t % 4 == 3:
                    mc4 = t // 4
                    p.wait(tensor, "cv_wk")
                    p.wait(tensor, "cv_wv")
                    p.wait(tensor, "xev_%d_3" % t)
                    if mc4 >= 1:
                        p.wait(tensor, "kev_%d" % (mc4 - 1))
                        p.wait(tensor, "vev_%d" % (mc4 - 1))
                    last = None
                    for dd in range(DB):
                        last = tensor.matmul(psA[:, 0:512],
                                             wkT[:, dd * 128:(dd + 1) * 128],
                                             xnkv[:, dd, :],
                                             start=(dd == 0), stop=(dd == DB - 1))
                    p.mark(last, "t", "kmm_%d" % mc4)
                    last = None
                    for dd in range(DB):
                        last = tensor.matmul(psA[:, 512:1024],
                                             wvT[:, dd * 128:(dd + 1) * 128],
                                             xnkv[:, dd, :],
                                             start=(dd == 0), stop=(dd == DB - 1))
                    p.mark(last, "t", "vmm_%d" % mc4)
            for r in range(2):
                p.wait(tensor, "vev_3")
                p.wait(tensor, "xev_23_3")
                if r == 1:
                    p.wait(tensor, "vtev_0")
                last = None
                for b8 in range(8):
                    kb = r * 8 + b8
                    last = tensor.transpose(
                        psB[:, (r % 2) * 1024 + b8 * 128:(r % 2) * 1024 + (b8 + 1) * 128],
                        vT[:, kb * 128:(kb + 1) * 128], ident[:])
                p.mark(last, "t", "vtrr_%d" % r)

        blk.sync(sync_f)
        blk.scalar(scalar_f)
        blk.vector(vector_f)
        blk.tensor(tensor_f)

    def emit_p2(self, FIN, d_wqT, d_xnT, ident, mask, wqbs, kT, vtok, xnownT,
                attnT, ssum, srec, nmaxb, wq_f, wq_b, QT, probs,
                probsT, psA, psB, psC):
        p = self
        blk = self.blk
        taps = self.taps

        def nrounds(i):
            return cdv(2 * i + 2, 8)

        def gp_f(gp):
            p.gate(gp, FIN)
            p.mark(gp.dma_start(out=d_xnT,
                                in_=xnownT[:].rearrange("p b m -> p (b m)")),
                   "g", "gxn", 16)

        def sync_f(sync):
            p.gate(sync, FIN)
            for h in range(H):
                if h >= 1:
                    p.wait(sync, "cvq_%d" % (h - 1))
                p.mark(sync.dma_start(
                    out=wq_f[:].rearrange("p (b f) -> p b f", b=DB),
                    in_=d_wqT[:, h * 128:(h + 1) * 128]
                    .rearrange("(b p) f -> p b f", p=128)),
                    "q", "ldq_%d" % h, 16)

        def vector_f(vector):
            p.gate(vector, FIN)
            for h in range(H):
                p.wait(vector, "ldq_%d" % h)
                if h >= 2:
                    p.wait(vector, "qmm_%d_1" % (h - 2))
                p.mark(vector.tensor_copy(out=wq_b[:, h % 2, :], in_=wq_f[:]),
                       "v", "cvq_%d" % h)
                for i in range(NSLOT):
                    ii = (h * NSLOT + i) % 16
                    Ki = 256 * (i + 1)
                    p.wait(vector, "smm_%d_%d" % (h, i))
                    if i >= 1:
                        p.wait(vector, "exp_%d_%d" % (h, i - 1))
                        vector.reciprocal(out=srec[:, ii - 1:ii],
                                          in_=ssum[:, ii - 1:ii])
                    else:
                        self.vsp(vector)
                    p.mark(vector.reduce_max(out=nmaxb[:, ii:ii + 1],
                                             in_=psA[:, 0:Ki],
                                             axis=AX.X, negate=True),
                           "v", "rmax_%d_%d" % (h, i))
                    if i >= 1:
                        Kp = 256 * i
                        p.mark(vector.tensor_scalar(
                            out=probs[:, (i - 1) % 2, 0:Kp],
                            in0=probs[:, (i - 1) % 2, 0:Kp],
                            scalar1=srec[:, ii - 1:ii], scalar2=None,
                            op0=ALU.mult), "v", "vnorm_%d_%d" % (h, i - 1))
                    else:
                        self.vsp(vector)
                    if i >= 1:
                        if i >= 3:
                            p.wait(vector, "avmm_%d_%d" % (h, i - 3))
                        elif h >= 1:
                            p.wait(vector, "avmm_%d_%d" % (h - 1, 5 + i))
                        for r in range(nrounds(i - 1)):
                            nb_r = min(8, 2 * i - r * 8)
                            p.wait(vector, "ttr_%d_%d_%d" % (h, i - 1, r))
                            p.mark(vector.tensor_copy(
                                out=probsT[:, (i - 1) % 2,
                                           r * 1024:r * 1024 + nb_r * 128],
                                in_=psB[:, (r % 2) * 1024:
                                        (r % 2) * 1024 + nb_r * 128]),
                                "v", "pev_%d_%d_%d" % (h, i - 1, r))
                ii7 = (h * NSLOT + 7) % 16
                p.wait(vector, "exp_%d_7" % h)
                vector.reciprocal(out=srec[:, ii7:ii7 + 1], in_=ssum[:, ii7:ii7 + 1])
                self.vsp(vector)
                p.mark(vector.tensor_scalar(
                    out=probs[:, 1, 0:2048], in0=probs[:, 1, 0:2048],
                    scalar1=srec[:, ii7:ii7 + 1], scalar2=None, op0=ALU.mult),
                    "v", "vnorm_%d_7" % h)
                p.wait(vector, "avmm_%d_5" % h)
                for r in range(2):
                    p.wait(vector, "ttr_%d_7_%d" % (h, r))
                    p.mark(vector.tensor_copy(
                        out=probsT[:, 1, r * 1024:(r + 1) * 1024],
                        in_=psB[:, (r % 2) * 1024:(r % 2) * 1024 + 1024]),
                        "v", "pev_%d_7_%d" % (h, r))

        def scalar_f(scalar):
            p.gate(scalar, FIN)
            for h in range(H):
                for mc in range(2):
                    p.wait(scalar, "qmm_%d_%d" % (h, mc))
                    if h >= 2:
                        p.wait(scalar, "smm_%d_7" % (h - 2))
                    p.mark(scalar.activation(
                        out=QT[:, h % 2, mc * 512:(mc + 1) * 512],
                        in_=psC[:, 512:1024], func=AF.Identity,
                        bias=wqbs[:, h:h + 1], scale=C_ATT),
                        "s", "qev_%d_%d" % (h, mc))
                for i in range(NSLOT):
                    ii = (h * NSLOT + i) % 16
                    Ki = 256 * (i + 1)
                    p.wait(scalar, "rmax_%d_%d" % (h, i))
                    if i >= 2:
                        p.wait(scalar, "ttr_last_%d_%d" % (h, i - 2))
                    elif h >= 1:
                        p.wait(scalar, "ttr_last_%d_%d" % (h - 1, 6 + i))
                    if h >= 2:
                        p.wait(scalar, "vnorm_%d_%d" % (h - 2, i))
                    p.mark(scalar.activation(
                        out=probs[:, i % 2, 0:Ki], in_=psA[:, 0:Ki],
                        func=AF.Exp, bias=nmaxb[:, ii:ii + 1], scale=1.0,
                        accum_out=ssum[:, ii:ii + 1]),
                        "s", "exp_%d_%d" % (h, i))
                    if i >= 1:
                        p.wait(scalar, "avmm_%d_%d" % (h, i - 1))
                        p.mark(scalar.activation(
                            out=attnT[:, h * M + (i - 1) * 128: h * M + i * 128],
                            in_=psC[:, ((i - 1) % 2) * 128:((i - 1) % 2) * 128 + 128],
                            func=AF.Identity, bias=0.0, scale=1.0),
                            "s", "avev_%d_%d" % (h, i - 1))
                p.wait(scalar, "avmm_%d_7" % h)
                p.mark(scalar.activation(
                    out=attnT[:, h * M + 7 * 128: h * M + 8 * 128],
                    in_=psC[:, 128:256], func=AF.Identity, bias=0.0, scale=1.0),
                    "s", "avev_%d_7" % h)

        def tensor_f(tensor):
            p.gate(tensor, FIN)

            def ptr_attnv(h, i):
                nblk = 2 * i + 2
                p.wait(tensor, "vnorm_%d_%d" % (h, i))
                for r in range(nrounds(i)):
                    nb_r = min(8, nblk - r * 8)
                    rid = self.p2_rounds
                    self.p2_rounds += 1
                    prev = self.p2_round_ev.get(rid - 2)
                    if prev is not None:
                        p.wait(tensor, prev)
                    self.p2_round_ev[rid] = "pev_%d_%d_%d" % (h, i, r)
                    last = None
                    for u in range(nb_r):
                        tb = r * 8 + u
                        last = tensor.transpose(
                            psB[:, (r % 2) * 1024 + u * 128:
                                (r % 2) * 1024 + (u + 1) * 128],
                            probs[:, i % 2, tb * 128:(tb + 1) * 128], ident[:])
                    p.mark(last, "t", "ttr_%d_%d_%d" % (h, i, r))
                self.evhere("ttr_last_%d_%d" % (h, i), "t")
                if i >= 2:
                    p.wait(tensor, "avev_%d_%d" % (h, i - 2))
                elif h >= 1:
                    p.wait(tensor, "avev_%d_%d" % (h - 1, 6 + i))
                last = None
                for tb in range(nblk):
                    if tb % 8 == 0:
                        p.wait(tensor, "pev_%d_%d_%d" % (h, i, tb // 8))
                    last = tensor.matmul(
                        psC[:, (i % 2) * 128:(i % 2) * 128 + 128],
                        vtok[:, tb * 128:(tb + 1) * 128],
                        probsT[:, i % 2, tb * 128:(tb + 1) * 128],
                        start=(tb == 0), stop=(tb == nblk - 1))
                p.mark(last, "t", "avmm_%d_%d" % (h, i))

            for h in range(H):
                p.wait(tensor, "cvq_%d" % h)
                for mc in range(2):
                    if h == 0 and mc == 0:
                        pass
                    elif mc == 1:
                        p.wait(tensor, "qev_%d_0" % h)
                    else:
                        p.wait(tensor, "qev_%d_1" % (h - 1))
                    last = None
                    for dd in range(DB):
                        last = tensor.matmul(
                            psC[:, 512:1024],
                            wq_b[:, h % 2, dd * 128:(dd + 1) * 128],
                            xnownT[:, dd, mc * 512:(mc + 1) * 512],
                            start=(dd == 0), stop=(dd == DB - 1))
                    p.mark(last, "t", "qmm_%d_%d" % (h, mc))
                for i in range(NSLOT):
                    Ki = 256 * (i + 1)
                    p.wait(tensor, "qev_%d_1" % h)
                    if i >= 1:
                        p.wait(tensor, "exp_%d_%d" % (h, i - 1))
                        p.wait(tensor, "rmax_%d_%d" % (h, i - 1))
                    elif h >= 1:
                        p.wait(tensor, "exp_%d_7" % (h - 1))
                        p.wait(tensor, "rmax_%d_7" % (h - 1))
                    nch = cdv(Ki, 512)
                    for c in range(nch):
                        nw = min(512, Ki - c * 512)
                        tensor.matmul(
                            psA[:, c * 512:c * 512 + nw],
                            QT[:, h % 2, i * 128:(i + 1) * 128],
                            kT[:, c * 512:c * 512 + nw], start=True,
                            stop=(c != nch - 1))
                    reg0 = (nch - 1) * 512
                    mrhs = mask[:, 0:512] if nw == 512 else mask[:, 256:512]
                    p.mark(tensor.matmul(psA[:, reg0:reg0 + nw], ident[:],
                                         mrhs, start=False, stop=True),
                           "t", "smm_%d_%d" % (h, i))
                    if i >= 1:
                        ptr_attnv(h, i - 1)
                ptr_attnv(h, 7)

        self.p2_rounds = 0
        self.p2_round_ev = {}
        blk.gpsimd(gp_f)
        blk.sync(sync_f)
        blk.vector(vector_f)
        blk.scalar(scalar_f)
        blk.tensor(tensor_f)

    def emit_p3(self, FIN, d_woT, d_sT, wob, attnT, wo_f, wo_b, sst, psA):
        p = self
        blk = self.blk
        taps = self.taps

        def sync_f(sync):
            p.gate(sync, FIN)
            for j in range(DB):
                if j >= 1:
                    p.wait(sync, "cvo_%d" % (j - 1))
                p.mark(sync.dma_start(
                    out=wo_f[:].rearrange("p (b f) -> p b f", b=DB)[:, 0:16, :],
                    in_=d_woT[0:2048, j * 128:(j + 1) * 128]
                    .rearrange("(b p) f -> p b f", p=128)),
                    "q", "ldo_%d" % j, 16)

        def vector_f(vector):
            p.gate(vector, FIN)
            for j in range(DB):
                p.wait(vector, "ldo_%d" % j)
                p.wait(vector, "ldo2_%d" % j)
                if j >= 2:
                    p.wait(vector, "womm_%d_1" % (j - 2))
                p.mark(vector.tensor_copy(out=wo_b[:, j % 2, :], in_=wo_f[:]),
                       "v", "cvo_%d" % j)

        def tensor_f(tensor):
            p.gate(tensor, FIN)
            for j in range(DB):
                p.wait(tensor, "cvo_%d" % j)
                for mc in range(2):
                    if j >= 2:
                        p.wait(tensor, "sev_%d_%d" % (j - 2, mc))
                    reg = (j % 2) * 1024 + mc * 512
                    last = None
                    for a in range(H):
                        last = tensor.matmul(
                            psA[:, reg:reg + 512],
                            wo_b[:, j % 2, a * 128:(a + 1) * 128],
                            attnT[:, a * M + mc * 512: a * M + (mc + 1) * 512],
                            start=(a == 0), stop=(a == H - 1))
                    p.mark(last, "t", "womm_%d_%d" % (j, mc))

        def scalar_f(scalar):
            p.gate(scalar, FIN)
            for j in range(DB):
                for mc in range(2):
                    p.wait(scalar, "womm_%d_%d" % (j, mc))
                    if j >= 2:
                        p.wait(scalar, "gsT_%d" % (j - 2))
                    reg = (j % 2) * 1024 + mc * 512
                    p.mark(scalar.activation(
                        out=sst[:, j % 2, mc * 512:(mc + 1) * 512],
                        in_=psA[:, reg:reg + 512], func=AF.Identity,
                        bias=wob[:, j:j + 1], scale=1.0),
                        "s", "sev_%d_%d" % (j, mc))

        def gp_f(gp):
            p.gate(gp, FIN)
            for j in range(DB):
                if j >= 1:
                    p.wait(gp, "cvo_%d" % (j - 1))
                p.mark(gp.dma_start(
                    out=wo_f[:].rearrange("p (b f) -> p b f", b=DB)[:, 16:32, :],
                    in_=d_woT[2048:4096, j * 128:(j + 1) * 128]
                    .rearrange("(b p) f -> p b f", p=128)),
                    "g", "ldo2_%d" % j, 16)
                if j >= 1:
                    p.wait(gp, "sev_%d_0" % (j - 1))
                    p.wait(gp, "sev_%d_1" % (j - 1))
                    p.mark(gp.dma_start(out=d_sT[:, (j - 1) * M:j * M],
                                        in_=sst[:, (j - 1) % 2, :]),
                           "g", "gsT_%d" % (j - 1), 16)
            p.wait(gp, "sev_31_0")
            p.wait(gp, "sev_31_1")
            p.mark(gp.dma_start(out=d_sT[:, 31 * M:32 * M],
                                in_=sst[:, 1, :]),
                   "g", "gsT_31", 16)
            if DEBUG_TAPS:
                p.wait(gp, "avev_31_7")
                p.mark(gp.dma_start(out=taps["attnT"], in_=attnT[:]), "g", None, 16)
                p.wait(gp, "gsT_31")
                p.mark(gp.dma_start(out=taps["sT"], in_=d_sT), "g", None, 16)

        blk.sync(sync_f)
        blk.vector(vector_f)
        blk.tensor(tensor_f)
        blk.scalar(scalar_f)
        blk.gpsimd(gp_f)

    def emit_mm1(self, mc, FIN, d_w1, d_w3, d_s1, d_s3, d_w1t, d_w3t,
                 d_xnT, hT, xnsl, pkr, sci1, sci3, nat1, nat3, w1t, w3t,
                 sc1, sc3, asil, bsb, psA, psB, ident):
        p = self
        blk = self.blk
        P = "m1%d_" % mc

        def sync_f(sync):
            p.gate(sync, FIN)
            if mc == 0:
                p.wait(sync, "gxn")
            else:
                p.wait(sync, "m10_mm_85")
            p.mark(sync.dma_start(
                out=xnsl[:],
                in_=d_xnT.rearrange("p (b m) -> p b m", b=DB)[:, :, mc * 512:(mc + 1) * 512]),
                "q", P + "ldxn", 16)
            for b in range(HB):
                if mc == 0:
                    if b >= 1:
                        p.wait(sync, P + "dqv1_%d" % (b - 1))
                    p.mark(sync.dma_start(
                        out=pkr[:, 0, :],
                        in_=d_w1[b * 4096:(b + 1) * 4096, :]
                        .rearrange("(p r) k -> p (r k)", p=128)),
                        "q", P + "ldw1_%d" % b, 16)
                    if b >= 2:
                        p.wait(sync, P + "dqh1_%d_1" % (b - 2))
                    p.mark(sync.dma_start(
                        out=sc1[:, b % 2, :],
                        in_=d_s1[b * 8192:(b + 1) * 8192]
                        .rearrange("(p g) -> p g", p=128)),
                        "q", P + "ldsc1_%d" % b, 16)
                else:
                    if b >= 2:
                        p.wait(sync, P + "mm_%d" % (b - 2))
                    p.mark(sync.dma_start(out=w1t[:, b % 2, :], in_=d_w1t[b]),
                           "q", P + "ldt1_%d" % b, 16)

        def gpl_f(gp):
            p.gate(gp, FIN)

            def gws(b):
                for which in (1, 3):
                    wdst = w1t if which == 1 else w3t
                    wdram = d_w1t if which == 1 else d_w3t
                    p.wait(gp, P + "wev%d_%d_3" % (which, b))
                    p.mark(gp.dma_start(out=wdram[b], in_=wdst[:, b % 2, :]),
                           "g", P + "gw%d_%d" % (which, b), 16)

            for b in range(HB):
                if mc == 0:
                    if b >= 1:
                        p.wait(gp, P + "dqv3_%d" % (b - 1))
                    p.mark(gp.dma_start(
                        out=pkr[:, 1, :],
                        in_=d_w3[b * 4096:(b + 1) * 4096, :]
                        .rearrange("(p r) k -> p (r k)", p=128)),
                        "g", P + "ldw3_%d" % b, 16)
                    if b >= 2:
                        p.wait(gp, P + "dqh3_%d_1" % (b - 2))
                    p.mark(gp.dma_start(
                        out=sc3[:, b % 2, :],
                        in_=d_s3[b * 8192:(b + 1) * 8192]
                        .rearrange("(p g) -> p g", p=128)),
                        "g", P + "ldsc3_%d" % b, 16)
                    if b >= 2:
                        gws(b - 2)
                else:
                    if b == 0:
                        p.wait(gp, P + "ldxn")
                    if b >= 2:
                        p.wait(gp, P + "mm_%d" % (b - 2))
                    p.mark(gp.dma_start(out=w3t[:, b % 2, :], in_=d_w3t[b]),
                           "g", P + "ldt3_%d" % b, 16)
            if mc == 0:
                gws(HB - 2)
                gws(HB - 1)

        def vec_f(vector):
            p.gate(vector, FIN)
            for b in range(HB):
                if mc == 0:
                    p.wait(vector, P + "ldw1_%d" % b)
                    p.wait(vector, P + "ldsc1_%d" % b)
                    p.wait(vector, P + "ldw3_%d" % b)
                    p.wait(vector, P + "ldsc3_%d" % b)
                    pk1v = pkr[:, 0, :].rearrange("p (q j) -> p q j", q=32)
                    pk3v = pkr[:, 1, :].rearrange("p (q j) -> p q j", q=32)
                    m1 = nat1[:].rearrange("p (q n j) -> p q n j", q=32, n=2)
                    m3 = nat3[:].rearrange("p (q n j) -> p q n j", q=32, n=2)
                    s1v = sc1[:, b % 2, :].rearrange("p (q n) -> p q n", n=2)
                    s3v = sc3[:, b % 2, :].rearrange("p (q n) -> p q n", n=2)
                    sc1v = sci1[:].rearrange("p (q j) -> p q j", q=16)
                    sc3v = sci3[:].rearrange("p (q j) -> p q j", q=16)
                    for h in (0, 1):
                        q0, q1 = 16 * h, 16 * h + 16
                        if b >= 1:
                            p.wait(vector, P + "tr1_%d_%d" % (b - 1, h))
                            p.wait(vector, P + "tr3_%d_%d" % (b - 1, h))
                        vector.tensor_scalar(
                            out=sc1v, in0=pk1v[:, q0:q1, :], scalar1=4,
                            scalar2=None, op0=ALU.arith_shift_right)
                        vector.tensor_scalar(
                            out=sc3v, in0=pk3v[:, q0:q1, :], scalar1=4,
                            scalar2=None, op0=ALU.arith_shift_right)
                        vector.tensor_mul(
                            out=m1[:, q0:q1, 0, :], in0=sc1v,
                            in1=s1v[:, q0:q1, 0:1].broadcast_to([128, 16, 64]))
                        vector.tensor_mul(
                            out=m3[:, q0:q1, 0, :], in0=sc3v,
                            in1=s3v[:, q0:q1, 0:1].broadcast_to([128, 16, 64]))
                        ax1 = vector.tensor_scalar(
                            out=sc1v, in0=pk1v[:, q0:q1, :], scalar1=15,
                            scalar2=8, op0=ALU.bitwise_and, op1=ALU.bitwise_xor)
                        if h == 1:
                            p.mark(ax1, "v", P + "dqv1_%d" % b)
                        ax3 = vector.tensor_scalar(
                            out=sc3v, in0=pk3v[:, q0:q1, :], scalar1=15,
                            scalar2=8, op0=ALU.bitwise_and, op1=ALU.bitwise_xor)
                        if h == 1:
                            p.mark(ax3, "v", P + "dqv3_%d" % b)
                        p.mark(vector.scalar_tensor_tensor(
                            out=m1[:, q0:q1, 1, :], in0=sc1v, scalar=-8.0,
                            in1=s1v[:, q0:q1, 1:2].broadcast_to([128, 16, 64]),
                            op0=ALU.add, op1=ALU.mult),
                            "v", P + "dqh1_%d_%d" % (b, h))
                        p.mark(vector.scalar_tensor_tensor(
                            out=m3[:, q0:q1, 1, :], in0=sc3v, scalar=-8.0,
                            in1=s3v[:, q0:q1, 1:2].broadcast_to([128, 16, 64]),
                            op0=ALU.add, op1=ALU.mult),
                            "v", P + "dqh3_%d_%d" % (b, h))
                if b >= 1:
                    p.wait(vector, P + "abev_%d" % (b - 1))
                    p.mark(vector.tensor_mul(
                        out=hT[:, b - 1, :], in0=asil[:, (b - 1) % 2, :],
                        in1=bsb[:, (b - 1) % 2, :]),
                        "v", P + "hTT_%d" % (b - 1))
            p.wait(vector, P + "abev_85")
            p.mark(vector.tensor_mul(
                out=hT[:, 85, :], in0=asil[:, 1, :], in1=bsb[:, 1, :]),
                "v", P + "hTT_85")

        def ten_f(tensor):
            p.gate(tensor, FIN)
            p.wait(tensor, P + "ldxn")
            rid = [0]
            rev = {}

            def tr_mat(b, which):
                nat = nat1 if which == 1 else nat3
                for h in (0, 1):
                    p.wait(tensor, P + "dqh%d_%d_%d" % (which, b, h))
                    for r in (2 * h, 2 * h + 1):
                        k = rid[0]
                        rid[0] += 1
                        if k >= 2:
                            p.wait(tensor, rev[k - 2])
                        rev[k] = P + "wev%d_%d_%d" % (which, b, r)
                        last = None
                        for u in range(8):
                            q = r * 8 + u
                            last = tensor.transpose(
                                psB[:, (k % 2) * 1024 + u * 128:(k % 2) * 1024 + (u + 1) * 128],
                                nat[:, q * 128:(q + 1) * 128], ident[:])
                        p.mark(last, "t", P + "wtr%d_%d_%d" % (which, b, r))
                    self.evhere(P + "tr%d_%d_%d" % (which, b, h), "t")

            def mms(b):
                if mc == 0:
                    p.wait(tensor, P + "wev1_%d_3" % b)
                    p.wait(tensor, P + "wev3_%d_3" % b)
                else:
                    p.wait(tensor, P + "ldt1_%d" % b)
                    p.wait(tensor, P + "ldt3_%d" % b)
                if b >= 2:
                    p.wait(tensor, P + "abev_%d" % (b - 2))
                reg_a = (b % 2) * 1024
                reg_b = (b % 2) * 1024 + 512
                last = None
                for dd in range(DB):
                    last = tensor.matmul(psA[:, reg_a:reg_a + 512],
                                         w1t[:, b % 2, dd * 128:(dd + 1) * 128],
                                         xnsl[:, dd, :],
                                         start=(dd == 0), stop=(dd == DB - 1))
                for dd in range(DB):
                    last = tensor.matmul(psA[:, reg_b:reg_b + 512],
                                         w3t[:, b % 2, dd * 128:(dd + 1) * 128],
                                         xnsl[:, dd, :],
                                         start=(dd == 0), stop=(dd == DB - 1))
                p.mark(last, "t", P + "mm_%d" % b)

            if mc == 0:
                for b in range(HB):
                    tr_mat(b, 1)
                    tr_mat(b, 3)
                    if b >= 1:
                        mms(b - 1)
                mms(85)
            else:
                for b in range(HB):
                    mms(b)

        def sca_f(scalar):
            p.gate(scalar, FIN)
            for b in range(HB):
                if mc == 0:
                    for which in (1, 3):
                        wdst = w1t if which == 1 else w3t
                        for r in range(4):
                            p.wait(scalar, P + "wtr%d_%d_%d" % (which, b, r))
                            if b >= 2 and r == 0:
                                p.wait(scalar, P + "mm_%d" % (b - 2))
                                p.wait(scalar, P + "gw%d_%d" % (which, b - 2))
                            k = b * 8 + (0 if which == 1 else 4) + r
                            p.mark(scalar.activation(
                                out=wdst[:, b % 2, r * 1024:(r + 1) * 1024],
                                in_=psB[:, (k % 2) * 1024:(k % 2) * 1024 + 1024],
                                func=AF.Identity, bias=0.0, scale=1.0),
                                "s", P + "wev%d_%d_%d" % (which, b, r))
                if b >= 1:
                    bb = b - 1
                    p.wait(scalar, P + "mm_%d" % bb)
                    if bb >= 2:
                        p.wait(scalar, P + "hTT_%d" % (bb - 2))
                    reg_a = (bb % 2) * 1024
                    reg_b = (bb % 2) * 1024 + 512
                    scalar.activation(out=asil[:, bb % 2, :],
                                      in_=psA[:, reg_a:reg_a + 512],
                                      func=AF.Silu, bias=0.0, scale=1.0)
                    p.mark(scalar.activation(out=bsb[:, bb % 2, :],
                                             in_=psA[:, reg_b:reg_b + 512],
                                             func=AF.Identity, bias=0.0, scale=1.0),
                           "s", P + "abev_%d" % bb)
            p.wait(scalar, P + "mm_85")
            p.wait(scalar, P + "hTT_83")
            scalar.activation(out=asil[:, 1, :], in_=psA[:, 1024:1536],
                              func=AF.Silu, bias=0.0, scale=1.0)
            p.mark(scalar.activation(out=bsb[:, 1, :], in_=psA[:, 1536:2048],
                                     func=AF.Identity, bias=0.0, scale=1.0),
                   "s", P + "abev_85")

        blk.sync(sync_f)
        blk.gpsimd(gpl_f)
        blk.vector(vec_f)
        blk.tensor(ten_f)
        blk.scalar(sca_f)

        fin = [P + "mm_85", P + "hTT_85", P + "abev_85", P + "ldxn"]
        if mc == 0:
            fin += [P + "gw1_85", P + "gw3_85", P + "ldw3_85"]
        else:
            fin += [P + "ldt3_85"]
        return fin

    def emit_mm2(self, mc, FIN, d_w2, d_s2, d_w2t, hT, pk2, sci2, nat2, w2t,
                 sc2, gT, psA, psB, ident):
        p = self
        blk = self.blk
        taps = self.taps
        P = "m2%d_" % mc
        d_w2v = d_w2.rearrange("(a q) k -> a q k", q=HB)

        def sync_f(sync):
            p.gate(sync, FIN)
            for k in range(DB):
                if mc == 0:
                    if k >= 1:
                        p.wait(sync, P + "dqv_%d_0" % (k - 1))
                    p.mark(sync.dma_start(
                        out=pk2[:, 0, :].rearrange("p (q j) -> p q j", q=43),
                        in_=d_w2v[k * 128:(k + 1) * 128, 0:43, :]),
                        "q", P + "ldp_%d_0" % k, 16)
                    if k >= 2:
                        p.wait(sync, P + "dq_%d_1" % (k - 2))
                    p.mark(sync.dma_start(
                        out=sc2[:, k % 2, :],
                        in_=d_s2[k * 128 * 172:(k + 1) * 128 * 172]
                        .rearrange("(p g) -> p g", p=128)),
                        "q", P + "lds_%d" % k, 16)
                else:
                    if k >= 2:
                        p.wait(sync, P + "mm_%d" % (k - 2))
                    p.mark(sync.dma_start(out=w2t[:, k % 2, 0:5504],
                                          in_=d_w2t[k][:, 0:5504]),
                           "q", P + "ldt_%d_0" % k, 16)

        def gpl_f(gp):
            p.gate(gp, FIN)

            def gws(k):
                p.wait(gp, P + "wev_%d_1_5" % k)
                p.mark(gp.dma_start(out=d_w2t[k], in_=w2t[:, k % 2, :]),
                       "g", P + "gw_%d" % k, 16)

            for k in range(DB):
                if mc == 0:
                    if k >= 1:
                        p.wait(gp, P + "dqv_%d_1" % (k - 1))
                    p.mark(gp.dma_start(
                        out=pk2[:, 1, :].rearrange("p (q j) -> p q j", q=43),
                        in_=d_w2v[k * 128:(k + 1) * 128, 43:86, :]),
                        "g", P + "ldp_%d_1" % k, 16)
                    if k >= 2:
                        gws(k - 2)
                else:
                    if k == 0:
                        p.wait(gp, P + "ldt_0_0")
                    if k >= 2:
                        p.wait(gp, P + "mm_%d" % (k - 2))
                    p.mark(gp.dma_start(out=w2t[:, k % 2, 5504:11008],
                                        in_=d_w2t[k][:, 5504:11008]),
                           "g", P + "ldt_%d_1" % k, 16)
            if mc == 0:
                gws(DB - 2)
                gws(DB - 1)
            if DEBUG_TAPS and mc == 0:
                p.wait(gp, "m10_hTT_85")
                p.mark(gp.dma_start(out=taps["hT"],
                                    in_=hT[:].rearrange("p b m -> p (b m)")),
                       "g", None, 16)

        def vec_f(vector):
            p.gate(vector, FIN)
            if mc != 0:
                return
            for k in range(DB):
                for cc in range(2):
                    p.wait(vector, P + "ldp_%d_%d" % (k, cc))
                    p.wait(vector, P + "lds_%d" % k)
                    pkv = pk2[:, cc, :].rearrange("p (q j) -> p q j", q=43)
                    mo = nat2[:].rearrange("p (q n j) -> p q n j", q=43, n=2)
                    sv = sc2[:, k % 2, 86 * cc:86 * (cc + 1)].rearrange(
                        "p (q n) -> p q n", n=2)
                    sci0 = sci2[:, 0:1536].rearrange("p (q j) -> p q j", q=24)
                    sci1_ = sci2[:, 0:1216].rearrange("p (q j) -> p q j", q=19)
                    pk_, pc_ = (k, 0) if cc == 1 else (k - 1, 1)
                    if k > 0 or cc > 0:
                        p.wait(vector, P + "trh_%d_%d_%d" % (pk_, pc_, 0))
                    vector.tensor_scalar(
                        out=sci0, in0=pkv[:, 0:24, :], scalar1=4,
                        scalar2=None, op0=ALU.arith_shift_right)
                    self.vsp(vector, 1)
                    vector.tensor_mul(
                        out=mo[:, 0:24, 0, :], in0=sci0,
                        in1=sv[:, 0:24, 0:1].broadcast_to([128, 24, 64]))
                    vector.tensor_scalar(
                        out=sci0, in0=pkv[:, 0:24, :],
                        scalar1=15, scalar2=8,
                        op0=ALU.bitwise_and, op1=ALU.bitwise_xor)
                    self.vsp(vector, 1)
                    p.mark(vector.scalar_tensor_tensor(
                        out=mo[:, 0:24, 1, :], in0=sci0, scalar=-8.0,
                        in1=sv[:, 0:24, 1:2].broadcast_to([128, 24, 64]),
                        op0=ALU.add, op1=ALU.mult),
                        "v", P + "dqh_%d_%d_0" % (k, cc))
                    if k > 0 or cc > 0:
                        p.wait(vector, P + "trh_%d_%d_%d" % (pk_, pc_, 1))
                    vector.tensor_scalar(
                        out=sci1_, in0=pkv[:, 24:43, :], scalar1=4,
                        scalar2=None, op0=ALU.arith_shift_right)
                    self.vsp(vector, 1)
                    vector.tensor_mul(
                        out=mo[:, 24:43, 0, :], in0=sci1_,
                        in1=sv[:, 24:43, 0:1].broadcast_to([128, 19, 64]))
                    p.mark(vector.tensor_scalar(
                        out=sci1_, in0=pkv[:, 24:43, :],
                        scalar1=15, scalar2=8,
                        op0=ALU.bitwise_and, op1=ALU.bitwise_xor),
                        "v", P + "dqv_%d_%d" % (k, cc))
                    self.vsp(vector, 1)
                    p.mark(vector.scalar_tensor_tensor(
                        out=mo[:, 24:43, 1, :], in0=sci1_, scalar=-8.0,
                        in1=sv[:, 24:43, 1:2].broadcast_to([128, 19, 64]),
                        op0=ALU.add, op1=ALU.mult),
                        "v", P + "dqh_%d_%d_1" % (k, cc))
                    self.evhere(P + "dq_%d_%d" % (k, cc), "v")

        def ten_f(tensor):
            p.gate(tensor, FIN)
            p.wait(tensor, "m1%d_hTT_85" % mc)
            rid = [0]
            rev = {}

            def trs(k):
                for cc in range(2):
                    for h in (0, 1):
                        p.wait(tensor, P + "dqh_%d_%d_%d" % (k, cc, h))
                        for r in (range(0, 3) if h == 0 else range(3, 6)):
                            nb_r = min(8, 43 - r * 8)
                            kk = rid[0]
                            rid[0] += 1
                            if kk >= 2:
                                p.wait(tensor, rev[kk - 2])
                            rev[kk] = P + "wev_%d_%d_%d" % (k, cc, r)
                            last = None
                            for u in range(nb_r):
                                q = r * 8 + u
                                last = tensor.transpose(
                                    psB[:, (kk % 2) * 1024 + u * 128:
                                        (kk % 2) * 1024 + (u + 1) * 128],
                                    nat2[:, q * 128:(q + 1) * 128], ident[:])
                            p.mark(last, "t", P + "wtr_%d_%d_%d" % (k, cc, r))
                        self.evhere(P + "trh_%d_%d_%d" % (k, cc, h), "t")

            def mms(k):
                if mc == 0:
                    p.wait(tensor, P + "wev_%d_1_5" % k)
                else:
                    p.wait(tensor, P + "ldt_%d_0" % k)
                    p.wait(tensor, P + "ldt_%d_1" % k)
                if k >= 2:
                    p.wait(tensor, P + "gev_%d" % (k - 2))
                reg = (k % 2) * 512
                last = None
                for c in range(HB):
                    last = tensor.matmul(psA[:, reg:reg + 512],
                                         w2t[:, k % 2, c * 128:(c + 1) * 128],
                                         hT[:, c, :],
                                         start=(c == 0), stop=(c == HB - 1))
                p.mark(last, "t", P + "mm_%d" % k)

            if mc == 0:
                for k in range(DB):
                    trs(k)
                    if k >= 1:
                        mms(k - 1)
                mms(DB - 1)
            else:
                for k in range(DB):
                    mms(k)

        def sca_f(scalar):
            p.gate(scalar, FIN)

            def gev(k):
                p.wait(scalar, P + "mm_%d" % k)
                reg = (k % 2) * 512
                p.mark(scalar.activation(out=gT[:, k, :], in_=psA[:, reg:reg + 512],
                                         func=AF.Silu, bias=0.0, scale=1.0),
                       "s", P + "gev_%d" % k)

            for k in range(DB):
                if mc == 0:
                    for cc in range(2):
                        for r in range(6):
                            nb_r = min(8, 43 - r * 8)
                            p.wait(scalar, P + "wtr_%d_%d_%d" % (k, cc, r))
                            if k >= 2 and cc == 0 and r == 0:
                                p.wait(scalar, P + "mm_%d" % (k - 2))
                                p.wait(scalar, P + "gw_%d" % (k - 2))
                            kk = k * 12 + cc * 6 + r
                            p.mark(scalar.activation(
                                out=w2t[:, k % 2,
                                        (cc * 43 + r * 8) * 128:
                                        (cc * 43 + r * 8 + nb_r) * 128],
                                in_=psB[:, (kk % 2) * 1024:(kk % 2) * 1024 + nb_r * 128],
                                func=AF.Identity, bias=0.0, scale=1.0),
                                "s", P + "wev_%d_%d_%d" % (k, cc, r))
                    if k >= 1:
                        gev(k - 1)
                else:
                    gev(k)
            if mc == 0:
                gev(DB - 1)

        blk.sync(sync_f)
        blk.gpsimd(gpl_f)
        blk.vector(vec_f)
        blk.tensor(ten_f)
        blk.scalar(sca_f)

        fin = [P + "mm_31", P + "gev_31"]
        if mc == 0:
            fin += [P + "gw_31", P + "dq_31_1", P + "lds_31"]
        else:
            fin += [P + "ldt_31_0", P + "ldt_31_1"]
        return fin

    def emit_combine(self, mc, FIN, d_sT, d_xown, d_out, gT, ssl, xcf,
                     outst, psB, ident):
        p = self
        blk = self.blk
        taps = self.taps
        P = "cb%d_" % mc

        def sync_f(sync):
            p.gate(sync, FIN)
            p.mark(sync.dma_start(
                out=ssl[:],
                in_=d_sT.rearrange("p (b m) -> p b m", b=DB)[:, :, mc * 512:(mc + 1) * 512]),
                "q", P + "lds", 16)
            for t in range(4):
                if t >= 2:
                    p.wait(sync, P + "add_%d_7" % (t - 2))
                p.mark(sync.dma_start(
                    out=xcf[:, t % 2, :],
                    in_=d_xown[(mc * 4 + t) * 128:(mc * 4 + t + 1) * 128, :]),
                    "q", P + "ldx_%d" % t, 16)

        def vec_f(vector):
            p.gate(vector, FIN)
            p.wait(vector, P + "lds")
            for c in range(8):
                p.mark(vector.tensor_mul(out=gT[:, c * 4:(c + 1) * 4, :],
                                         in0=gT[:, c * 4:(c + 1) * 4, :],
                                         in1=ssl[:, c * 4:(c + 1) * 4, :]),
                       "v", P + "ptt_%d" % c)
            for t in range(4):
                p.wait(vector, P + "ldx_%d" % t)
                for rr in range(8):
                    p.wait(vector, P + "ctr_%d_%d" % (t, rr))
                    if rr >= 2:
                        p.wait(vector, P + "out_%d_%d" % (t, rr - 2))
                    elif t >= 1:
                        p.wait(vector, P + "out_%d_%d" % (t - 1, 6 + rr))
                    p.mark(vector.tensor_add(
                        out=outst[:, rr % 2, :],
                        in0=psB[:, (rr % 2) * 1024:(rr % 2) * 1024 + 512],
                        in1=xcf[:, t % 2, rr * 512:(rr + 1) * 512]),
                        "v", P + "add_%d_%d" % (t, rr))

        def ten_f(tensor):
            p.gate(tensor, FIN)
            for t in range(4):
                for rr in range(8):
                    p.wait(tensor, P + "ptt_%d" % rr)
                    rid = t * 8 + rr
                    if rid >= 2:
                        p.wait(tensor, P + "add_%d_%d" % ((rid - 2) // 8, (rid - 2) % 8))
                    last = None
                    for u in range(4):
                        k = rr * 4 + u
                        last = tensor.transpose(
                            psB[:, (rr % 2) * 1024 + u * 128:(rr % 2) * 1024 + (u + 1) * 128],
                            gT[:, k, t * 128:(t + 1) * 128], ident[:])
                    p.mark(last, "t", P + "ctr_%d_%d" % (t, rr))

        def gp_f(gp):
            p.gate(gp, FIN)
            for t in range(4):
                for rr in range(8):
                    p.wait(gp, P + "add_%d_%d" % (t, rr))
                    p.mark(gp.dma_start(
                        out=d_out[(mc * 4 + t) * 128:(mc * 4 + t + 1) * 128,
                                  rr * 512:(rr + 1) * 512],
                        in_=outst[:, rr % 2, :]),
                        "g", P + "out_%d_%d" % (t, rr), 16)
            if DEBUG_TAPS and mc == 0:
                p.wait(gp, "m20_gev_31")
                p.mark(gp.dma_start(out=taps["gT"],
                                    in_=gT[:].rearrange("p b m -> p (b m)")),
                       "g", None, 16)

        blk.sync(sync_f)
        blk.vector(vec_f)
        blk.tensor(ten_f)
        blk.gpsimd(gp_f)

        return [P + "ctr_3_7", P + "add_3_7", P + "out_3_7", P + "ldx_3"]


def build_program():
    b1 = Builder(events=None)
    b1.build()
    b2 = Builder(events=b1.rec)
    nc = b2.build()
    assert b2.cnt == b1.cnt, (b1.cnt, b2.cnt)
    return nc


_NC_CACHE = {}
_LAST_RES = {}


def _get_nc():
    if "nc" not in _NC_CACHE:
        _NC_CACHE["nc"] = build_program()
    return _NC_CACHE["nc"]


def _make_mask(par):
    q = np.arange(128)[:, None]
    k = np.arange(128)[None, :]
    tri = np.where(k <= q, 0.0, -30000.0).astype(np.float32)
    m = np.zeros((128, 512), np.float32)
    if par == 0:
        m[:, 256:384] = tri
        m[:, 384:512] = -30000.0
    else:
        m[:, 384:512] = tri
    return np.ascontiguousarray(m.astype(bfdt))


def kernel(L0_attn_x, L0_attn_wq_w, L0_attn_wq_b, L0_attn_wk_w, L0_attn_wk_b,
           L0_attn_wv_w, L0_attn_wv_b, L0_attn_wo_w, L0_attn_wo_b,
           L0_ffn_w1, L0_ffn_s1, L0_ffn_w2, L0_ffn_s2, L0_ffn_w3, L0_ffn_s3):
    x = np.ascontiguousarray(np.asarray(L0_attn_x, dtype=np.float32))
    wqT = np.ascontiguousarray(np.asarray(L0_attn_wq_w, np.float32).T)
    wkT = np.ascontiguousarray(np.asarray(L0_attn_wk_w, np.float32).T)
    wvT = np.ascontiguousarray(np.asarray(L0_attn_wv_w, np.float32).T)
    woT = np.ascontiguousarray(np.asarray(L0_attn_wo_w, np.float32).T)
    wqb = np.ascontiguousarray(np.asarray(L0_attn_wq_b, np.float32).reshape(32, 128).T)
    wob = np.ascontiguousarray(np.asarray(L0_attn_wo_b, np.float32).reshape(32, 128).T)
    wkb = np.ascontiguousarray(np.asarray(L0_attn_wk_b, np.float32).reshape(128, 1))
    wvb = np.ascontiguousarray(np.asarray(L0_attn_wv_b, np.float32).reshape(128, 1))
    w1 = np.ascontiguousarray(np.asarray(L0_ffn_w1, np.int32))
    w2 = np.ascontiguousarray(np.asarray(L0_ffn_w2, np.int32))
    w3 = np.ascontiguousarray(np.asarray(L0_ffn_w3, np.int32))
    s1 = np.ascontiguousarray(np.asarray(L0_ffn_s1, np.float32))
    s2 = np.ascontiguousarray(np.asarray(L0_ffn_s2, np.float32))
    s3 = np.ascontiguousarray(np.asarray(L0_ffn_s3, np.float32))

    masks = [_make_mask(0), _make_mask(1)]
    in_maps = []
    for c in range(8):
        b, par = c // 2, c % 2
        xb = np.ascontiguousarray(x[b])
        xown = np.ascontiguousarray(xb.reshape(16, 128, D)[par::2].reshape(M, D))
        in_maps.append(dict(
            xkv=xb, xown=xown, wqT=wqT, wkT=wkT, wvT=wvT, woT=woT,
            wqb=wqb, wob=wob, wkb=wkb, wvb=wvb,
            w1p=w1, w2p=w2, w3p=w3, s1=s1, s2=s2, s3=s3,
            mask=masks[par]))

    nc = _get_nc()
    res = run_bass_kernel_spmd(nc, in_maps, core_ids=list(range(8)))
    _LAST_RES["res"] = res

    out = np.empty((B, T, D), np.float32)
    for c in range(8):
        b, par = c // 2, c % 2
        oc = np.asarray(res.results[c]["out"]).reshape(NSLOT, 128, D)
        out.reshape(B, 16, 128, D)[b, par::2] = oc
    return out


# version 09
# speedup vs baseline: 1.1535x; 1.1414x over previous
import numpy as np
import ml_dtypes
import concourse.bass as bass
import concourse.mybir as mybir
from concourse.bass_utils import run_bass_kernel_spmd
from contextlib import ExitStack

F32 = mybir.dt.float32
BF16 = mybir.dt.bfloat16
I32 = mybir.dt.int32
AF = mybir.ActivationFunctionType
ALU = mybir.AluOpType
AX = mybir.AxisListType
bfdt = ml_dtypes.bfloat16

B, T, D = 4, 2048, 4096
H, HD = 32, 128
HH = 11008
M = 1024
NSLOT = 8
C_ATT = 1.0 / float(np.sqrt(128.0))
DB = 32
HB = 86
NT = 24

DEBUG_TAPS = False
BUILD_UPTO = "all"
DETECT_RACES = True


def cdv(a, b):
    return (a + b - 1) // b


class Builder:
    def __init__(self, events=None):
        self.nc = nc = bass.Bass("TRN2", target_bir_lowering=False,
                                 detect_race_conditions=DETECT_RACES)
        self.events = events
        self.rec = {}
        self.sem = {}
        self.cnt = {}
        for nm in ("t", "v", "s", "g", "q"):
            self.sem[nm] = nc.alloc_semaphore("sem_" + nm)
            self.cnt[nm] = 0

    def mark(self, instr, key, name=None, k=1):
        instr.then_inc(self.sem[key], k)
        self.cnt[key] += k
        if name is not None:
            assert name not in self.rec, "dup event " + name
            self.rec[name] = (key, self.cnt[key])

    def evhere(self, name, key):
        if name not in self.rec:
            self.rec[name] = (key, self.cnt[key])

    def wait(self, engine, name):
        if self.events is None:
            return
        key, n = self.events[name]
        if n > 0:
            engine.wait_ge(self.sem[key], n)

    def gate(self, engine, names):
        for nm in names:
            self.wait(engine, nm)

    def vsp(self, v, n=2):
        for _ in range(n):
            v.tensor_scalar(out=self.junk[:, 0:8], in0=self.junk[:, 0:8],
                            scalar1=1.0, scalar2=None, op0=ALU.mult)

    def build(self):
        nc = self.nc
        p = self

        d_xkv = nc.dram_tensor("xkv", [T, D], F32, kind="ExternalInput").ap()
        d_xown = nc.dram_tensor("xown", [M, D], F32, kind="ExternalInput").ap()
        d_wqT = nc.dram_tensor("wqT", [D, D], F32, kind="ExternalInput").ap()
        d_wkT = nc.dram_tensor("wkT", [D, HD], F32, kind="ExternalInput").ap()
        d_wvT = nc.dram_tensor("wvT", [D, HD], F32, kind="ExternalInput").ap()
        d_woT = nc.dram_tensor("woT", [D, D], F32, kind="ExternalInput").ap()
        d_wqb = nc.dram_tensor("wqb", [128, 32], F32, kind="ExternalInput").ap()
        d_wob = nc.dram_tensor("wob", [128, 32], F32, kind="ExternalInput").ap()
        d_wkb = nc.dram_tensor("wkb", [128, 1], F32, kind="ExternalInput").ap()
        d_wvb = nc.dram_tensor("wvb", [128, 1], F32, kind="ExternalInput").ap()
        d_w1 = nc.dram_tensor("w1p", [352256, 64], I32, kind="ExternalInput").ap()
        d_w2 = nc.dram_tensor("w2p", [352256, 64], I32, kind="ExternalInput").ap()
        d_w3 = nc.dram_tensor("w3p", [352256, 64], I32, kind="ExternalInput").ap()
        d_s1 = nc.dram_tensor("s1", [704512], F32, kind="ExternalInput").ap()
        d_s2 = nc.dram_tensor("s2", [704512], F32, kind="ExternalInput").ap()
        d_s3 = nc.dram_tensor("s3", [704512], F32, kind="ExternalInput").ap()
        d_mask = nc.dram_tensor("mask", [128, 512], BF16, kind="ExternalInput").ap()
        d_out = nc.dram_tensor("out", [M, D], F32, kind="ExternalOutput").ap()
        d_ident = nc.inline_tensor(np.eye(128, dtype=bfdt), "ident_const").ap()

        d_xnT = nc.dram_tensor("xnT_sc", [128, DB * M], BF16, kind="Internal").ap()
        d_sT = nc.dram_tensor("sT_sc", [128, DB * M], BF16, kind="Internal").ap()
        d_w1t = nc.dram_tensor("w1t_sc", [HB, 128, D], BF16, kind="Internal").ap()
        d_w3t = nc.dram_tensor("w3t_sc", [HB, 128, D], BF16, kind="Internal").ap()
        d_w2t = nc.dram_tensor("w2t_sc", [DB, 128, HH], BF16, kind="Internal").ap()

        taps = {}
        if DEBUG_TAPS:
            taps["kT"] = nc.dram_tensor("tap_kT", [128, T], BF16, kind="ExternalOutput").ap()
            taps["vtok"] = nc.dram_tensor("tap_vtok", [128, T], BF16, kind="ExternalOutput").ap()
            taps["xnT"] = nc.dram_tensor("tap_xnT", [128, DB * M], BF16, kind="ExternalOutput").ap()
            taps["attnT"] = nc.dram_tensor("tap_attnT", [128, H * M], BF16, kind="ExternalOutput").ap()
            taps["sT"] = nc.dram_tensor("tap_sT", [128, DB * M], BF16, kind="ExternalOutput").ap()
            taps["hT"] = nc.dram_tensor("tap_hT", [128, HB * 512], BF16, kind="ExternalOutput").ap()
            taps["gT"] = nc.dram_tensor("tap_gT", [128, DB * 512], BF16, kind="ExternalOutput").ap()
            taps["st"] = nc.dram_tensor("tap_st", [128, 160], F32, kind="ExternalOutput").ap()
            taps["QT"] = nc.dram_tensor("tap_QT", [128, 2048], BF16, kind="ExternalOutput").ap()
            taps["probs"] = nc.dram_tensor("tap_probs", [128, 2048], BF16, kind="ExternalOutput").ap()
            taps["pT"] = nc.dram_tensor("tap_pT", [128, 2048], BF16, kind="ExternalOutput").ap()
            taps["sm"] = nc.dram_tensor("tap_sm", [128, 64], F32, kind="ExternalOutput").ap()
        self.taps = taps

        ctx = ExitStack()
        with ctx:
            S = lambda name, shape, dt: ctx.enter_context(nc.sbuf_tensor(name, shape, dt))
            ident = S("c_ident", [128, 128], BF16)
            mask = S("c_mask", [128, 512], BF16)
            wqbs = S("c_wqbs", [128, 32], F32)
            wqb = S("c_wqb", [128, 32], F32)
            wob = S("c_wob", [128, 32], F32)
            wkb = S("c_wkb", [128, 1], F32)
            wvb = S("c_wvb", [128, 1], F32)
            self.junk = S("c_junk", [128, 8], F32)
            s1w = S("c_s1w", [128, 32], F32)
            s2w = S("c_s2w", [128, 32], F32)
            mnw = S("c_mnw", [128, 32], F32)
            vrw = S("c_vrw", [128, 32], F32)
            rsw = S("c_rsw", [128, 32], F32)
            ssum = S("c_ssum", [128, 16], F32)
            srec = S("c_srec", [128, 16], F32)
            nmaxb = S("c_nmax", [128, 16], F32)

            psA = ctx.enter_context(nc.psum_tensor("psA", [128, 2048], F32))
            psB = ctx.enter_context(nc.psum_tensor("psB", [128, 2048], BF16))
            psC = ctx.enter_context(nc.psum_tensor("psC", [128, 1024], F32))

            blk = ctx.enter_context(nc.Block())
            self.blk = blk

            with ExitStack() as cAB:
                kT = cAB.enter_context(nc.sbuf_tensor("c_kT", [128, T], BF16))
                vtok = cAB.enter_context(nc.sbuf_tensor("c_vtok", [128, 16 * 128], BF16))
                xnownT = cAB.enter_context(nc.sbuf_tensor("b_xnownT", [128, DB, M], BF16))

                with ExitStack() as c1:
                    S1 = lambda n_, sh, dt: c1.enter_context(nc.sbuf_tensor(n_, sh, dt))
                    xf = S1("p1_xf", [128, 3, D], F32)
                    wkT = S1("c_wkT", [128, DB * 128], BF16)
                    wvT = S1("c_wvT", [128, DB * 128], BF16)
                    xnb = S1("p1_xnb", [128, 2, D], BF16)
                    xnkv = S1("p1_xnkv", [128, DB, 512], BF16)
                    vT = S1("p1_vT", [128, T], BF16)
                    self.emit_p1(d_xkv, d_xown, d_wkT, d_wvT, d_ident, d_mask,
                                 d_wqb, d_wob, d_wkb, d_wvb,
                                 ident, mask, wqb, wqbs, wob, wkb, wvb, wkT, wvT, kT,
                                 vtok, s1w, s2w, mnw, vrw, rsw,
                                 xnownT, xf, xnb, xnkv, vT, psA, psB)

                P1_FIN = ["vtrr_1", "vtev_1", "vev_3", "ld_x_23"]
                if DEBUG_TAPS:
                    def p1_taps(gp):
                        p.gate(gp, P1_FIN)
                        p.mark(gp.dma_start(out=taps["st"][:, 0:32], in_=s1w[:]), "g", None, 16)
                        p.mark(gp.dma_start(out=taps["st"][:, 32:64], in_=s2w[:]), "g", None, 16)
                        p.mark(gp.dma_start(out=taps["st"][:, 64:96], in_=mnw[:]), "g", None, 16)
                        p.mark(gp.dma_start(out=taps["st"][:, 96:128], in_=vrw[:]), "g", None, 16)
                        p.mark(gp.dma_start(out=taps["st"][:, 128:160], in_=rsw[:]), "g", None, 16)
                        p.mark(gp.dma_start(out=taps["kT"], in_=kT[:]), "g", None, 16)
                        p.mark(gp.dma_start(out=taps["vtok"], in_=vtok[:]), "g", None, 16)
                        p.mark(gp.dma_start(out=taps["xnT"],
                                            in_=xnownT[:].rearrange("p b m -> p (b m)")),
                               "g", None, 16)
                    blk.gpsimd(p1_taps)
                if BUILD_UPTO == "p1":
                    def p_final0(gp):
                        gp.wait_ge(self.sem["g"], self.cnt["g"])
                    blk.gpsimd(p_final0)
                    return self.nc

                attnT = cAB.enter_context(nc.sbuf_tensor("b_attnT", [128, H * M], BF16))

                with ExitStack() as c2:
                    S2 = lambda n_, sh, dt: c2.enter_context(nc.sbuf_tensor(n_, sh, dt))
                    wq_f = S2("p2_wqf", [128, D], F32)
                    wq_b = S2("p2_wqb", [128, 2, D], BF16)
                    QT = S2("p2_QT", [128, 2, M], BF16)
                    probs = S2("p2_probs", [128, 2, 2048], BF16)
                    probsT = S2("p2_probsT", [128, 2, 2048], BF16)
                    self.emit_p2(P1_FIN, d_wqT, d_xnT, ident, mask, wqbs,
                                 kT, vtok, xnownT, attnT,
                                 ssum, srec, nmaxb,
                                 wq_f, wq_b, QT, probs, probsT, psA, psB, psC)

                P2_FIN = ["avmm_31_7", "pev_31_7_1", "avev_31_7", "ldq_31", "gxn"]
                if DEBUG_TAPS:
                    def p2_taps(gp):
                        p.gate(gp, P2_FIN)
                        p.mark(gp.dma_start(out=taps["QT"][:, 0:1024], in_=QT[:, 1, :]), "g", None, 16)
                        p.mark(gp.dma_start(out=taps["probs"], in_=probs[:, 1, :]), "g", None, 16)
                        p.mark(gp.dma_start(out=taps["pT"], in_=probsT[:, 1, :]), "g", None, 16)
                        p.mark(gp.dma_start(out=taps["sm"][:, 0:16], in_=ssum[:]), "g", None, 16)
                        p.mark(gp.dma_start(out=taps["sm"][:, 16:32], in_=srec[:]), "g", None, 16)
                        p.mark(gp.dma_start(out=taps["sm"][:, 32:48], in_=nmaxb[:]), "g", None, 16)
                        p.mark(gp.dma_start(out=taps["attnT"], in_=attnT[:]), "g", None, 16)
                    blk.gpsimd(p2_taps)
                if BUILD_UPTO == "p2":
                    def p_final0(gp):
                        gp.wait_ge(self.sem["g"], self.cnt["g"])
                    blk.gpsimd(p_final0)
                    return self.nc

                with ExitStack() as c3:
                    S3 = lambda n_, sh, dt: c3.enter_context(nc.sbuf_tensor(n_, sh, dt))
                    wo_f = S3("p3_wof", [128, D], F32)
                    wo_b = S3("p3_wob", [128, 2, D], BF16)
                    sst = S3("p3_sst", [128, 2, M], BF16)
                    self.emit_p3(P2_FIN, d_woT, d_sT, wob, attnT,
                                 wo_f, wo_b, sst, psA)

            P3_FIN = ["womm_31_1", "cvo_31", "sev_31_1", "ldo_31", "gsT_31"]
            if BUILD_UPTO == "p3":
                def p_final0(gp):
                    gp.wait_ge(self.sem["g"], self.cnt["g"])
                blk.gpsimd(p_final0)
                return self.nc

            with ExitStack() as c4:
                hT = c4.enter_context(nc.sbuf_tensor("b_hT", [128, HB, 512], BF16))
                prev = P3_FIN
                for mc in range(2):
                    with ExitStack() as cm1:
                        Sm = lambda n_, sh, dt: cm1.enter_context(
                            nc.sbuf_tensor(n_ + "_%d" % mc, sh, dt))
                        xnsl = Sm("p4_xnsl", [128, DB, 512], BF16)
                        pkr = Sm("p4_pkr", [128, 2, 2048], I32)
                        sci1 = Sm("p4_sci1", [128, 1024], I32)
                        sci3 = Sm("p4_sci3", [128, 1024], I32)
                        nat1 = Sm("p4_nat1", [128, D], BF16)
                        nat3 = Sm("p4_nat3", [128, D], BF16)
                        w1t = Sm("p4_w1t", [128, 2, D], BF16)
                        w3t = Sm("p4_w3t", [128, 2, D], BF16)
                        sc1 = Sm("p4_sc1", [128, 2, 64], F32)
                        sc3 = Sm("p4_sc3", [128, 2, 64], F32)
                        asil = Sm("p4_asil", [128, 2, 512], BF16)
                        bsb = Sm("p4_bsb", [128, 2, 512], BF16)
                        prev = self.emit_mm1(mc, prev, d_w1, d_w3, d_s1, d_s3,
                                             d_w1t, d_w3t, d_xnT, hT, xnsl, pkr,
                                             sci1, sci3, nat1, nat3, w1t, w3t,
                                             sc1, sc3, asil, bsb, psA, psB, ident)
                    if BUILD_UPTO == "mm1":
                        break
                    with ExitStack() as cm2:
                        gT = cm2.enter_context(
                            nc.sbuf_tensor("p5_gT_%d" % mc, [128, DB, 512], BF16))
                        with ExitStack() as cm2i:
                            Si = lambda n_, sh, dt: cm2i.enter_context(
                                nc.sbuf_tensor(n_ + "_%d" % mc, sh, dt))
                            pk2 = Si("p5_pk2", [128, 2, 2752], I32)
                            sci2 = Si("p5_sci2", [128, 1536], I32)
                            nat2 = Si("p5_nat2", [128, 5504], BF16)
                            w2t = Si("p5_w2t", [128, 2, HH], BF16)
                            sc2 = Si("p5_sc2", [128, 2, 172], F32)
                            prev = self.emit_mm2(mc, prev, d_w2, d_s2, d_w2t, hT,
                                                 pk2, sci2, nat2, w2t, sc2, gT,
                                                 psA, psB, ident)
                        if BUILD_UPTO == "mm2":
                            break
                        with ExitStack() as cm3:
                            Sc = lambda n_, sh, dt: cm3.enter_context(
                                nc.sbuf_tensor(n_ + "_%d" % mc, sh, dt))
                            ssl = Sc("p6_ssl", [128, DB, 512], BF16)
                            xcf = Sc("p6_xcf", [128, 2, D], F32)
                            outst = Sc("p6_outst", [128, 2, 512], F32)
                            prev = self.emit_combine(mc, prev, d_sT, d_xown,
                                                     d_out, gT, ssl, xcf,
                                                     outst, psB, ident)
                    if BUILD_UPTO in ("mm1", "mm2", "cb0"):
                        break

                def p_final(gp):
                    gp.wait_ge(self.sem["g"], self.cnt["g"])
                blk.gpsimd(p_final)

        return self.nc

    def emit_p1(self, d_xkv, d_xown, d_wkT, d_wvT, d_ident, d_mask,
                d_wqb, d_wob, d_wkb, d_wvb,
                ident, mask, wqb, wqbs, wob, wkb, wvb, wkT, wvT, kT,
                vtok, s1w, s2w, mnw, vrw, rsw,
                xnownT, xf, xnb, xnkv, vT, psA, psB):
        p = self
        blk = self.blk

        def sync_f(sync):
            for dst, src in ((ident, d_ident), (mask, d_mask), (wqb, d_wqb),
                             (wob, d_wob), (wkb, d_wkb), (wvb, d_wvb)):
                p.mark(sync.dma_start(out=dst[:], in_=src), "q",
                       "ld_wqb" if dst is wqb else None, 16)
            p.mark(sync.dma_start(
                out=xf[:, 0, :].rearrange("p (b f) -> p b f", b=DB),
                in_=d_wkT.rearrange("(b p) f -> p b f", p=128)),
                "q", "ld_wk", 16)
            p.mark(sync.dma_start(
                out=xf[:, 1, :].rearrange("p (b f) -> p b f", b=DB),
                in_=d_wvT.rearrange("(b p) f -> p b f", p=128)),
                "q", "ld_wv", 16)
            for t in range(NT):
                if t == 0:
                    p.wait(sync, "cv_wk")
                elif t == 1:
                    p.wait(sync, "cv_wv")
                else:
                    if t >= 3:
                        p.wait(sync, "stt_%d" % (t - 3))
                        p.wait(sync, "sq_%d" % (t - 3))
                src = (d_xkv[t * 128:(t + 1) * 128, :] if t < 16
                       else d_xown[(t - 16) * 128:(t - 15) * 128, :])
                p.mark(sync.dma_start(out=xf[:, t % 3, :], in_=src),
                       "q", "ld_x_%d" % t, 16)

        def scalar_f(scalar):
            for t in range(NT):
                p.wait(scalar, "ld_x_%d" % t)
                if t >= 2:
                    p.wait(scalar, "xtr_%d" % (t - 2))
                p.mark(scalar.activation(out=xnb[:, t % 2, :], in_=xf[:, t % 3, :],
                                         func=AF.Square, bias=0.0, scale=1.0,
                                         accum_out=s2w[:, t:t + 1]),
                       "s", "sq_%d" % t)
                if t % 2 == 1:
                    g = t // 2
                    c0 = g * 2
                    p.wait(scalar, "var_%d" % g)
                    p.mark(scalar.activation(out=rsw[:, c0:c0 + 2],
                                             in_=vrw[:, c0:c0 + 2],
                                             func=AF.Sqrt, bias=0.0, scale=1.0),
                           "s", "sqrt_%d" % g)
                if t < 16 and t % 4 == 3:
                    mc4 = t // 4
                    p.wait(scalar, "kmm_%d" % mc4)
                    p.mark(scalar.activation(out=kT[:, mc4 * 512:(mc4 + 1) * 512],
                                             in_=psA[:, 0:512], func=AF.Identity,
                                             bias=wkb[:], scale=1.0),
                           "s", "kev_%d" % mc4)
                    p.wait(scalar, "vmm_%d" % mc4)
                    p.mark(scalar.activation(out=vT[:, mc4 * 512:(mc4 + 1) * 512],
                                             in_=psA[:, 512:1024], func=AF.Identity,
                                             bias=wvb[:], scale=1.0),
                           "s", "vev_%d" % mc4)

        def vector_f(vector):
            vector.memset(self.junk[:], 0.0)
            for buf in (s1w, s2w, mnw, vrw, rsw):
                vector.memset(buf[:], 0.0)
            p.wait(vector, "ld_wqb")
            vector.tensor_scalar(out=wqbs[:], in0=wqb[:], scalar1=C_ATT,
                                 scalar2=None, op0=ALU.mult)
            p.wait(vector, "ld_wk")
            p.mark(vector.tensor_copy(out=wkT[:], in_=xf[:, 0, :]), "v", "cv_wk")
            p.wait(vector, "ld_wv")
            p.mark(vector.tensor_copy(out=wvT[:], in_=xf[:, 1, :]), "v", "cv_wv")
            for t in range(NT):
                p.wait(vector, "ld_x_%d" % t)
                p.mark(vector.reduce_sum(out=s1w[:, t:t + 1], in_=xf[:, t % 3, :],
                                         axis=AX.X), "v", "rsum_%d" % t)
                if t % 2 != 1:
                    continue
                g = t // 2
                c0 = g * 2
                p.wait(vector, "sq_%d" % t)
                self.vsp(vector)
                vector.tensor_scalar(out=mnw[:, c0:c0 + 2], in0=s1w[:, c0:c0 + 2],
                                     scalar1=1.0 / D, scalar2=None, op0=ALU.mult)
                vector.tensor_scalar(out=vrw[:, c0:c0 + 2], in0=s2w[:, c0:c0 + 2],
                                     scalar1=1.0 / D, scalar2=None, op0=ALU.mult)
                vector.tensor_mul(out=s2w[:, c0:c0 + 2], in0=mnw[:, c0:c0 + 2],
                                  in1=mnw[:, c0:c0 + 2])
                self.vsp(vector)
                p.mark(vector.scalar_tensor_tensor(
                    out=vrw[:, c0:c0 + 2], in0=vrw[:, c0:c0 + 2], scalar=1e-5,
                    in1=s2w[:, c0:c0 + 2], op0=ALU.add, op1=ALU.subtract),
                    "v", "var_%d" % g)
                p.wait(vector, "sqrt_%d" % g)
                vector.reciprocal(out=rsw[:, c0:c0 + 2], in_=rsw[:, c0:c0 + 2])
                self.vsp(vector)
                for tt in (t - 1, t):
                    p.mark(vector.scalar_tensor_tensor(
                        out=xnb[:, tt % 2, :], in0=xf[:, tt % 3, :],
                        scalar=mnw[:, tt:tt + 1],
                        in1=rsw[:, tt:tt + 1].broadcast_to([128, D]),
                        op0=ALU.subtract, op1=ALU.mult), "v", "stt_%d" % tt)
                    for r in range(4):
                        p.wait(vector, "xtrr_%d_%d" % (tt, r))
                        if tt < 16:
                            if tt >= 4:
                                p.wait(vector, "vmm_%d" % (tt // 4 - 1))
                            dst = xnkv[:, r * 8:(r + 1) * 8,
                                       (tt % 4) * 128:(tt % 4) * 128 + 128]
                        else:
                            dst = xnownT[:, r * 8:(r + 1) * 8,
                                         (tt - 16) * 128:(tt - 15) * 128]
                        p.mark(vector.tensor_copy(
                            out=dst,
                            in_=psB[:, (r % 2) * 1024:(r % 2) * 1024 + 1024]
                            .rearrange("p (b f) -> p b f", b=8)),
                            "v", "xev_%d_%d" % (tt, r))
            for r in range(2):
                p.wait(vector, "vtrr_%d" % r)
                p.mark(vector.tensor_copy(
                    out=vtok[:, r * 1024:(r + 1) * 1024],
                    in_=psB[:, (r % 2) * 1024:(r % 2) * 1024 + 1024]),
                    "v", "vtev_%d" % r)

        def tensor_f(tensor):
            for t in range(NT):
                p.wait(tensor, "stt_%d" % t)
                for r in range(4):
                    rid = t * 4 + r
                    if rid >= 2:
                        p.wait(tensor, "xev_%d_%d" % ((rid - 2) // 4, (rid - 2) % 4))
                    last = None
                    for qq in range(8):
                        q = r * 8 + qq
                        last = tensor.transpose(
                            psB[:, (r % 2) * 1024 + qq * 128:
                                (r % 2) * 1024 + (qq + 1) * 128],
                            xnb[:, t % 2, q * 128:(q + 1) * 128], ident[:])
                    p.mark(last, "t", "xtrr_%d_%d" % (t, r))
                self.evhere("xtr_%d" % t, "t")
                if t < 16 and t % 4 == 3:
                    mc4 = t // 4
                    p.wait(tensor, "cv_wk")
                    p.wait(tensor, "cv_wv")
                    p.wait(tensor, "xev_%d_3" % t)
                    if mc4 >= 1:
                        p.wait(tensor, "kev_%d" % (mc4 - 1))
                        p.wait(tensor, "vev_%d" % (mc4 - 1))
                    last = None
                    for dd in range(DB):
                        last = tensor.matmul(psA[:, 0:512],
                                             wkT[:, dd * 128:(dd + 1) * 128],
                                             xnkv[:, dd, :],
                                             start=(dd == 0), stop=(dd == DB - 1))
                    p.mark(last, "t", "kmm_%d" % mc4)
                    last = None
                    for dd in range(DB):
                        last = tensor.matmul(psA[:, 512:1024],
                                             wvT[:, dd * 128:(dd + 1) * 128],
                                             xnkv[:, dd, :],
                                             start=(dd == 0), stop=(dd == DB - 1))
                    p.mark(last, "t", "vmm_%d" % mc4)
            for r in range(2):
                p.wait(tensor, "vev_3")
                p.wait(tensor, "xev_23_3")
                if r == 1:
                    p.wait(tensor, "vtev_0")
                last = None
                for b8 in range(8):
                    kb = r * 8 + b8
                    last = tensor.transpose(
                        psB[:, (r % 2) * 1024 + b8 * 128:(r % 2) * 1024 + (b8 + 1) * 128],
                        vT[:, kb * 128:(kb + 1) * 128], ident[:])
                p.mark(last, "t", "vtrr_%d" % r)

        blk.sync(sync_f)
        blk.scalar(scalar_f)
        blk.vector(vector_f)
        blk.tensor(tensor_f)

    def emit_p2(self, FIN, d_wqT, d_xnT, ident, mask, wqbs, kT, vtok, xnownT,
                attnT, ssum, srec, nmaxb, wq_f, wq_b, QT, probs,
                probsT, psA, psB, psC):
        p = self
        blk = self.blk
        taps = self.taps

        def nrounds(i):
            return cdv(2 * i + 2, 8)

        def gp_f(gp):
            p.gate(gp, FIN)
            p.mark(gp.dma_start(out=d_xnT,
                                in_=xnownT[:].rearrange("p b m -> p (b m)")),
                   "g", "gxn", 16)

        def sync_f(sync):
            p.gate(sync, FIN)
            for h in range(H):
                if h >= 1:
                    p.wait(sync, "cvq_%d" % (h - 1))
                p.mark(sync.dma_start(
                    out=wq_f[:].rearrange("p (b f) -> p b f", b=DB),
                    in_=d_wqT[:, h * 128:(h + 1) * 128]
                    .rearrange("(b p) f -> p b f", p=128)),
                    "q", "ldq_%d" % h, 16)

        def vector_f(vector):
            p.gate(vector, FIN)
            for h in range(H):
                p.wait(vector, "ldq_%d" % h)
                if h >= 2:
                    p.wait(vector, "qmm_%d_1" % (h - 2))
                p.mark(vector.tensor_copy(out=wq_b[:, h % 2, :], in_=wq_f[:]),
                       "v", "cvq_%d" % h)
                for i in range(NSLOT):
                    ii = (h * NSLOT + i) % 16
                    Ki = 256 * (i + 1)
                    p.wait(vector, "smm_%d_%d" % (h, i))
                    if i >= 1:
                        p.wait(vector, "exp_%d_%d" % (h, i - 1))
                        vector.reciprocal(out=srec[:, ii - 1:ii],
                                          in_=ssum[:, ii - 1:ii])
                    else:
                        self.vsp(vector)
                    p.mark(vector.reduce_max(out=nmaxb[:, ii:ii + 1],
                                             in_=psA[:, 0:Ki],
                                             axis=AX.X, negate=True),
                           "v", "rmax_%d_%d" % (h, i))
                    if i >= 1:
                        Kp = 256 * i
                        p.mark(vector.tensor_scalar(
                            out=probs[:, (i - 1) % 2, 0:Kp],
                            in0=probs[:, (i - 1) % 2, 0:Kp],
                            scalar1=srec[:, ii - 1:ii], scalar2=None,
                            op0=ALU.mult), "v", "vnorm_%d_%d" % (h, i - 1))
                    else:
                        self.vsp(vector)
                    if i >= 1:
                        if i >= 3:
                            p.wait(vector, "avmm_%d_%d" % (h, i - 3))
                        elif h >= 1:
                            p.wait(vector, "avmm_%d_%d" % (h - 1, 5 + i))
                        for r in range(nrounds(i - 1)):
                            nb_r = min(8, 2 * i - r * 8)
                            p.wait(vector, "ttr_%d_%d_%d" % (h, i - 1, r))
                            p.mark(vector.tensor_copy(
                                out=probsT[:, (i - 1) % 2,
                                           r * 1024:r * 1024 + nb_r * 128],
                                in_=psB[:, (r % 2) * 1024:
                                        (r % 2) * 1024 + nb_r * 128]),
                                "v", "pev_%d_%d_%d" % (h, i - 1, r))
                ii7 = (h * NSLOT + 7) % 16
                p.wait(vector, "exp_%d_7" % h)
                vector.reciprocal(out=srec[:, ii7:ii7 + 1], in_=ssum[:, ii7:ii7 + 1])
                self.vsp(vector)
                p.mark(vector.tensor_scalar(
                    out=probs[:, 1, 0:2048], in0=probs[:, 1, 0:2048],
                    scalar1=srec[:, ii7:ii7 + 1], scalar2=None, op0=ALU.mult),
                    "v", "vnorm_%d_7" % h)
                p.wait(vector, "avmm_%d_5" % h)
                for r in range(2):
                    p.wait(vector, "ttr_%d_7_%d" % (h, r))
                    p.mark(vector.tensor_copy(
                        out=probsT[:, 1, r * 1024:(r + 1) * 1024],
                        in_=psB[:, (r % 2) * 1024:(r % 2) * 1024 + 1024]),
                        "v", "pev_%d_7_%d" % (h, r))

        def scalar_f(scalar):
            p.gate(scalar, FIN)
            for h in range(H):
                for mc in range(2):
                    p.wait(scalar, "qmm_%d_%d" % (h, mc))
                    if h >= 2:
                        p.wait(scalar, "smm_%d_7" % (h - 2))
                    p.mark(scalar.activation(
                        out=QT[:, h % 2, mc * 512:(mc + 1) * 512],
                        in_=psC[:, 512:1024], func=AF.Identity,
                        bias=wqbs[:, h:h + 1], scale=C_ATT),
                        "s", "qev_%d_%d" % (h, mc))
                for i in range(NSLOT):
                    ii = (h * NSLOT + i) % 16
                    Ki = 256 * (i + 1)
                    p.wait(scalar, "rmax_%d_%d" % (h, i))
                    if i >= 2:
                        p.wait(scalar, "ttr_last_%d_%d" % (h, i - 2))
                    elif h >= 1:
                        p.wait(scalar, "ttr_last_%d_%d" % (h - 1, 6 + i))
                    if h >= 2:
                        p.wait(scalar, "vnorm_%d_%d" % (h - 2, i))
                    p.mark(scalar.activation(
                        out=probs[:, i % 2, 0:Ki], in_=psA[:, 0:Ki],
                        func=AF.Exp, bias=nmaxb[:, ii:ii + 1], scale=1.0,
                        accum_out=ssum[:, ii:ii + 1]),
                        "s", "exp_%d_%d" % (h, i))
                    if i >= 1:
                        p.wait(scalar, "avmm_%d_%d" % (h, i - 1))
                        p.mark(scalar.activation(
                            out=attnT[:, h * M + (i - 1) * 128: h * M + i * 128],
                            in_=psC[:, ((i - 1) % 2) * 128:((i - 1) % 2) * 128 + 128],
                            func=AF.Identity, bias=0.0, scale=1.0),
                            "s", "avev_%d_%d" % (h, i - 1))
                p.wait(scalar, "avmm_%d_7" % h)
                p.mark(scalar.activation(
                    out=attnT[:, h * M + 7 * 128: h * M + 8 * 128],
                    in_=psC[:, 128:256], func=AF.Identity, bias=0.0, scale=1.0),
                    "s", "avev_%d_7" % h)

        def tensor_f(tensor):
            p.gate(tensor, FIN)

            def ptr_attnv(h, i):
                nblk = 2 * i + 2
                p.wait(tensor, "vnorm_%d_%d" % (h, i))
                for r in range(nrounds(i)):
                    nb_r = min(8, nblk - r * 8)
                    rid = self.p2_rounds
                    self.p2_rounds += 1
                    prev = self.p2_round_ev.get(rid - 2)
                    if prev is not None:
                        p.wait(tensor, prev)
                    self.p2_round_ev[rid] = "pev_%d_%d_%d" % (h, i, r)
                    last = None
                    for u in range(nb_r):
                        tb = r * 8 + u
                        last = tensor.transpose(
                            psB[:, (r % 2) * 1024 + u * 128:
                                (r % 2) * 1024 + (u + 1) * 128],
                            probs[:, i % 2, tb * 128:(tb + 1) * 128], ident[:])
                    p.mark(last, "t", "ttr_%d_%d_%d" % (h, i, r))
                self.evhere("ttr_last_%d_%d" % (h, i), "t")
                if i >= 2:
                    p.wait(tensor, "avev_%d_%d" % (h, i - 2))
                elif h >= 1:
                    p.wait(tensor, "avev_%d_%d" % (h - 1, 6 + i))
                last = None
                for tb in range(nblk):
                    if tb % 8 == 0:
                        p.wait(tensor, "pev_%d_%d_%d" % (h, i, tb // 8))
                    last = tensor.matmul(
                        psC[:, (i % 2) * 128:(i % 2) * 128 + 128],
                        vtok[:, tb * 128:(tb + 1) * 128],
                        probsT[:, i % 2, tb * 128:(tb + 1) * 128],
                        start=(tb == 0), stop=(tb == nblk - 1))
                p.mark(last, "t", "avmm_%d_%d" % (h, i))

            for h in range(H):
                p.wait(tensor, "cvq_%d" % h)
                for mc in range(2):
                    if h == 0 and mc == 0:
                        pass
                    elif mc == 1:
                        p.wait(tensor, "qev_%d_0" % h)
                    else:
                        p.wait(tensor, "qev_%d_1" % (h - 1))
                    last = None
                    for dd in range(DB):
                        last = tensor.matmul(
                            psC[:, 512:1024],
                            wq_b[:, h % 2, dd * 128:(dd + 1) * 128],
                            xnownT[:, dd, mc * 512:(mc + 1) * 512],
                            start=(dd == 0), stop=(dd == DB - 1))
                    p.mark(last, "t", "qmm_%d_%d" % (h, mc))
                for i in range(NSLOT):
                    Ki = 256 * (i + 1)
                    p.wait(tensor, "qev_%d_1" % h)
                    if i >= 1:
                        p.wait(tensor, "exp_%d_%d" % (h, i - 1))
                        p.wait(tensor, "rmax_%d_%d" % (h, i - 1))
                    elif h >= 1:
                        p.wait(tensor, "exp_%d_7" % (h - 1))
                        p.wait(tensor, "rmax_%d_7" % (h - 1))
                    nch = cdv(Ki, 512)
                    for c in range(nch):
                        nw = min(512, Ki - c * 512)
                        tensor.matmul(
                            psA[:, c * 512:c * 512 + nw],
                            QT[:, h % 2, i * 128:(i + 1) * 128],
                            kT[:, c * 512:c * 512 + nw], start=True,
                            stop=(c != nch - 1))
                    reg0 = (nch - 1) * 512
                    mrhs = mask[:, 0:512] if nw == 512 else mask[:, 256:512]
                    p.mark(tensor.matmul(psA[:, reg0:reg0 + nw], ident[:],
                                         mrhs, start=False, stop=True),
                           "t", "smm_%d_%d" % (h, i))
                    if i >= 1:
                        ptr_attnv(h, i - 1)
                ptr_attnv(h, 7)

        self.p2_rounds = 0
        self.p2_round_ev = {}
        blk.gpsimd(gp_f)
        blk.sync(sync_f)
        blk.vector(vector_f)
        blk.scalar(scalar_f)
        blk.tensor(tensor_f)

    def emit_p3(self, FIN, d_woT, d_sT, wob, attnT, wo_f, wo_b, sst, psA):
        p = self
        blk = self.blk
        taps = self.taps

        def sync_f(sync):
            p.gate(sync, FIN)
            for j in range(DB):
                if j >= 1:
                    p.wait(sync, "cvo_%d" % (j - 1))
                p.mark(sync.dma_start(
                    out=wo_f[:].rearrange("p (b f) -> p b f", b=DB)[:, 0:16, :],
                    in_=d_woT[0:2048, j * 128:(j + 1) * 128]
                    .rearrange("(b p) f -> p b f", p=128)),
                    "q", "ldo_%d" % j, 16)

        def vector_f(vector):
            p.gate(vector, FIN)
            for j in range(DB):
                p.wait(vector, "ldo_%d" % j)
                p.wait(vector, "ldo2_%d" % j)
                if j >= 2:
                    p.wait(vector, "womm_%d_1" % (j - 2))
                p.mark(vector.tensor_copy(out=wo_b[:, j % 2, :], in_=wo_f[:]),
                       "v", "cvo_%d" % j)

        def tensor_f(tensor):
            p.gate(tensor, FIN)
            for j in range(DB):
                p.wait(tensor, "cvo_%d" % j)
                for mc in range(2):
                    if j >= 2:
                        p.wait(tensor, "sev_%d_%d" % (j - 2, mc))
                    reg = (j % 2) * 1024 + mc * 512
                    last = None
                    for a in range(H):
                        last = tensor.matmul(
                            psA[:, reg:reg + 512],
                            wo_b[:, j % 2, a * 128:(a + 1) * 128],
                            attnT[:, a * M + mc * 512: a * M + (mc + 1) * 512],
                            start=(a == 0), stop=(a == H - 1))
                    p.mark(last, "t", "womm_%d_%d" % (j, mc))

        def scalar_f(scalar):
            p.gate(scalar, FIN)
            for j in range(DB):
                for mc in range(2):
                    p.wait(scalar, "womm_%d_%d" % (j, mc))
                    if j >= 2:
                        p.wait(scalar, "gsT_%d" % (j - 2))
                    reg = (j % 2) * 1024 + mc * 512
                    p.mark(scalar.activation(
                        out=sst[:, j % 2, mc * 512:(mc + 1) * 512],
                        in_=psA[:, reg:reg + 512], func=AF.Identity,
                        bias=wob[:, j:j + 1], scale=1.0),
                        "s", "sev_%d_%d" % (j, mc))

        def gp_f(gp):
            p.gate(gp, FIN)
            for j in range(DB):
                if j >= 1:
                    p.wait(gp, "cvo_%d" % (j - 1))
                p.mark(gp.dma_start(
                    out=wo_f[:].rearrange("p (b f) -> p b f", b=DB)[:, 16:32, :],
                    in_=d_woT[2048:4096, j * 128:(j + 1) * 128]
                    .rearrange("(b p) f -> p b f", p=128)),
                    "g", "ldo2_%d" % j, 16)
                if j >= 1:
                    p.wait(gp, "sev_%d_0" % (j - 1))
                    p.wait(gp, "sev_%d_1" % (j - 1))
                    p.mark(gp.dma_start(out=d_sT[:, (j - 1) * M:j * M],
                                        in_=sst[:, (j - 1) % 2, :]),
                           "g", "gsT_%d" % (j - 1), 16)
            p.wait(gp, "sev_31_0")
            p.wait(gp, "sev_31_1")
            p.mark(gp.dma_start(out=d_sT[:, 31 * M:32 * M],
                                in_=sst[:, 1, :]),
                   "g", "gsT_31", 16)
            if DEBUG_TAPS:
                p.wait(gp, "avev_31_7")
                p.mark(gp.dma_start(out=taps["attnT"], in_=attnT[:]), "g", None, 16)
                p.wait(gp, "gsT_31")
                p.mark(gp.dma_start(out=taps["sT"], in_=d_sT), "g", None, 16)

        blk.sync(sync_f)
        blk.vector(vector_f)
        blk.tensor(tensor_f)
        blk.scalar(scalar_f)
        blk.gpsimd(gp_f)

    def emit_mm1(self, mc, FIN, d_w1, d_w3, d_s1, d_s3, d_w1t, d_w3t,
                 d_xnT, hT, xnsl, pkr, sci1, sci3, nat1, nat3, w1t, w3t,
                 sc1, sc3, asil, bsb, psA, psB, ident):
        p = self
        blk = self.blk
        P = "m1%d_" % mc

        def sync_f(sync):
            p.gate(sync, FIN)
            if mc == 0:
                p.wait(sync, "gxn")
            else:
                p.wait(sync, "m10_mm_85")
            p.mark(sync.dma_start(
                out=xnsl[:],
                in_=d_xnT.rearrange("p (b m) -> p b m", b=DB)[:, :, mc * 512:(mc + 1) * 512]),
                "q", P + "ldxn", 16)
            for b in range(HB):
                if mc == 0:
                    if b >= 1:
                        p.wait(sync, P + "dqv1_%d" % (b - 1))
                    p.mark(sync.dma_start(
                        out=pkr[:, 0, :],
                        in_=d_w1[b * 4096:(b + 1) * 4096, :]
                        .rearrange("(p r) k -> p (r k)", p=128)),
                        "q", P + "ldw1_%d" % b, 16)
                    if b >= 2:
                        p.wait(sync, P + "dqh1_%d_1" % (b - 2))
                    p.mark(sync.dma_start(
                        out=sc1[:, b % 2, :],
                        in_=d_s1[b * 8192:(b + 1) * 8192]
                        .rearrange("(p g) -> p g", p=128)),
                        "q", P + "ldsc1_%d" % b, 16)
                else:
                    if b >= 2:
                        p.wait(sync, P + "mm_%d" % (b - 2))
                    p.mark(sync.dma_start(out=w1t[:, b % 2, :], in_=d_w1t[b]),
                           "q", P + "ldt1_%d" % b, 16)

        def gpl_f(gp):
            p.gate(gp, FIN)

            def gws(b):
                for which in (1, 3):
                    wdst = w1t if which == 1 else w3t
                    wdram = d_w1t if which == 1 else d_w3t
                    p.wait(gp, P + "wev%d_%d_3" % (which, b))
                    p.mark(gp.dma_start(out=wdram[b], in_=wdst[:, b % 2, :]),
                           "g", P + "gw%d_%d" % (which, b), 16)

            def htt(bb):
                p.wait(gp, P + "abev_%d" % bb)
                p.mark(gp.tensor_mul(
                    out=hT[:, bb, :], in0=asil[:, bb % 2, :],
                    in1=bsb[:, bb % 2, :]),
                    "g", P + "hTT_%d" % bb, 1)

            for b in range(HB):
                if mc == 0:
                    if b >= 1:
                        p.wait(gp, P + "dqv3_%d" % (b - 1))
                    p.mark(gp.dma_start(
                        out=pkr[:, 1, :],
                        in_=d_w3[b * 4096:(b + 1) * 4096, :]
                        .rearrange("(p r) k -> p (r k)", p=128)),
                        "g", P + "ldw3_%d" % b, 16)
                    if b >= 2:
                        p.wait(gp, P + "dqh3_%d_1" % (b - 2))
                    p.mark(gp.dma_start(
                        out=sc3[:, b % 2, :],
                        in_=d_s3[b * 8192:(b + 1) * 8192]
                        .rearrange("(p g) -> p g", p=128)),
                        "g", P + "ldsc3_%d" % b, 16)
                    if b >= 2:
                        gws(b - 2)
                        htt(b - 2)
                else:
                    if b == 0:
                        p.wait(gp, P + "ldxn")
                    if b >= 2:
                        p.wait(gp, P + "mm_%d" % (b - 2))
                    p.mark(gp.dma_start(out=w3t[:, b % 2, :], in_=d_w3t[b]),
                           "g", P + "ldt3_%d" % b, 16)
                    if b >= 2:
                        htt(b - 2)
            if mc == 0:
                gws(HB - 2)
                gws(HB - 1)
            htt(HB - 2)
            htt(HB - 1)

        def vec_f(vector):
            p.gate(vector, FIN)
            for b in range(HB):
                if mc == 0:
                    p.wait(vector, P + "ldw1_%d" % b)
                    p.wait(vector, P + "ldsc1_%d" % b)
                    p.wait(vector, P + "ldw3_%d" % b)
                    p.wait(vector, P + "ldsc3_%d" % b)
                    pk1v = pkr[:, 0, :].rearrange("p (q j) -> p q j", q=32)
                    pk3v = pkr[:, 1, :].rearrange("p (q j) -> p q j", q=32)
                    m1 = nat1[:].rearrange("p (q n j) -> p q n j", q=32, n=2)
                    m3 = nat3[:].rearrange("p (q n j) -> p q n j", q=32, n=2)
                    s1v = sc1[:, b % 2, :].rearrange("p (q n) -> p q n", n=2)
                    s3v = sc3[:, b % 2, :].rearrange("p (q n) -> p q n", n=2)
                    sc1v = sci1[:].rearrange("p (q j) -> p q j", q=16)
                    sc3v = sci3[:].rearrange("p (q j) -> p q j", q=16)
                    for h in (0, 1):
                        q0, q1 = 16 * h, 16 * h + 16
                        if b >= 1:
                            p.wait(vector, P + "tr1_%d_%d" % (b - 1, h))
                            p.wait(vector, P + "tr3_%d_%d" % (b - 1, h))
                        vector.tensor_scalar(
                            out=sc1v, in0=pk1v[:, q0:q1, :], scalar1=4,
                            scalar2=None, op0=ALU.arith_shift_right)
                        vector.tensor_scalar(
                            out=sc3v, in0=pk3v[:, q0:q1, :], scalar1=4,
                            scalar2=None, op0=ALU.arith_shift_right)
                        vector.tensor_mul(
                            out=m1[:, q0:q1, 0, :], in0=sc1v,
                            in1=s1v[:, q0:q1, 0:1].broadcast_to([128, 16, 64]))
                        vector.tensor_mul(
                            out=m3[:, q0:q1, 0, :], in0=sc3v,
                            in1=s3v[:, q0:q1, 0:1].broadcast_to([128, 16, 64]))
                        ax1 = vector.tensor_scalar(
                            out=sc1v, in0=pk1v[:, q0:q1, :], scalar1=15,
                            scalar2=8, op0=ALU.bitwise_and, op1=ALU.bitwise_xor)
                        if h == 1:
                            p.mark(ax1, "v", P + "dqv1_%d" % b)
                        ax3 = vector.tensor_scalar(
                            out=sc3v, in0=pk3v[:, q0:q1, :], scalar1=15,
                            scalar2=8, op0=ALU.bitwise_and, op1=ALU.bitwise_xor)
                        if h == 1:
                            p.mark(ax3, "v", P + "dqv3_%d" % b)
                        p.mark(vector.scalar_tensor_tensor(
                            out=m1[:, q0:q1, 1, :], in0=sc1v, scalar=-8.0,
                            in1=s1v[:, q0:q1, 1:2].broadcast_to([128, 16, 64]),
                            op0=ALU.add, op1=ALU.mult),
                            "v", P + "dqh1_%d_%d" % (b, h))
                        p.mark(vector.scalar_tensor_tensor(
                            out=m3[:, q0:q1, 1, :], in0=sc3v, scalar=-8.0,
                            in1=s3v[:, q0:q1, 1:2].broadcast_to([128, 16, 64]),
                            op0=ALU.add, op1=ALU.mult),
                            "v", P + "dqh3_%d_%d" % (b, h))


        def ten_f(tensor):
            p.gate(tensor, FIN)
            p.wait(tensor, P + "ldxn")
            rid = [0]
            rev = {}

            def tr_half(b, which, h):
                nat = nat1 if which == 1 else nat3
                p.wait(tensor, P + "dqh%d_%d_%d" % (which, b, h))
                for r in (2 * h, 2 * h + 1):
                    k = rid[0]
                    rid[0] += 1
                    if k >= 2:
                        p.wait(tensor, rev[k - 2])
                    rev[k] = P + "wev%d_%d_%d" % (which, b, r)
                    last = None
                    for u in range(8):
                        q = r * 8 + u
                        last = tensor.transpose(
                            psB[:, (k % 2) * 1024 + u * 128:(k % 2) * 1024 + (u + 1) * 128],
                            nat[:, q * 128:(q + 1) * 128], ident[:])
                    p.mark(last, "t", P + "wtr%d_%d_%d" % (which, b, r))
                self.evhere(P + "tr%d_%d_%d" % (which, b, h), "t")

            def tr_mat_pair(b):
                for h in (0, 1):
                    tr_half(b, 1, h)
                    tr_half(b, 3, h)

            def mms(b):
                if mc == 0:
                    p.wait(tensor, P + "wev1_%d_3" % b)
                    p.wait(tensor, P + "wev3_%d_3" % b)
                else:
                    p.wait(tensor, P + "ldt1_%d" % b)
                    p.wait(tensor, P + "ldt3_%d" % b)
                if b >= 2:
                    p.wait(tensor, P + "abev_%d" % (b - 2))
                reg_a = (b % 2) * 1024
                reg_b = (b % 2) * 1024 + 512
                last = None
                for dd in range(DB):
                    last = tensor.matmul(psA[:, reg_a:reg_a + 512],
                                         w1t[:, b % 2, dd * 128:(dd + 1) * 128],
                                         xnsl[:, dd, :],
                                         start=(dd == 0), stop=(dd == DB - 1))
                for dd in range(DB):
                    last = tensor.matmul(psA[:, reg_b:reg_b + 512],
                                         w3t[:, b % 2, dd * 128:(dd + 1) * 128],
                                         xnsl[:, dd, :],
                                         start=(dd == 0), stop=(dd == DB - 1))
                p.mark(last, "t", P + "mm_%d" % b)

            if mc == 0:
                for b in range(HB):
                    tr_mat_pair(b)
                    if b >= 1:
                        mms(b - 1)
                mms(85)
            else:
                for b in range(HB):
                    mms(b)

        def sca_f(scalar):
            p.gate(scalar, FIN)
            for b in range(HB):
                if mc == 0:
                    kq = b * 8
                    for h in (0, 1):
                        for which in (1, 3):
                            wdst = w1t if which == 1 else w3t
                            for r in (2 * h, 2 * h + 1):
                                p.wait(scalar, P + "wtr%d_%d_%d" % (which, b, r))
                                if b >= 2 and h == 0 and r == 0:
                                    p.wait(scalar, P + "mm_%d" % (b - 2))
                                    p.wait(scalar, P + "gw%d_%d" % (which, b - 2))
                                p.mark(scalar.activation(
                                    out=wdst[:, b % 2, r * 1024:(r + 1) * 1024],
                                    in_=psB[:, (kq % 2) * 1024:(kq % 2) * 1024 + 1024],
                                    func=AF.Identity, bias=0.0, scale=1.0),
                                    "s", P + "wev%d_%d_%d" % (which, b, r))
                                kq += 1
                if b >= 1:
                    bb = b - 1
                    p.wait(scalar, P + "mm_%d" % bb)
                    if bb >= 2:
                        p.wait(scalar, P + "hTT_%d" % (bb - 2))
                    reg_a = (bb % 2) * 1024
                    reg_b = (bb % 2) * 1024 + 512
                    scalar.activation(out=asil[:, bb % 2, :],
                                      in_=psA[:, reg_a:reg_a + 512],
                                      func=AF.Silu, bias=0.0, scale=1.0)
                    p.mark(scalar.activation(out=bsb[:, bb % 2, :],
                                             in_=psA[:, reg_b:reg_b + 512],
                                             func=AF.Identity, bias=0.0, scale=1.0),
                           "s", P + "abev_%d" % bb)
            p.wait(scalar, P + "mm_85")
            p.wait(scalar, P + "hTT_83")
            scalar.activation(out=asil[:, 1, :], in_=psA[:, 1024:1536],
                              func=AF.Silu, bias=0.0, scale=1.0)
            p.mark(scalar.activation(out=bsb[:, 1, :], in_=psA[:, 1536:2048],
                                     func=AF.Identity, bias=0.0, scale=1.0),
                   "s", P + "abev_85")

        blk.sync(sync_f)
        blk.gpsimd(gpl_f)
        blk.vector(vec_f)
        blk.tensor(ten_f)
        blk.scalar(sca_f)

        fin = [P + "mm_85", P + "hTT_85", P + "abev_85", P + "ldxn"]
        if mc == 0:
            fin += [P + "gw1_85", P + "gw3_85", P + "ldw3_85"]
        else:
            fin += [P + "ldt3_85"]
        return fin

    def emit_mm2(self, mc, FIN, d_w2, d_s2, d_w2t, hT, pk2, sci2, nat2, w2t,
                 sc2, gT, psA, psB, ident):
        p = self
        blk = self.blk
        taps = self.taps
        P = "m2%d_" % mc
        d_w2v = d_w2.rearrange("(a q) k -> a q k", q=HB)

        def sync_f(sync):
            p.gate(sync, FIN)
            for k in range(DB):
                if mc == 0:
                    if k >= 1:
                        p.wait(sync, P + "dqv_%d_0" % (k - 1))
                    p.mark(sync.dma_start(
                        out=pk2[:, 0, :].rearrange("p (q j) -> p q j", q=43),
                        in_=d_w2v[k * 128:(k + 1) * 128, 0:43, :]),
                        "q", P + "ldp_%d_0" % k, 16)
                    if k >= 2:
                        p.wait(sync, P + "dq_%d_1" % (k - 2))
                    p.mark(sync.dma_start(
                        out=sc2[:, k % 2, :],
                        in_=d_s2[k * 128 * 172:(k + 1) * 128 * 172]
                        .rearrange("(p g) -> p g", p=128)),
                        "q", P + "lds_%d" % k, 16)
                else:
                    if k >= 2:
                        p.wait(sync, P + "mm_%d" % (k - 2))
                    p.mark(sync.dma_start(out=w2t[:, k % 2, 0:5504],
                                          in_=d_w2t[k][:, 0:5504]),
                           "q", P + "ldt_%d_0" % k, 16)

        def gpl_f(gp):
            p.gate(gp, FIN)

            def gws(k):
                p.wait(gp, P + "wev_%d_1_5" % k)
                p.mark(gp.dma_start(out=d_w2t[k], in_=w2t[:, k % 2, :]),
                       "g", P + "gw_%d" % k, 16)

            for k in range(DB):
                if mc == 0:
                    if k >= 1:
                        p.wait(gp, P + "dqv_%d_1" % (k - 1))
                    p.mark(gp.dma_start(
                        out=pk2[:, 1, :].rearrange("p (q j) -> p q j", q=43),
                        in_=d_w2v[k * 128:(k + 1) * 128, 43:86, :]),
                        "g", P + "ldp_%d_1" % k, 16)
                    if k >= 2:
                        gws(k - 2)
                else:
                    if k == 0:
                        p.wait(gp, P + "ldt_0_0")
                    if k >= 2:
                        p.wait(gp, P + "mm_%d" % (k - 2))
                    p.mark(gp.dma_start(out=w2t[:, k % 2, 5504:11008],
                                        in_=d_w2t[k][:, 5504:11008]),
                           "g", P + "ldt_%d_1" % k, 16)
            if mc == 0:
                gws(DB - 2)
                gws(DB - 1)
            if DEBUG_TAPS and mc == 0:
                p.wait(gp, "m10_hTT_85")
                p.mark(gp.dma_start(out=taps["hT"],
                                    in_=hT[:].rearrange("p b m -> p (b m)")),
                       "g", None, 16)

        def vec_f(vector):
            p.gate(vector, FIN)
            if mc != 0:
                return
            for k in range(DB):
                for cc in range(2):
                    p.wait(vector, P + "ldp_%d_%d" % (k, cc))
                    p.wait(vector, P + "lds_%d" % k)
                    pkv = pk2[:, cc, :].rearrange("p (q j) -> p q j", q=43)
                    mo = nat2[:].rearrange("p (q n j) -> p q n j", q=43, n=2)
                    sv = sc2[:, k % 2, 86 * cc:86 * (cc + 1)].rearrange(
                        "p (q n) -> p q n", n=2)
                    sci0 = sci2[:, 0:1536].rearrange("p (q j) -> p q j", q=24)
                    sci1_ = sci2[:, 0:1216].rearrange("p (q j) -> p q j", q=19)
                    pk_, pc_ = (k, 0) if cc == 1 else (k - 1, 1)
                    if k > 0 or cc > 0:
                        p.wait(vector, P + "trh_%d_%d_%d" % (pk_, pc_, 0))
                    vector.tensor_scalar(
                        out=sci0, in0=pkv[:, 0:24, :], scalar1=4,
                        scalar2=None, op0=ALU.arith_shift_right)
                    self.vsp(vector, 1)
                    vector.tensor_mul(
                        out=mo[:, 0:24, 0, :], in0=sci0,
                        in1=sv[:, 0:24, 0:1].broadcast_to([128, 24, 64]))
                    vector.tensor_scalar(
                        out=sci0, in0=pkv[:, 0:24, :],
                        scalar1=15, scalar2=8,
                        op0=ALU.bitwise_and, op1=ALU.bitwise_xor)
                    self.vsp(vector, 1)
                    p.mark(vector.scalar_tensor_tensor(
                        out=mo[:, 0:24, 1, :], in0=sci0, scalar=-8.0,
                        in1=sv[:, 0:24, 1:2].broadcast_to([128, 24, 64]),
                        op0=ALU.add, op1=ALU.mult),
                        "v", P + "dqh_%d_%d_0" % (k, cc))
                    if k > 0 or cc > 0:
                        p.wait(vector, P + "trh_%d_%d_%d" % (pk_, pc_, 1))
                    vector.tensor_scalar(
                        out=sci1_, in0=pkv[:, 24:43, :], scalar1=4,
                        scalar2=None, op0=ALU.arith_shift_right)
                    self.vsp(vector, 1)
                    vector.tensor_mul(
                        out=mo[:, 24:43, 0, :], in0=sci1_,
                        in1=sv[:, 24:43, 0:1].broadcast_to([128, 19, 64]))
                    p.mark(vector.tensor_scalar(
                        out=sci1_, in0=pkv[:, 24:43, :],
                        scalar1=15, scalar2=8,
                        op0=ALU.bitwise_and, op1=ALU.bitwise_xor),
                        "v", P + "dqv_%d_%d" % (k, cc))
                    self.vsp(vector, 1)
                    p.mark(vector.scalar_tensor_tensor(
                        out=mo[:, 24:43, 1, :], in0=sci1_, scalar=-8.0,
                        in1=sv[:, 24:43, 1:2].broadcast_to([128, 19, 64]),
                        op0=ALU.add, op1=ALU.mult),
                        "v", P + "dqh_%d_%d_1" % (k, cc))
                    self.evhere(P + "dq_%d_%d" % (k, cc), "v")

        def ten_f(tensor):
            p.gate(tensor, FIN)
            p.wait(tensor, "m1%d_hTT_85" % mc)
            rid = [0]
            rev = {}

            def trs(k):
                for cc in range(2):
                    for h in (0, 1):
                        p.wait(tensor, P + "dqh_%d_%d_%d" % (k, cc, h))
                        for r in (range(0, 3) if h == 0 else range(3, 6)):
                            nb_r = min(8, 43 - r * 8)
                            kk = rid[0]
                            rid[0] += 1
                            if kk >= 2:
                                p.wait(tensor, rev[kk - 2])
                            rev[kk] = P + "wev_%d_%d_%d" % (k, cc, r)
                            last = None
                            for u in range(nb_r):
                                q = r * 8 + u
                                last = tensor.transpose(
                                    psB[:, (kk % 2) * 1024 + u * 128:
                                        (kk % 2) * 1024 + (u + 1) * 128],
                                    nat2[:, q * 128:(q + 1) * 128], ident[:])
                            p.mark(last, "t", P + "wtr_%d_%d_%d" % (k, cc, r))
                        self.evhere(P + "trh_%d_%d_%d" % (k, cc, h), "t")

            def mms(k):
                if mc == 0:
                    p.wait(tensor, P + "wev_%d_1_5" % k)
                else:
                    p.wait(tensor, P + "ldt_%d_0" % k)
                    p.wait(tensor, P + "ldt_%d_1" % k)
                if k >= 2:
                    p.wait(tensor, P + "gev_%d" % (k - 2))
                reg = (k % 2) * 512
                last = None
                for c in range(HB):
                    last = tensor.matmul(psA[:, reg:reg + 512],
                                         w2t[:, k % 2, c * 128:(c + 1) * 128],
                                         hT[:, c, :],
                                         start=(c == 0), stop=(c == HB - 1))
                p.mark(last, "t", P + "mm_%d" % k)

            if mc == 0:
                for k in range(DB):
                    trs(k)
                    if k >= 1:
                        mms(k - 1)
                mms(DB - 1)
            else:
                for k in range(DB):
                    mms(k)

        def sca_f(scalar):
            p.gate(scalar, FIN)

            def gev(k):
                p.wait(scalar, P + "mm_%d" % k)
                reg = (k % 2) * 512
                p.mark(scalar.activation(out=gT[:, k, :], in_=psA[:, reg:reg + 512],
                                         func=AF.Silu, bias=0.0, scale=1.0),
                       "s", P + "gev_%d" % k)

            for k in range(DB):
                if mc == 0:
                    for cc in range(2):
                        for r in range(6):
                            nb_r = min(8, 43 - r * 8)
                            p.wait(scalar, P + "wtr_%d_%d_%d" % (k, cc, r))
                            if k >= 2 and cc == 0 and r == 0:
                                p.wait(scalar, P + "mm_%d" % (k - 2))
                                p.wait(scalar, P + "gw_%d" % (k - 2))
                            kk = k * 12 + cc * 6 + r
                            p.mark(scalar.activation(
                                out=w2t[:, k % 2,
                                        (cc * 43 + r * 8) * 128:
                                        (cc * 43 + r * 8 + nb_r) * 128],
                                in_=psB[:, (kk % 2) * 1024:(kk % 2) * 1024 + nb_r * 128],
                                func=AF.Identity, bias=0.0, scale=1.0),
                                "s", P + "wev_%d_%d_%d" % (k, cc, r))
                    if k >= 1:
                        gev(k - 1)
                else:
                    gev(k)
            if mc == 0:
                gev(DB - 1)

        blk.sync(sync_f)
        blk.gpsimd(gpl_f)
        blk.vector(vec_f)
        blk.tensor(ten_f)
        blk.scalar(sca_f)

        fin = [P + "mm_31", P + "gev_31"]
        if mc == 0:
            fin += [P + "gw_31", P + "dq_31_1", P + "lds_31"]
        else:
            fin += [P + "ldt_31_0", P + "ldt_31_1"]
        return fin

    def emit_combine(self, mc, FIN, d_sT, d_xown, d_out, gT, ssl, xcf,
                     outst, psB, ident):
        p = self
        blk = self.blk
        taps = self.taps
        P = "cb%d_" % mc

        def sync_f(sync):
            p.gate(sync, FIN)
            p.mark(sync.dma_start(
                out=ssl[:],
                in_=d_sT.rearrange("p (b m) -> p b m", b=DB)[:, :, mc * 512:(mc + 1) * 512]),
                "q", P + "lds", 16)
            for t in range(4):
                if t >= 2:
                    p.wait(sync, P + "add_%d_7" % (t - 2))
                p.mark(sync.dma_start(
                    out=xcf[:, t % 2, :],
                    in_=d_xown[(mc * 4 + t) * 128:(mc * 4 + t + 1) * 128, :]),
                    "q", P + "ldx_%d" % t, 16)

        def vec_f(vector):
            p.gate(vector, FIN)
            p.wait(vector, P + "lds")
            for c in range(8):
                p.mark(vector.tensor_mul(out=gT[:, c * 4:(c + 1) * 4, :],
                                         in0=gT[:, c * 4:(c + 1) * 4, :],
                                         in1=ssl[:, c * 4:(c + 1) * 4, :]),
                       "v", P + "ptt_%d" % c)
            for t in range(4):
                p.wait(vector, P + "ldx_%d" % t)
                for rr in range(8):
                    p.wait(vector, P + "ctr_%d_%d" % (t, rr))
                    if rr >= 2:
                        p.wait(vector, P + "out_%d_%d" % (t, rr - 2))
                    elif t >= 1:
                        p.wait(vector, P + "out_%d_%d" % (t - 1, 6 + rr))
                    p.mark(vector.tensor_add(
                        out=outst[:, rr % 2, :],
                        in0=psB[:, (rr % 2) * 1024:(rr % 2) * 1024 + 512],
                        in1=xcf[:, t % 2, rr * 512:(rr + 1) * 512]),
                        "v", P + "add_%d_%d" % (t, rr))

        def ten_f(tensor):
            p.gate(tensor, FIN)
            for t in range(4):
                for rr in range(8):
                    p.wait(tensor, P + "ptt_%d" % rr)
                    rid = t * 8 + rr
                    if rid >= 2:
                        p.wait(tensor, P + "add_%d_%d" % ((rid - 2) // 8, (rid - 2) % 8))
                    last = None
                    for u in range(4):
                        k = rr * 4 + u
                        last = tensor.transpose(
                            psB[:, (rr % 2) * 1024 + u * 128:(rr % 2) * 1024 + (u + 1) * 128],
                            gT[:, k, t * 128:(t + 1) * 128], ident[:])
                    p.mark(last, "t", P + "ctr_%d_%d" % (t, rr))

        def gp_f(gp):
            p.gate(gp, FIN)
            for t in range(4):
                for rr in range(8):
                    p.wait(gp, P + "add_%d_%d" % (t, rr))
                    p.mark(gp.dma_start(
                        out=d_out[(mc * 4 + t) * 128:(mc * 4 + t + 1) * 128,
                                  rr * 512:(rr + 1) * 512],
                        in_=outst[:, rr % 2, :]),
                        "g", P + "out_%d_%d" % (t, rr), 16)
            if DEBUG_TAPS and mc == 0:
                p.wait(gp, "m20_gev_31")
                p.mark(gp.dma_start(out=taps["gT"],
                                    in_=gT[:].rearrange("p b m -> p (b m)")),
                       "g", None, 16)

        blk.sync(sync_f)
        blk.vector(vec_f)
        blk.tensor(ten_f)
        blk.gpsimd(gp_f)

        return [P + "ctr_3_7", P + "add_3_7", P + "out_3_7", P + "ldx_3"]


def build_program():
    b1 = Builder(events=None)
    b1.build()
    b2 = Builder(events=b1.rec)
    nc = b2.build()
    assert b2.cnt == b1.cnt, (b1.cnt, b2.cnt)
    return nc


_NC_CACHE = {}
_LAST_RES = {}


def _get_nc():
    if "nc" not in _NC_CACHE:
        _NC_CACHE["nc"] = build_program()
    return _NC_CACHE["nc"]


def _make_mask(par):
    q = np.arange(128)[:, None]
    k = np.arange(128)[None, :]
    tri = np.where(k <= q, 0.0, -30000.0).astype(np.float32)
    m = np.zeros((128, 512), np.float32)
    if par == 0:
        m[:, 256:384] = tri
        m[:, 384:512] = -30000.0
    else:
        m[:, 384:512] = tri
    return np.ascontiguousarray(m.astype(bfdt))


def kernel(L0_attn_x, L0_attn_wq_w, L0_attn_wq_b, L0_attn_wk_w, L0_attn_wk_b,
           L0_attn_wv_w, L0_attn_wv_b, L0_attn_wo_w, L0_attn_wo_b,
           L0_ffn_w1, L0_ffn_s1, L0_ffn_w2, L0_ffn_s2, L0_ffn_w3, L0_ffn_s3):
    x = np.ascontiguousarray(np.asarray(L0_attn_x, dtype=np.float32))
    wqT = np.ascontiguousarray(np.asarray(L0_attn_wq_w, np.float32).T)
    wkT = np.ascontiguousarray(np.asarray(L0_attn_wk_w, np.float32).T)
    wvT = np.ascontiguousarray(np.asarray(L0_attn_wv_w, np.float32).T)
    woT = np.ascontiguousarray(np.asarray(L0_attn_wo_w, np.float32).T)
    wqb = np.ascontiguousarray(np.asarray(L0_attn_wq_b, np.float32).reshape(32, 128).T)
    wob = np.ascontiguousarray(np.asarray(L0_attn_wo_b, np.float32).reshape(32, 128).T)
    wkb = np.ascontiguousarray(np.asarray(L0_attn_wk_b, np.float32).reshape(128, 1))
    wvb = np.ascontiguousarray(np.asarray(L0_attn_wv_b, np.float32).reshape(128, 1))
    w1 = np.ascontiguousarray(np.asarray(L0_ffn_w1, np.int32))
    w2 = np.ascontiguousarray(np.asarray(L0_ffn_w2, np.int32))
    w3 = np.ascontiguousarray(np.asarray(L0_ffn_w3, np.int32))
    s1 = np.ascontiguousarray(np.asarray(L0_ffn_s1, np.float32))
    s2 = np.ascontiguousarray(np.asarray(L0_ffn_s2, np.float32))
    s3 = np.ascontiguousarray(np.asarray(L0_ffn_s3, np.float32))

    masks = [_make_mask(0), _make_mask(1)]
    in_maps = []
    for c in range(8):
        b, par = c // 2, c % 2
        xb = np.ascontiguousarray(x[b])
        xown = np.ascontiguousarray(xb.reshape(16, 128, D)[par::2].reshape(M, D))
        in_maps.append(dict(
            xkv=xb, xown=xown, wqT=wqT, wkT=wkT, wvT=wvT, woT=woT,
            wqb=wqb, wob=wob, wkb=wkb, wvb=wvb,
            w1p=w1, w2p=w2, w3p=w3, s1=s1, s2=s2, s3=s3,
            mask=masks[par]))

    nc = _get_nc()
    res = run_bass_kernel_spmd(nc, in_maps, core_ids=list(range(8)))
    _LAST_RES["res"] = res

    out = np.empty((B, T, D), np.float32)
    for c in range(8):
        b, par = c // 2, c % 2
        oc = np.asarray(res.results[c]["out"]).reshape(NSLOT, 128, D)
        out.reshape(B, 16, 128, D)[b, par::2] = oc
    return out


# version 10 (final)
# speedup vs baseline: 1.1797x; 1.0227x over previous
import numpy as np
import ml_dtypes
import concourse.bass as bass
import concourse.mybir as mybir
from concourse.bass_utils import run_bass_kernel_spmd
from contextlib import ExitStack

F32 = mybir.dt.float32
BF16 = mybir.dt.bfloat16
I32 = mybir.dt.int32
AF = mybir.ActivationFunctionType
ALU = mybir.AluOpType
AX = mybir.AxisListType
bfdt = ml_dtypes.bfloat16

B, T, D = 4, 2048, 4096
H, HD = 32, 128
HH = 11008
M = 1024
NSLOT = 8
C_ATT = 1.0 / float(np.sqrt(128.0))
DB = 32
HB = 86
NT = 24

DEBUG_TAPS = False
BUILD_UPTO = "all"
DETECT_RACES = True


def cdv(a, b):
    return (a + b - 1) // b


class Builder:
    def __init__(self, events=None):
        self.nc = nc = bass.Bass("TRN2", target_bir_lowering=False,
                                 detect_race_conditions=DETECT_RACES)
        self.events = events
        self.rec = {}
        self.sem = {}
        self.cnt = {}
        for nm in ("t", "v", "s", "g", "q"):
            self.sem[nm] = nc.alloc_semaphore("sem_" + nm)
            self.cnt[nm] = 0

    def mark(self, instr, key, name=None, k=1):
        instr.then_inc(self.sem[key], k)
        self.cnt[key] += k
        if name is not None:
            assert name not in self.rec, "dup event " + name
            self.rec[name] = (key, self.cnt[key])

    def evhere(self, name, key):
        if name not in self.rec:
            self.rec[name] = (key, self.cnt[key])

    def wait(self, engine, name):
        if self.events is None:
            return
        key, n = self.events[name]
        if n > 0:
            engine.wait_ge(self.sem[key], n)

    def gate(self, engine, names):
        for nm in names:
            self.wait(engine, nm)

    def vsp(self, v, n=2):
        for _ in range(n):
            v.tensor_scalar(out=self.junk[:, 0:8], in0=self.junk[:, 0:8],
                            scalar1=1.0, scalar2=None, op0=ALU.mult)

    def build(self):
        nc = self.nc
        p = self

        d_xkv = nc.dram_tensor("xkv", [T, D], F32, kind="ExternalInput").ap()
        d_xown = nc.dram_tensor("xown", [M, D], F32, kind="ExternalInput").ap()
        d_wqT = nc.dram_tensor("wqT", [D, D], F32, kind="ExternalInput").ap()
        d_wkT = nc.dram_tensor("wkT", [D, HD], F32, kind="ExternalInput").ap()
        d_wvT = nc.dram_tensor("wvT", [D, HD], F32, kind="ExternalInput").ap()
        d_woT = nc.dram_tensor("woT", [D, D], F32, kind="ExternalInput").ap()
        d_wqb = nc.dram_tensor("wqb", [128, 32], F32, kind="ExternalInput").ap()
        d_wob = nc.dram_tensor("wob", [128, 32], F32, kind="ExternalInput").ap()
        d_wkb = nc.dram_tensor("wkb", [128, 1], F32, kind="ExternalInput").ap()
        d_wvb = nc.dram_tensor("wvb", [128, 1], F32, kind="ExternalInput").ap()
        d_w1 = nc.dram_tensor("w1p", [352256, 64], I32, kind="ExternalInput").ap()
        d_w2 = nc.dram_tensor("w2p", [352256, 64], I32, kind="ExternalInput").ap()
        d_w3 = nc.dram_tensor("w3p", [352256, 64], I32, kind="ExternalInput").ap()
        d_s1 = nc.dram_tensor("s1", [704512], F32, kind="ExternalInput").ap()
        d_s2 = nc.dram_tensor("s2", [704512], F32, kind="ExternalInput").ap()
        d_s3 = nc.dram_tensor("s3", [704512], F32, kind="ExternalInput").ap()
        d_mask = nc.dram_tensor("mask", [128, 512], BF16, kind="ExternalInput").ap()
        d_out = nc.dram_tensor("out", [M, D], F32, kind="ExternalOutput").ap()
        d_ident = nc.inline_tensor(np.eye(128, dtype=bfdt), "ident_const").ap()

        d_xnT = nc.dram_tensor("xnT_sc", [128, DB * M], BF16, kind="Internal").ap()
        d_sT = nc.dram_tensor("sT_sc", [128, DB * M], BF16, kind="Internal").ap()
        d_w1t = nc.dram_tensor("w1t_sc", [HB, 128, D], BF16, kind="Internal").ap()
        d_w3t = nc.dram_tensor("w3t_sc", [HB, 128, D], BF16, kind="Internal").ap()
        d_w2t = nc.dram_tensor("w2t_sc", [DB, 128, HH], BF16, kind="Internal").ap()

        taps = {}
        if DEBUG_TAPS:
            taps["kT"] = nc.dram_tensor("tap_kT", [128, T], BF16, kind="ExternalOutput").ap()
            taps["vtok"] = nc.dram_tensor("tap_vtok", [128, T], BF16, kind="ExternalOutput").ap()
            taps["xnT"] = nc.dram_tensor("tap_xnT", [128, DB * M], BF16, kind="ExternalOutput").ap()
            taps["attnT"] = nc.dram_tensor("tap_attnT", [128, H * M], BF16, kind="ExternalOutput").ap()
            taps["sT"] = nc.dram_tensor("tap_sT", [128, DB * M], BF16, kind="ExternalOutput").ap()
            taps["hT"] = nc.dram_tensor("tap_hT", [128, HB * 512], BF16, kind="ExternalOutput").ap()
            taps["gT"] = nc.dram_tensor("tap_gT", [128, DB * 512], BF16, kind="ExternalOutput").ap()
            taps["st"] = nc.dram_tensor("tap_st", [128, 160], F32, kind="ExternalOutput").ap()
            taps["QT"] = nc.dram_tensor("tap_QT", [128, 2048], BF16, kind="ExternalOutput").ap()
            taps["probs"] = nc.dram_tensor("tap_probs", [128, 2048], BF16, kind="ExternalOutput").ap()
            taps["pT"] = nc.dram_tensor("tap_pT", [128, 2048], BF16, kind="ExternalOutput").ap()
            taps["sm"] = nc.dram_tensor("tap_sm", [128, 64], F32, kind="ExternalOutput").ap()
        self.taps = taps

        ctx = ExitStack()
        with ctx:
            S = lambda name, shape, dt: ctx.enter_context(nc.sbuf_tensor(name, shape, dt))
            ident = S("c_ident", [128, 128], BF16)
            mask = S("c_mask", [128, 512], BF16)
            wqbs = S("c_wqbs", [128, 32], F32)
            wqb = S("c_wqb", [128, 32], F32)
            wob = S("c_wob", [128, 32], F32)
            wkb = S("c_wkb", [128, 1], F32)
            wvb = S("c_wvb", [128, 1], F32)
            self.junk = S("c_junk", [128, 8], F32)
            s1w = S("c_s1w", [128, 32], F32)
            s2w = S("c_s2w", [128, 32], F32)
            mnw = S("c_mnw", [128, 32], F32)
            vrw = S("c_vrw", [128, 32], F32)
            rsw = S("c_rsw", [128, 32], F32)
            ssum = S("c_ssum", [128, 16], F32)
            srec = S("c_srec", [128, 16], F32)
            nmaxb = S("c_nmax", [128, 16], F32)

            psA = ctx.enter_context(nc.psum_tensor("psA", [128, 2048], F32))
            psB = ctx.enter_context(nc.psum_tensor("psB", [128, 2048], BF16))
            psC = ctx.enter_context(nc.psum_tensor("psC", [128, 1024], F32))

            blk = ctx.enter_context(nc.Block())
            self.blk = blk

            with ExitStack() as cAB:
                kT = cAB.enter_context(nc.sbuf_tensor("c_kT", [128, T], BF16))
                vtok = cAB.enter_context(nc.sbuf_tensor("c_vtok", [128, 16 * 128], BF16))
                xnownT = cAB.enter_context(nc.sbuf_tensor("b_xnownT", [128, DB, M], BF16))

                with ExitStack() as c1:
                    S1 = lambda n_, sh, dt: c1.enter_context(nc.sbuf_tensor(n_, sh, dt))
                    xf = S1("p1_xf", [128, 3, D], F32)
                    wkT = S1("c_wkT", [128, DB * 128], BF16)
                    wvT = S1("c_wvT", [128, DB * 128], BF16)
                    xnb = S1("p1_xnb", [128, 2, D], BF16)
                    xnkv = S1("p1_xnkv", [128, DB, 512], BF16)
                    vT = S1("p1_vT", [128, T], BF16)
                    self.emit_p1(d_xkv, d_xown, d_wkT, d_wvT, d_ident, d_mask,
                                 d_wqb, d_wob, d_wkb, d_wvb,
                                 ident, mask, wqb, wqbs, wob, wkb, wvb, wkT, wvT, kT,
                                 vtok, s1w, s2w, mnw, vrw, rsw,
                                 xnownT, xf, xnb, xnkv, vT, psA, psB)

                P1_FIN = ["vtrr_1", "vtev_1", "vev_3", "ld_x_23"]
                if DEBUG_TAPS:
                    def p1_taps(gp):
                        p.gate(gp, P1_FIN)
                        p.mark(gp.dma_start(out=taps["st"][:, 0:32], in_=s1w[:]), "g", None, 16)
                        p.mark(gp.dma_start(out=taps["st"][:, 32:64], in_=s2w[:]), "g", None, 16)
                        p.mark(gp.dma_start(out=taps["st"][:, 64:96], in_=mnw[:]), "g", None, 16)
                        p.mark(gp.dma_start(out=taps["st"][:, 96:128], in_=vrw[:]), "g", None, 16)
                        p.mark(gp.dma_start(out=taps["st"][:, 128:160], in_=rsw[:]), "g", None, 16)
                        p.mark(gp.dma_start(out=taps["kT"], in_=kT[:]), "g", None, 16)
                        p.mark(gp.dma_start(out=taps["vtok"], in_=vtok[:]), "g", None, 16)
                        p.mark(gp.dma_start(out=taps["xnT"],
                                            in_=xnownT[:].rearrange("p b m -> p (b m)")),
                               "g", None, 16)
                    blk.gpsimd(p1_taps)
                if BUILD_UPTO == "p1":
                    def p_final0(gp):
                        gp.wait_ge(self.sem["g"], self.cnt["g"])
                    blk.gpsimd(p_final0)
                    return self.nc

                attnT = cAB.enter_context(nc.sbuf_tensor("b_attnT", [128, H * M], BF16))

                with ExitStack() as c2:
                    S2 = lambda n_, sh, dt: c2.enter_context(nc.sbuf_tensor(n_, sh, dt))
                    wq_f = S2("p2_wqf", [128, D], F32)
                    wq_b = S2("p2_wqb", [128, 2, D], BF16)
                    QT = S2("p2_QT", [128, 2, M], BF16)
                    probs = S2("p2_probs", [128, 2, 2048], BF16)
                    probsT = S2("p2_probsT", [128, 2, 2048], BF16)
                    self.emit_p2(P1_FIN, d_wqT, d_xnT, ident, mask, wqbs,
                                 kT, vtok, xnownT, attnT,
                                 ssum, srec, nmaxb,
                                 wq_f, wq_b, QT, probs, probsT, psA, psB, psC)

                P2_FIN = ["avmm_31_7", "pev_31_7_1", "avev_31_7", "ldq_31", "gxn"]
                if DEBUG_TAPS:
                    def p2_taps(gp):
                        p.gate(gp, P2_FIN)
                        p.mark(gp.dma_start(out=taps["QT"][:, 0:1024], in_=QT[:, 1, :]), "g", None, 16)
                        p.mark(gp.dma_start(out=taps["probs"], in_=probs[:, 1, :]), "g", None, 16)
                        p.mark(gp.dma_start(out=taps["pT"], in_=probsT[:, 1, :]), "g", None, 16)
                        p.mark(gp.dma_start(out=taps["sm"][:, 0:16], in_=ssum[:]), "g", None, 16)
                        p.mark(gp.dma_start(out=taps["sm"][:, 16:32], in_=srec[:]), "g", None, 16)
                        p.mark(gp.dma_start(out=taps["sm"][:, 32:48], in_=nmaxb[:]), "g", None, 16)
                        p.mark(gp.dma_start(out=taps["attnT"], in_=attnT[:]), "g", None, 16)
                    blk.gpsimd(p2_taps)
                if BUILD_UPTO == "p2":
                    def p_final0(gp):
                        gp.wait_ge(self.sem["g"], self.cnt["g"])
                    blk.gpsimd(p_final0)
                    return self.nc

                with ExitStack() as c3:
                    S3 = lambda n_, sh, dt: c3.enter_context(nc.sbuf_tensor(n_, sh, dt))
                    wo_f = S3("p3_wof", [128, D], F32)
                    wo_b = S3("p3_wob", [128, 2, D], BF16)
                    sst = S3("p3_sst", [128, 2, M], BF16)
                    self.emit_p3(P2_FIN, d_woT, d_sT, wob, attnT,
                                 wo_f, wo_b, sst, psA)

            P3_FIN = ["womm_31_1", "cvo_31", "sev_31_1", "ldo_31", "gsT_31"]
            if BUILD_UPTO == "p3":
                def p_final0(gp):
                    gp.wait_ge(self.sem["g"], self.cnt["g"])
                blk.gpsimd(p_final0)
                return self.nc

            with ExitStack() as c4:
                hT = c4.enter_context(nc.sbuf_tensor("b_hT", [128, HB, 512], BF16))
                prev = P3_FIN
                for mc in range(2):
                    with ExitStack() as cm1:
                        Sm = lambda n_, sh, dt: cm1.enter_context(
                            nc.sbuf_tensor(n_ + "_%d" % mc, sh, dt))
                        xnsl = Sm("p4_xnsl", [128, DB, 512], BF16)
                        pkr = Sm("p4_pkr", [128, 2, 2048], I32)
                        sci1 = Sm("p4_sci1", [128, 1024], I32)
                        sci3 = Sm("p4_sci3", [128, 1024], I32)
                        nat1 = Sm("p4_nat1", [128, D], BF16)
                        nat3 = Sm("p4_nat3", [128, D], BF16)
                        w1t = Sm("p4_w1t", [128, 2, D], BF16)
                        w3t = Sm("p4_w3t", [128, 2, D], BF16)
                        sc1 = Sm("p4_sc1", [128, 2, 64], F32)
                        sc3 = Sm("p4_sc3", [128, 2, 64], F32)
                        asil = Sm("p4_asil", [128, 2, 512], BF16)
                        bsb = Sm("p4_bsb", [128, 2, 512], BF16)
                        prev = self.emit_mm1(mc, prev, d_w1, d_w3, d_s1, d_s3,
                                             d_w1t, d_w3t, d_xnT, hT, xnsl, pkr,
                                             sci1, sci3, nat1, nat3, w1t, w3t,
                                             sc1, sc3, asil, bsb, psA, psB, ident)
                    if BUILD_UPTO == "mm1":
                        break
                    with ExitStack() as cm2:
                        gT = cm2.enter_context(
                            nc.sbuf_tensor("p5_gT_%d" % mc, [128, DB, 512], BF16))
                        with ExitStack() as cm2i:
                            Si = lambda n_, sh, dt: cm2i.enter_context(
                                nc.sbuf_tensor(n_ + "_%d" % mc, sh, dt))
                            pk2 = Si("p5_pk2", [128, 2, 2752], I32)
                            sci2 = Si("p5_sci2", [128, 1536], I32)
                            nat2 = Si("p5_nat2", [128, 5504], BF16)
                            w2t = Si("p5_w2t", [128, 2, HH], BF16)
                            sc2 = Si("p5_sc2", [128, 2, 172], F32)
                            prev = self.emit_mm2(mc, prev, d_w2, d_s2, d_w2t, hT,
                                                 pk2, sci2, nat2, w2t, sc2, gT,
                                                 psA, psB, ident)
                        if BUILD_UPTO == "mm2":
                            break
                        with ExitStack() as cm3:
                            Sc = lambda n_, sh, dt: cm3.enter_context(
                                nc.sbuf_tensor(n_ + "_%d" % mc, sh, dt))
                            ssl = Sc("p6_ssl", [128, DB, 512], BF16)
                            xcf = Sc("p6_xcf", [128, 2, D], F32)
                            outst = Sc("p6_outst", [128, 2, 512], F32)
                            prev = self.emit_combine(mc, prev, d_sT, d_xown,
                                                     d_out, gT, ssl, xcf,
                                                     outst, psB, ident)
                    if BUILD_UPTO in ("mm1", "mm2", "cb0"):
                        break

                def p_final(gp):
                    gp.wait_ge(self.sem["g"], self.cnt["g"])
                blk.gpsimd(p_final)

        return self.nc

    def emit_p1(self, d_xkv, d_xown, d_wkT, d_wvT, d_ident, d_mask,
                d_wqb, d_wob, d_wkb, d_wvb,
                ident, mask, wqb, wqbs, wob, wkb, wvb, wkT, wvT, kT,
                vtok, s1w, s2w, mnw, vrw, rsw,
                xnownT, xf, xnb, xnkv, vT, psA, psB):
        p = self
        blk = self.blk

        def sync_f(sync):
            for dst, src in ((ident, d_ident), (mask, d_mask), (wqb, d_wqb),
                             (wob, d_wob), (wkb, d_wkb), (wvb, d_wvb)):
                p.mark(sync.dma_start(out=dst[:], in_=src), "q",
                       "ld_wqb" if dst is wqb else None, 16)
            p.mark(sync.dma_start(
                out=xf[:, 0, :].rearrange("p (b f) -> p b f", b=DB),
                in_=d_wkT.rearrange("(b p) f -> p b f", p=128)),
                "q", "ld_wk", 16)
            p.mark(sync.dma_start(
                out=xf[:, 1, :].rearrange("p (b f) -> p b f", b=DB),
                in_=d_wvT.rearrange("(b p) f -> p b f", p=128)),
                "q", "ld_wv", 16)
            for t in range(NT):
                if t == 0:
                    p.wait(sync, "cv_wk")
                elif t == 1:
                    p.wait(sync, "cv_wv")
                else:
                    if t >= 3:
                        p.wait(sync, "stt_%d" % (t - 3))
                        p.wait(sync, "sq_%d" % (t - 3))
                src = (d_xkv[t * 128:(t + 1) * 128, :] if t < 16
                       else d_xown[(t - 16) * 128:(t - 15) * 128, :])
                p.mark(sync.dma_start(out=xf[:, t % 3, :], in_=src),
                       "q", "ld_x_%d" % t, 16)

        def scalar_f(scalar):
            for t in range(NT):
                p.wait(scalar, "ld_x_%d" % t)
                if t >= 2:
                    p.wait(scalar, "xtr_%d" % (t - 2))
                p.mark(scalar.activation(out=xnb[:, t % 2, :], in_=xf[:, t % 3, :],
                                         func=AF.Square, bias=0.0, scale=1.0,
                                         accum_out=s2w[:, t:t + 1]),
                       "s", "sq_%d" % t)
                if t % 2 == 1:
                    g = t // 2
                    c0 = g * 2
                    p.wait(scalar, "var_%d" % g)
                    p.mark(scalar.activation(out=rsw[:, c0:c0 + 2],
                                             in_=vrw[:, c0:c0 + 2],
                                             func=AF.Sqrt, bias=0.0, scale=1.0),
                           "s", "sqrt_%d" % g)
                if t < 16 and t % 4 == 3:
                    mc4 = t // 4
                    p.wait(scalar, "kmm_%d" % mc4)
                    p.mark(scalar.activation(out=kT[:, mc4 * 512:(mc4 + 1) * 512],
                                             in_=psA[:, 0:512], func=AF.Identity,
                                             bias=wkb[:], scale=1.0),
                           "s", "kev_%d" % mc4)
                    p.wait(scalar, "vmm_%d" % mc4)
                    p.mark(scalar.activation(out=vT[:, mc4 * 512:(mc4 + 1) * 512],
                                             in_=psA[:, 512:1024], func=AF.Identity,
                                             bias=wvb[:], scale=1.0),
                           "s", "vev_%d" % mc4)

        def vector_f(vector):
            vector.memset(self.junk[:], 0.0)
            for buf in (s1w, s2w, mnw, vrw, rsw):
                vector.memset(buf[:], 0.0)
            p.wait(vector, "ld_wqb")
            vector.tensor_scalar(out=wqbs[:], in0=wqb[:], scalar1=C_ATT,
                                 scalar2=None, op0=ALU.mult)
            p.wait(vector, "ld_wk")
            p.mark(vector.tensor_copy(out=wkT[:], in_=xf[:, 0, :]), "v", "cv_wk")
            p.wait(vector, "ld_wv")
            p.mark(vector.tensor_copy(out=wvT[:], in_=xf[:, 1, :]), "v", "cv_wv")
            for t in range(NT):
                p.wait(vector, "ld_x_%d" % t)
                p.mark(vector.reduce_sum(out=s1w[:, t:t + 1], in_=xf[:, t % 3, :],
                                         axis=AX.X), "v", "rsum_%d" % t)
                if t % 2 != 1:
                    continue
                g = t // 2
                c0 = g * 2
                p.wait(vector, "sq_%d" % t)
                self.vsp(vector)
                vector.tensor_scalar(out=mnw[:, c0:c0 + 2], in0=s1w[:, c0:c0 + 2],
                                     scalar1=1.0 / D, scalar2=None, op0=ALU.mult)
                vector.tensor_scalar(out=vrw[:, c0:c0 + 2], in0=s2w[:, c0:c0 + 2],
                                     scalar1=1.0 / D, scalar2=None, op0=ALU.mult)
                vector.tensor_mul(out=s2w[:, c0:c0 + 2], in0=mnw[:, c0:c0 + 2],
                                  in1=mnw[:, c0:c0 + 2])
                self.vsp(vector)
                p.mark(vector.scalar_tensor_tensor(
                    out=vrw[:, c0:c0 + 2], in0=vrw[:, c0:c0 + 2], scalar=1e-5,
                    in1=s2w[:, c0:c0 + 2], op0=ALU.add, op1=ALU.subtract),
                    "v", "var_%d" % g)
                p.wait(vector, "sqrt_%d" % g)
                vector.reciprocal(out=rsw[:, c0:c0 + 2], in_=rsw[:, c0:c0 + 2])
                self.vsp(vector)
                for tt in (t - 1, t):
                    p.mark(vector.scalar_tensor_tensor(
                        out=xnb[:, tt % 2, :], in0=xf[:, tt % 3, :],
                        scalar=mnw[:, tt:tt + 1],
                        in1=rsw[:, tt:tt + 1].broadcast_to([128, D]),
                        op0=ALU.subtract, op1=ALU.mult), "v", "stt_%d" % tt)
                    for r in range(4):
                        p.wait(vector, "xtrr_%d_%d" % (tt, r))
                        if tt < 16:
                            if tt >= 4:
                                p.wait(vector, "vmm_%d" % (tt // 4 - 1))
                            dst = xnkv[:, r * 8:(r + 1) * 8,
                                       (tt % 4) * 128:(tt % 4) * 128 + 128]
                        else:
                            dst = xnownT[:, r * 8:(r + 1) * 8,
                                         (tt - 16) * 128:(tt - 15) * 128]
                        p.mark(vector.tensor_copy(
                            out=dst,
                            in_=psB[:, (r % 2) * 1024:(r % 2) * 1024 + 1024]
                            .rearrange("p (b f) -> p b f", b=8)),
                            "v", "xev_%d_%d" % (tt, r))
            for r in range(2):
                p.wait(vector, "vtrr_%d" % r)
                p.mark(vector.tensor_copy(
                    out=vtok[:, r * 1024:(r + 1) * 1024],
                    in_=psB[:, (r % 2) * 1024:(r % 2) * 1024 + 1024]),
                    "v", "vtev_%d" % r)

        def tensor_f(tensor):
            for t in range(NT):
                p.wait(tensor, "stt_%d" % t)
                for r in range(4):
                    rid = t * 4 + r
                    if rid >= 2:
                        p.wait(tensor, "xev_%d_%d" % ((rid - 2) // 4, (rid - 2) % 4))
                    last = None
                    for qq in range(8):
                        q = r * 8 + qq
                        last = tensor.transpose(
                            psB[:, (r % 2) * 1024 + qq * 128:
                                (r % 2) * 1024 + (qq + 1) * 128],
                            xnb[:, t % 2, q * 128:(q + 1) * 128], ident[:])
                    p.mark(last, "t", "xtrr_%d_%d" % (t, r))
                self.evhere("xtr_%d" % t, "t")
                if t < 16 and t % 4 == 3:
                    mc4 = t // 4
                    p.wait(tensor, "cv_wk")
                    p.wait(tensor, "cv_wv")
                    p.wait(tensor, "xev_%d_3" % t)
                    if mc4 >= 1:
                        p.wait(tensor, "kev_%d" % (mc4 - 1))
                        p.wait(tensor, "vev_%d" % (mc4 - 1))
                    last = None
                    for dd in range(DB):
                        last = tensor.matmul(psA[:, 0:512],
                                             wkT[:, dd * 128:(dd + 1) * 128],
                                             xnkv[:, dd, :],
                                             start=(dd == 0), stop=(dd == DB - 1))
                    p.mark(last, "t", "kmm_%d" % mc4)
                    last = None
                    for dd in range(DB):
                        last = tensor.matmul(psA[:, 512:1024],
                                             wvT[:, dd * 128:(dd + 1) * 128],
                                             xnkv[:, dd, :],
                                             start=(dd == 0), stop=(dd == DB - 1))
                    p.mark(last, "t", "vmm_%d" % mc4)
            for r in range(2):
                p.wait(tensor, "vev_3")
                p.wait(tensor, "xev_23_3")
                if r == 1:
                    p.wait(tensor, "vtev_0")
                last = None
                for b8 in range(8):
                    kb = r * 8 + b8
                    last = tensor.transpose(
                        psB[:, (r % 2) * 1024 + b8 * 128:(r % 2) * 1024 + (b8 + 1) * 128],
                        vT[:, kb * 128:(kb + 1) * 128], ident[:])
                p.mark(last, "t", "vtrr_%d" % r)

        blk.sync(sync_f)
        blk.scalar(scalar_f)
        blk.vector(vector_f)
        blk.tensor(tensor_f)

    def emit_p2(self, FIN, d_wqT, d_xnT, ident, mask, wqbs, kT, vtok, xnownT,
                attnT, ssum, srec, nmaxb, wq_f, wq_b, QT, probs,
                probsT, psA, psB, psC):
        p = self
        blk = self.blk
        taps = self.taps

        def nrounds(i):
            return cdv(2 * i + 2, 8)

        def gp_f(gp):
            p.gate(gp, FIN)
            p.mark(gp.dma_start(out=d_xnT,
                                in_=xnownT[:].rearrange("p b m -> p (b m)")),
                   "g", "gxn", 16)

        def sync_f(sync):
            p.gate(sync, FIN)
            for h in range(H):
                if h >= 1:
                    p.wait(sync, "cvq_%d" % (h - 1))
                p.mark(sync.dma_start(
                    out=wq_f[:].rearrange("p (b f) -> p b f", b=DB),
                    in_=d_wqT[:, h * 128:(h + 1) * 128]
                    .rearrange("(b p) f -> p b f", p=128)),
                    "q", "ldq_%d" % h, 16)

        def vector_f(vector):
            p.gate(vector, FIN)
            for h in range(H):
                p.wait(vector, "ldq_%d" % h)
                if h >= 2:
                    p.wait(vector, "qmm_%d_1" % (h - 2))
                p.mark(vector.tensor_copy(out=wq_b[:, h % 2, :], in_=wq_f[:]),
                       "v", "cvq_%d" % h)
                for i in range(NSLOT):
                    ii = (h * NSLOT + i) % 16
                    Ki = 256 * (i + 1)
                    ri = (i % 2) * 1024 if i <= 3 else 0
                    p.wait(vector, "smm_%d_%d" % (h, i))
                    if i >= 1:
                        p.wait(vector, "exp_%d_%d" % (h, i - 1))
                        vector.reciprocal(out=srec[:, ii - 1:ii],
                                          in_=ssum[:, ii - 1:ii])
                    else:
                        self.vsp(vector)
                    p.mark(vector.reduce_max(out=nmaxb[:, ii:ii + 1],
                                             in_=psA[:, ri:ri + Ki],
                                             axis=AX.X, negate=True),
                           "v", "rmax_%d_%d" % (h, i))
                    if i >= 1:
                        Kp = 256 * i
                        p.mark(vector.tensor_scalar(
                            out=probs[:, (i - 1) % 2, 0:Kp],
                            in0=probs[:, (i - 1) % 2, 0:Kp],
                            scalar1=srec[:, ii - 1:ii], scalar2=None,
                            op0=ALU.mult), "v", "vnorm_%d_%d" % (h, i - 1))
                    else:
                        self.vsp(vector)
                    if i >= 1:
                        if i >= 3:
                            p.wait(vector, "avmm_%d_%d" % (h, i - 3))
                        elif h >= 1:
                            p.wait(vector, "avmm_%d_%d" % (h - 1, 5 + i))
                        for r in range(nrounds(i - 1)):
                            nb_r = min(8, 2 * i - r * 8)
                            p.wait(vector, "ttr_%d_%d_%d" % (h, i - 1, r))
                            p.mark(vector.tensor_copy(
                                out=probsT[:, (i - 1) % 2,
                                           r * 1024:r * 1024 + nb_r * 128],
                                in_=psB[:, (r % 2) * 1024:
                                        (r % 2) * 1024 + nb_r * 128]),
                                "v", "pev_%d_%d_%d" % (h, i - 1, r))
                ii7 = (h * NSLOT + 7) % 16
                p.wait(vector, "exp_%d_7" % h)
                vector.reciprocal(out=srec[:, ii7:ii7 + 1], in_=ssum[:, ii7:ii7 + 1])
                self.vsp(vector)
                p.mark(vector.tensor_scalar(
                    out=probs[:, 1, 0:2048], in0=probs[:, 1, 0:2048],
                    scalar1=srec[:, ii7:ii7 + 1], scalar2=None, op0=ALU.mult),
                    "v", "vnorm_%d_7" % h)
                p.wait(vector, "avmm_%d_5" % h)
                for r in range(2):
                    p.wait(vector, "ttr_%d_7_%d" % (h, r))
                    p.mark(vector.tensor_copy(
                        out=probsT[:, 1, r * 1024:(r + 1) * 1024],
                        in_=psB[:, (r % 2) * 1024:(r % 2) * 1024 + 1024]),
                        "v", "pev_%d_7_%d" % (h, r))

        def scalar_f(scalar):
            p.gate(scalar, FIN)
            for h in range(H):
                for mc in range(2):
                    p.wait(scalar, "qmm_%d_%d" % (h, mc))
                    if h >= 2:
                        p.wait(scalar, "smm_%d_7" % (h - 2))
                    p.mark(scalar.activation(
                        out=QT[:, h % 2, mc * 512:(mc + 1) * 512],
                        in_=psC[:, 512:1024], func=AF.Identity,
                        bias=wqbs[:, h:h + 1], scale=C_ATT),
                        "s", "qev_%d_%d" % (h, mc))
                for i in range(NSLOT):
                    ii = (h * NSLOT + i) % 16
                    Ki = 256 * (i + 1)
                    p.wait(scalar, "rmax_%d_%d" % (h, i))
                    if i >= 2:
                        p.wait(scalar, "ttr_last_%d_%d" % (h, i - 2))
                    elif h >= 1:
                        p.wait(scalar, "ttr_last_%d_%d" % (h - 1, 6 + i))
                    if h >= 2:
                        p.wait(scalar, "vnorm_%d_%d" % (h - 2, i))
                    ri = (i % 2) * 1024 if i <= 3 else 0
                    p.mark(scalar.activation(
                        out=probs[:, i % 2, 0:Ki], in_=psA[:, ri:ri + Ki],
                        func=AF.Exp, bias=nmaxb[:, ii:ii + 1], scale=1.0,
                        accum_out=ssum[:, ii:ii + 1]),
                        "s", "exp_%d_%d" % (h, i))
                    if i >= 1:
                        p.wait(scalar, "avmm_%d_%d" % (h, i - 1))
                        p.mark(scalar.activation(
                            out=attnT[:, h * M + (i - 1) * 128: h * M + i * 128],
                            in_=psC[:, ((i - 1) % 2) * 128:((i - 1) % 2) * 128 + 128],
                            func=AF.Identity, bias=0.0, scale=1.0),
                            "s", "avev_%d_%d" % (h, i - 1))
                p.wait(scalar, "avmm_%d_7" % h)
                p.mark(scalar.activation(
                    out=attnT[:, h * M + 7 * 128: h * M + 8 * 128],
                    in_=psC[:, 128:256], func=AF.Identity, bias=0.0, scale=1.0),
                    "s", "avev_%d_7" % h)

        def tensor_f(tensor):
            p.gate(tensor, FIN)

            def ptr_attnv(h, i):
                nblk = 2 * i + 2
                p.wait(tensor, "vnorm_%d_%d" % (h, i))
                for r in range(nrounds(i)):
                    nb_r = min(8, nblk - r * 8)
                    rid = self.p2_rounds
                    self.p2_rounds += 1
                    prev = self.p2_round_ev.get(rid - 2)
                    if prev is not None:
                        p.wait(tensor, prev)
                    self.p2_round_ev[rid] = "pev_%d_%d_%d" % (h, i, r)
                    last = None
                    for u in range(nb_r):
                        tb = r * 8 + u
                        last = tensor.transpose(
                            psB[:, (r % 2) * 1024 + u * 128:
                                (r % 2) * 1024 + (u + 1) * 128],
                            probs[:, i % 2, tb * 128:(tb + 1) * 128], ident[:])
                    p.mark(last, "t", "ttr_%d_%d_%d" % (h, i, r))
                self.evhere("ttr_last_%d_%d" % (h, i), "t")
                if i >= 2:
                    p.wait(tensor, "avev_%d_%d" % (h, i - 2))
                elif h >= 1:
                    p.wait(tensor, "avev_%d_%d" % (h - 1, 6 + i))
                last = None
                for tb in range(nblk):
                    if tb % 8 == 0:
                        p.wait(tensor, "pev_%d_%d_%d" % (h, i, tb // 8))
                    last = tensor.matmul(
                        psC[:, (i % 2) * 128:(i % 2) * 128 + 128],
                        vtok[:, tb * 128:(tb + 1) * 128],
                        probsT[:, i % 2, tb * 128:(tb + 1) * 128],
                        start=(tb == 0), stop=(tb == nblk - 1))
                p.mark(last, "t", "avmm_%d_%d" % (h, i))

            for h in range(H):
                p.wait(tensor, "cvq_%d" % h)
                for mc in range(2):
                    if h == 0 and mc == 0:
                        pass
                    elif mc == 1:
                        p.wait(tensor, "qev_%d_0" % h)
                    else:
                        p.wait(tensor, "qev_%d_1" % (h - 1))
                    last = None
                    for dd in range(DB):
                        last = tensor.matmul(
                            psC[:, 512:1024],
                            wq_b[:, h % 2, dd * 128:(dd + 1) * 128],
                            xnownT[:, dd, mc * 512:(mc + 1) * 512],
                            start=(dd == 0), stop=(dd == DB - 1))
                    p.mark(last, "t", "qmm_%d_%d" % (h, mc))
                for i in range(NSLOT):
                    Ki = 256 * (i + 1)
                    ri = (i % 2) * 1024 if i <= 3 else 0
                    p.wait(tensor, "qev_%d_1" % h)
                    if i <= 1:
                        pus = [(h - 1, 7)] if h >= 1 else []
                    elif i in (2, 3):
                        pus = [(h, i - 2)]
                    elif i == 4:
                        pus = [(h, 2), (h, 3)]
                    else:
                        pus = [(h, i - 1)]
                    for (ph, pi) in pus:
                        p.wait(tensor, "exp_%d_%d" % (ph, pi))
                        p.wait(tensor, "rmax_%d_%d" % (ph, pi))
                    nch = cdv(Ki, 512)
                    for c in range(nch):
                        nw = min(512, Ki - c * 512)
                        tensor.matmul(
                            psA[:, ri + c * 512:ri + c * 512 + nw],
                            QT[:, h % 2, i * 128:(i + 1) * 128],
                            kT[:, c * 512:c * 512 + nw], start=True,
                            stop=(c != nch - 1))
                    reg0 = ri + (nch - 1) * 512
                    mrhs = mask[:, 0:512] if nw == 512 else mask[:, 256:512]
                    p.mark(tensor.matmul(psA[:, reg0:reg0 + nw], ident[:],
                                         mrhs, start=False, stop=True),
                           "t", "smm_%d_%d" % (h, i))
                    if i >= 1:
                        ptr_attnv(h, i - 1)
                ptr_attnv(h, 7)

        self.p2_rounds = 0
        self.p2_round_ev = {}
        blk.gpsimd(gp_f)
        blk.sync(sync_f)
        blk.vector(vector_f)
        blk.scalar(scalar_f)
        blk.tensor(tensor_f)

    def emit_p3(self, FIN, d_woT, d_sT, wob, attnT, wo_f, wo_b, sst, psA):
        p = self
        blk = self.blk
        taps = self.taps

        def sync_f(sync):
            p.gate(sync, FIN)
            for j in range(DB):
                if j >= 1:
                    p.wait(sync, "cvo_%d" % (j - 1))
                p.mark(sync.dma_start(
                    out=wo_f[:].rearrange("p (b f) -> p b f", b=DB)[:, 0:16, :],
                    in_=d_woT[0:2048, j * 128:(j + 1) * 128]
                    .rearrange("(b p) f -> p b f", p=128)),
                    "q", "ldo_%d" % j, 16)

        def vector_f(vector):
            p.gate(vector, FIN)
            for j in range(DB):
                p.wait(vector, "ldo_%d" % j)
                p.wait(vector, "ldo2_%d" % j)
                if j >= 2:
                    p.wait(vector, "womm_%d_1" % (j - 2))
                p.mark(vector.tensor_copy(out=wo_b[:, j % 2, :], in_=wo_f[:]),
                       "v", "cvo_%d" % j)

        def tensor_f(tensor):
            p.gate(tensor, FIN)
            for j in range(DB):
                p.wait(tensor, "cvo_%d" % j)
                for mc in range(2):
                    if j >= 2:
                        p.wait(tensor, "sev_%d_%d" % (j - 2, mc))
                    reg = (j % 2) * 1024 + mc * 512
                    last = None
                    for a in range(H):
                        last = tensor.matmul(
                            psA[:, reg:reg + 512],
                            wo_b[:, j % 2, a * 128:(a + 1) * 128],
                            attnT[:, a * M + mc * 512: a * M + (mc + 1) * 512],
                            start=(a == 0), stop=(a == H - 1))
                    p.mark(last, "t", "womm_%d_%d" % (j, mc))

        def scalar_f(scalar):
            p.gate(scalar, FIN)
            for j in range(DB):
                for mc in range(2):
                    p.wait(scalar, "womm_%d_%d" % (j, mc))
                    if j >= 2:
                        p.wait(scalar, "gsT_%d" % (j - 2))
                    reg = (j % 2) * 1024 + mc * 512
                    p.mark(scalar.activation(
                        out=sst[:, j % 2, mc * 512:(mc + 1) * 512],
                        in_=psA[:, reg:reg + 512], func=AF.Identity,
                        bias=wob[:, j:j + 1], scale=1.0),
                        "s", "sev_%d_%d" % (j, mc))

        def gp_f(gp):
            p.gate(gp, FIN)
            for j in range(DB):
                if j >= 1:
                    p.wait(gp, "cvo_%d" % (j - 1))
                p.mark(gp.dma_start(
                    out=wo_f[:].rearrange("p (b f) -> p b f", b=DB)[:, 16:32, :],
                    in_=d_woT[2048:4096, j * 128:(j + 1) * 128]
                    .rearrange("(b p) f -> p b f", p=128)),
                    "g", "ldo2_%d" % j, 16)
                if j >= 1:
                    p.wait(gp, "sev_%d_0" % (j - 1))
                    p.wait(gp, "sev_%d_1" % (j - 1))
                    p.mark(gp.dma_start(out=d_sT[:, (j - 1) * M:j * M],
                                        in_=sst[:, (j - 1) % 2, :]),
                           "g", "gsT_%d" % (j - 1), 16)
            p.wait(gp, "sev_31_0")
            p.wait(gp, "sev_31_1")
            p.mark(gp.dma_start(out=d_sT[:, 31 * M:32 * M],
                                in_=sst[:, 1, :]),
                   "g", "gsT_31", 16)
            if DEBUG_TAPS:
                p.wait(gp, "avev_31_7")
                p.mark(gp.dma_start(out=taps["attnT"], in_=attnT[:]), "g", None, 16)
                p.wait(gp, "gsT_31")
                p.mark(gp.dma_start(out=taps["sT"], in_=d_sT), "g", None, 16)

        blk.sync(sync_f)
        blk.vector(vector_f)
        blk.tensor(tensor_f)
        blk.scalar(scalar_f)
        blk.gpsimd(gp_f)

    def emit_mm1(self, mc, FIN, d_w1, d_w3, d_s1, d_s3, d_w1t, d_w3t,
                 d_xnT, hT, xnsl, pkr, sci1, sci3, nat1, nat3, w1t, w3t,
                 sc1, sc3, asil, bsb, psA, psB, ident):
        p = self
        blk = self.blk
        P = "m1%d_" % mc

        def sync_f(sync):
            p.gate(sync, FIN)
            if mc == 0:
                p.wait(sync, "gxn")
            else:
                p.wait(sync, "m10_mm_85")
            p.mark(sync.dma_start(
                out=xnsl[:],
                in_=d_xnT.rearrange("p (b m) -> p b m", b=DB)[:, :, mc * 512:(mc + 1) * 512]),
                "q", P + "ldxn", 16)
            for b in range(HB):
                if mc == 0:
                    if b >= 1:
                        p.wait(sync, P + "dqv1_%d" % (b - 1))
                    p.mark(sync.dma_start(
                        out=pkr[:, 0, :],
                        in_=d_w1[b * 4096:(b + 1) * 4096, :]
                        .rearrange("(p r) k -> p (r k)", p=128)),
                        "q", P + "ldw1_%d" % b, 16)
                    if b >= 2:
                        p.wait(sync, P + "dqh1_%d_1" % (b - 2))
                    p.mark(sync.dma_start(
                        out=sc1[:, b % 2, :],
                        in_=d_s1[b * 8192:(b + 1) * 8192]
                        .rearrange("(p g) -> p g", p=128)),
                        "q", P + "ldsc1_%d" % b, 16)
                else:
                    if b >= 2:
                        p.wait(sync, P + "mm_%d" % (b - 2))
                    p.mark(sync.dma_start(out=w1t[:, b % 2, :], in_=d_w1t[b]),
                           "q", P + "ldt1_%d" % b, 16)

        def gpl_f(gp):
            p.gate(gp, FIN)

            def gws(b):
                for which in (1, 3):
                    wdst = w1t if which == 1 else w3t
                    wdram = d_w1t if which == 1 else d_w3t
                    p.wait(gp, P + "wev%d_%d_3" % (which, b))
                    p.mark(gp.dma_start(out=wdram[b], in_=wdst[:, b % 2, :]),
                           "g", P + "gw%d_%d" % (which, b), 16)

            def htt(bb):
                p.wait(gp, P + "abev_%d" % bb)
                p.mark(gp.tensor_mul(
                    out=hT[:, bb, :], in0=asil[:, bb % 2, :],
                    in1=bsb[:, bb % 2, :]),
                    "g", P + "hTT_%d" % bb, 1)

            for b in range(HB):
                if mc == 0:
                    if b >= 1:
                        p.wait(gp, P + "dqv3_%d" % (b - 1))
                    p.mark(gp.dma_start(
                        out=pkr[:, 1, :],
                        in_=d_w3[b * 4096:(b + 1) * 4096, :]
                        .rearrange("(p r) k -> p (r k)", p=128)),
                        "g", P + "ldw3_%d" % b, 16)
                    if b >= 2:
                        p.wait(gp, P + "dqh3_%d_1" % (b - 2))
                    p.mark(gp.dma_start(
                        out=sc3[:, b % 2, :],
                        in_=d_s3[b * 8192:(b + 1) * 8192]
                        .rearrange("(p g) -> p g", p=128)),
                        "g", P + "ldsc3_%d" % b, 16)
                    if b >= 2:
                        gws(b - 2)
                        htt(b - 2)
                else:
                    if b == 0:
                        p.wait(gp, P + "ldxn")
                    if b >= 2:
                        p.wait(gp, P + "mm_%d" % (b - 2))
                    p.mark(gp.dma_start(out=w3t[:, b % 2, :], in_=d_w3t[b]),
                           "g", P + "ldt3_%d" % b, 16)
                    if b >= 2:
                        htt(b - 2)
            if mc == 0:
                gws(HB - 2)
                gws(HB - 1)
            htt(HB - 2)
            htt(HB - 1)

        def vec_f(vector):
            p.gate(vector, FIN)
            for b in range(HB):
                if mc == 0:
                    p.wait(vector, P + "ldw1_%d" % b)
                    p.wait(vector, P + "ldsc1_%d" % b)
                    p.wait(vector, P + "ldw3_%d" % b)
                    p.wait(vector, P + "ldsc3_%d" % b)
                    pk1v = pkr[:, 0, :].rearrange("p (q j) -> p q j", q=32)
                    pk3v = pkr[:, 1, :].rearrange("p (q j) -> p q j", q=32)
                    m1 = nat1[:].rearrange("p (q n j) -> p q n j", q=32, n=2)
                    m3 = nat3[:].rearrange("p (q n j) -> p q n j", q=32, n=2)
                    s1v = sc1[:, b % 2, :].rearrange("p (q n) -> p q n", n=2)
                    s3v = sc3[:, b % 2, :].rearrange("p (q n) -> p q n", n=2)
                    sc1v = sci1[:].rearrange("p (q j) -> p q j", q=16)
                    sc3v = sci3[:].rearrange("p (q j) -> p q j", q=16)
                    for h in (0, 1):
                        q0, q1 = 16 * h, 16 * h + 16
                        if b >= 1:
                            p.wait(vector, P + "tr1_%d_%d" % (b - 1, h))
                            p.wait(vector, P + "tr3_%d_%d" % (b - 1, h))
                        vector.tensor_scalar(
                            out=sc1v, in0=pk1v[:, q0:q1, :], scalar1=4,
                            scalar2=None, op0=ALU.arith_shift_right)
                        vector.tensor_scalar(
                            out=sc3v, in0=pk3v[:, q0:q1, :], scalar1=4,
                            scalar2=None, op0=ALU.arith_shift_right)
                        vector.tensor_mul(
                            out=m1[:, q0:q1, 0, :], in0=sc1v,
                            in1=s1v[:, q0:q1, 0:1].broadcast_to([128, 16, 64]))
                        vector.tensor_mul(
                            out=m3[:, q0:q1, 0, :], in0=sc3v,
                            in1=s3v[:, q0:q1, 0:1].broadcast_to([128, 16, 64]))
                        ax1 = vector.tensor_scalar(
                            out=sc1v, in0=pk1v[:, q0:q1, :], scalar1=15,
                            scalar2=8, op0=ALU.bitwise_and, op1=ALU.bitwise_xor)
                        if h == 1:
                            p.mark(ax1, "v", P + "dqv1_%d" % b)
                        ax3 = vector.tensor_scalar(
                            out=sc3v, in0=pk3v[:, q0:q1, :], scalar1=15,
                            scalar2=8, op0=ALU.bitwise_and, op1=ALU.bitwise_xor)
                        if h == 1:
                            p.mark(ax3, "v", P + "dqv3_%d" % b)
                        p.mark(vector.scalar_tensor_tensor(
                            out=m1[:, q0:q1, 1, :], in0=sc1v, scalar=-8.0,
                            in1=s1v[:, q0:q1, 1:2].broadcast_to([128, 16, 64]),
                            op0=ALU.add, op1=ALU.mult),
                            "v", P + "dqh1_%d_%d" % (b, h))
                        p.mark(vector.scalar_tensor_tensor(
                            out=m3[:, q0:q1, 1, :], in0=sc3v, scalar=-8.0,
                            in1=s3v[:, q0:q1, 1:2].broadcast_to([128, 16, 64]),
                            op0=ALU.add, op1=ALU.mult),
                            "v", P + "dqh3_%d_%d" % (b, h))


        def ten_f(tensor):
            p.gate(tensor, FIN)
            p.wait(tensor, P + "ldxn")
            rid = [0]
            rev = {}

            def tr_half(b, which, h):
                nat = nat1 if which == 1 else nat3
                p.wait(tensor, P + "dqh%d_%d_%d" % (which, b, h))
                for r in (2 * h, 2 * h + 1):
                    k = rid[0]
                    rid[0] += 1
                    if k >= 2:
                        p.wait(tensor, rev[k - 2])
                    rev[k] = P + "wev%d_%d_%d" % (which, b, r)
                    last = None
                    for u in range(8):
                        q = r * 8 + u
                        last = tensor.transpose(
                            psB[:, (k % 2) * 1024 + u * 128:(k % 2) * 1024 + (u + 1) * 128],
                            nat[:, q * 128:(q + 1) * 128], ident[:])
                    p.mark(last, "t", P + "wtr%d_%d_%d" % (which, b, r))
                self.evhere(P + "tr%d_%d_%d" % (which, b, h), "t")

            def tr_mat_pair(b):
                for h in (0, 1):
                    tr_half(b, 1, h)
                    tr_half(b, 3, h)

            def mms(b):
                if mc == 0:
                    p.wait(tensor, P + "wev1_%d_3" % b)
                    p.wait(tensor, P + "wev3_%d_3" % b)
                else:
                    p.wait(tensor, P + "ldt1_%d" % b)
                    p.wait(tensor, P + "ldt3_%d" % b)
                if b >= 2:
                    p.wait(tensor, P + "abev_%d" % (b - 2))
                reg_a = (b % 2) * 1024
                reg_b = (b % 2) * 1024 + 512
                last = None
                for dd in range(DB):
                    last = tensor.matmul(psA[:, reg_a:reg_a + 512],
                                         w1t[:, b % 2, dd * 128:(dd + 1) * 128],
                                         xnsl[:, dd, :],
                                         start=(dd == 0), stop=(dd == DB - 1))
                for dd in range(DB):
                    last = tensor.matmul(psA[:, reg_b:reg_b + 512],
                                         w3t[:, b % 2, dd * 128:(dd + 1) * 128],
                                         xnsl[:, dd, :],
                                         start=(dd == 0), stop=(dd == DB - 1))
                p.mark(last, "t", P + "mm_%d" % b)

            if mc == 0:
                for b in range(HB):
                    tr_mat_pair(b)
                    if b >= 1:
                        mms(b - 1)
                mms(85)
            else:
                for b in range(HB):
                    mms(b)

        def sca_f(scalar):
            p.gate(scalar, FIN)
            for b in range(HB):
                if mc == 0:
                    kq = b * 8
                    for h in (0, 1):
                        for which in (1, 3):
                            wdst = w1t if which == 1 else w3t
                            for r in (2 * h, 2 * h + 1):
                                p.wait(scalar, P + "wtr%d_%d_%d" % (which, b, r))
                                if b >= 2 and h == 0 and r == 0:
                                    p.wait(scalar, P + "mm_%d" % (b - 2))
                                    p.wait(scalar, P + "gw%d_%d" % (which, b - 2))
                                p.mark(scalar.activation(
                                    out=wdst[:, b % 2, r * 1024:(r + 1) * 1024],
                                    in_=psB[:, (kq % 2) * 1024:(kq % 2) * 1024 + 1024],
                                    func=AF.Identity, bias=0.0, scale=1.0),
                                    "s", P + "wev%d_%d_%d" % (which, b, r))
                                kq += 1
                if b >= 1:
                    bb = b - 1
                    p.wait(scalar, P + "mm_%d" % bb)
                    if bb >= 2:
                        p.wait(scalar, P + "hTT_%d" % (bb - 2))
                    reg_a = (bb % 2) * 1024
                    reg_b = (bb % 2) * 1024 + 512
                    scalar.activation(out=asil[:, bb % 2, :],
                                      in_=psA[:, reg_a:reg_a + 512],
                                      func=AF.Silu, bias=0.0, scale=1.0)
                    p.mark(scalar.activation(out=bsb[:, bb % 2, :],
                                             in_=psA[:, reg_b:reg_b + 512],
                                             func=AF.Identity, bias=0.0, scale=1.0),
                           "s", P + "abev_%d" % bb)
            p.wait(scalar, P + "mm_85")
            p.wait(scalar, P + "hTT_83")
            scalar.activation(out=asil[:, 1, :], in_=psA[:, 1024:1536],
                              func=AF.Silu, bias=0.0, scale=1.0)
            p.mark(scalar.activation(out=bsb[:, 1, :], in_=psA[:, 1536:2048],
                                     func=AF.Identity, bias=0.0, scale=1.0),
                   "s", P + "abev_85")

        blk.sync(sync_f)
        blk.gpsimd(gpl_f)
        blk.vector(vec_f)
        blk.tensor(ten_f)
        blk.scalar(sca_f)

        fin = [P + "mm_85", P + "hTT_85", P + "abev_85", P + "ldxn"]
        if mc == 0:
            fin += [P + "gw1_85", P + "gw3_85", P + "ldw3_85"]
        else:
            fin += [P + "ldt3_85"]
        return fin

    def emit_mm2(self, mc, FIN, d_w2, d_s2, d_w2t, hT, pk2, sci2, nat2, w2t,
                 sc2, gT, psA, psB, ident):
        p = self
        blk = self.blk
        taps = self.taps
        P = "m2%d_" % mc
        d_w2v = d_w2.rearrange("(a q) k -> a q k", q=HB)

        def sync_f(sync):
            p.gate(sync, FIN)
            for k in range(DB):
                if mc == 0:
                    if k >= 1:
                        p.wait(sync, P + "dqv_%d_0" % (k - 1))
                    p.mark(sync.dma_start(
                        out=pk2[:, 0, :].rearrange("p (q j) -> p q j", q=43),
                        in_=d_w2v[k * 128:(k + 1) * 128, 0:43, :]),
                        "q", P + "ldp_%d_0" % k, 16)
                    if k >= 2:
                        p.wait(sync, P + "dq_%d_1" % (k - 2))
                    p.mark(sync.dma_start(
                        out=sc2[:, k % 2, :],
                        in_=d_s2[k * 128 * 172:(k + 1) * 128 * 172]
                        .rearrange("(p g) -> p g", p=128)),
                        "q", P + "lds_%d" % k, 16)
                else:
                    if k >= 2:
                        p.wait(sync, P + "mm_%d" % (k - 2))
                    p.mark(sync.dma_start(out=w2t[:, k % 2, 0:5504],
                                          in_=d_w2t[k][:, 0:5504]),
                           "q", P + "ldt_%d_0" % k, 16)

        def gpl_f(gp):
            p.gate(gp, FIN)

            def gws(k):
                p.wait(gp, P + "wev_%d_1_5" % k)
                p.mark(gp.dma_start(out=d_w2t[k], in_=w2t[:, k % 2, :]),
                       "g", P + "gw_%d" % k, 16)

            for k in range(DB):
                if mc == 0:
                    if k >= 1:
                        p.wait(gp, P + "dqv_%d_1" % (k - 1))
                    p.mark(gp.dma_start(
                        out=pk2[:, 1, :].rearrange("p (q j) -> p q j", q=43),
                        in_=d_w2v[k * 128:(k + 1) * 128, 43:86, :]),
                        "g", P + "ldp_%d_1" % k, 16)
                    if k >= 2:
                        gws(k - 2)
                else:
                    if k == 0:
                        p.wait(gp, P + "ldt_0_0")
                    if k >= 2:
                        p.wait(gp, P + "mm_%d" % (k - 2))
                    p.mark(gp.dma_start(out=w2t[:, k % 2, 5504:11008],
                                        in_=d_w2t[k][:, 5504:11008]),
                           "g", P + "ldt_%d_1" % k, 16)
            if mc == 0:
                gws(DB - 2)
                gws(DB - 1)
            if DEBUG_TAPS and mc == 0:
                p.wait(gp, "m10_hTT_85")
                p.mark(gp.dma_start(out=taps["hT"],
                                    in_=hT[:].rearrange("p b m -> p (b m)")),
                       "g", None, 16)

        def vec_f(vector):
            p.gate(vector, FIN)
            if mc != 0:
                return
            for k in range(DB):
                for cc in range(2):
                    p.wait(vector, P + "ldp_%d_%d" % (k, cc))
                    p.wait(vector, P + "lds_%d" % k)
                    pkv = pk2[:, cc, :].rearrange("p (q j) -> p q j", q=43)
                    mo = nat2[:].rearrange("p (q n j) -> p q n j", q=43, n=2)
                    sv = sc2[:, k % 2, 86 * cc:86 * (cc + 1)].rearrange(
                        "p (q n) -> p q n", n=2)
                    sci0 = sci2[:, 0:1536].rearrange("p (q j) -> p q j", q=24)
                    sci1_ = sci2[:, 0:1216].rearrange("p (q j) -> p q j", q=19)
                    pk_, pc_ = (k, 0) if cc == 1 else (k - 1, 1)
                    if k > 0 or cc > 0:
                        p.wait(vector, P + "trh_%d_%d_%d" % (pk_, pc_, 0))
                    vector.tensor_scalar(
                        out=sci0, in0=pkv[:, 0:24, :], scalar1=4,
                        scalar2=None, op0=ALU.arith_shift_right)
                    self.vsp(vector, 1)
                    vector.tensor_mul(
                        out=mo[:, 0:24, 0, :], in0=sci0,
                        in1=sv[:, 0:24, 0:1].broadcast_to([128, 24, 64]))
                    vector.tensor_scalar(
                        out=sci0, in0=pkv[:, 0:24, :],
                        scalar1=15, scalar2=8,
                        op0=ALU.bitwise_and, op1=ALU.bitwise_xor)
                    self.vsp(vector, 1)
                    p.mark(vector.scalar_tensor_tensor(
                        out=mo[:, 0:24, 1, :], in0=sci0, scalar=-8.0,
                        in1=sv[:, 0:24, 1:2].broadcast_to([128, 24, 64]),
                        op0=ALU.add, op1=ALU.mult),
                        "v", P + "dqh_%d_%d_0" % (k, cc))
                    if k > 0 or cc > 0:
                        p.wait(vector, P + "trh_%d_%d_%d" % (pk_, pc_, 1))
                    vector.tensor_scalar(
                        out=sci1_, in0=pkv[:, 24:43, :], scalar1=4,
                        scalar2=None, op0=ALU.arith_shift_right)
                    self.vsp(vector, 1)
                    vector.tensor_mul(
                        out=mo[:, 24:43, 0, :], in0=sci1_,
                        in1=sv[:, 24:43, 0:1].broadcast_to([128, 19, 64]))
                    p.mark(vector.tensor_scalar(
                        out=sci1_, in0=pkv[:, 24:43, :],
                        scalar1=15, scalar2=8,
                        op0=ALU.bitwise_and, op1=ALU.bitwise_xor),
                        "v", P + "dqv_%d_%d" % (k, cc))
                    self.vsp(vector, 1)
                    p.mark(vector.scalar_tensor_tensor(
                        out=mo[:, 24:43, 1, :], in0=sci1_, scalar=-8.0,
                        in1=sv[:, 24:43, 1:2].broadcast_to([128, 19, 64]),
                        op0=ALU.add, op1=ALU.mult),
                        "v", P + "dqh_%d_%d_1" % (k, cc))
                    self.evhere(P + "dq_%d_%d" % (k, cc), "v")

        def ten_f(tensor):
            p.gate(tensor, FIN)
            p.wait(tensor, "m1%d_hTT_85" % mc)
            rid = [0]
            rev = {}

            def trs(k):
                for cc in range(2):
                    for h in (0, 1):
                        p.wait(tensor, P + "dqh_%d_%d_%d" % (k, cc, h))
                        for r in (range(0, 3) if h == 0 else range(3, 6)):
                            nb_r = min(8, 43 - r * 8)
                            kk = rid[0]
                            rid[0] += 1
                            if kk >= 2:
                                p.wait(tensor, rev[kk - 2])
                            rev[kk] = P + "wev_%d_%d_%d" % (k, cc, r)
                            last = None
                            for u in range(nb_r):
                                q = r * 8 + u
                                last = tensor.transpose(
                                    psB[:, (kk % 2) * 1024 + u * 128:
                                        (kk % 2) * 1024 + (u + 1) * 128],
                                    nat2[:, q * 128:(q + 1) * 128], ident[:])
                            p.mark(last, "t", P + "wtr_%d_%d_%d" % (k, cc, r))
                        self.evhere(P + "trh_%d_%d_%d" % (k, cc, h), "t")

            def mms(k):
                if mc == 0:
                    p.wait(tensor, P + "wev_%d_1_5" % k)
                else:
                    p.wait(tensor, P + "ldt_%d_0" % k)
                    p.wait(tensor, P + "ldt_%d_1" % k)
                if k >= 2:
                    p.wait(tensor, P + "gev_%d" % (k - 2))
                reg = (k % 2) * 512
                last = None
                for c in range(HB):
                    last = tensor.matmul(psA[:, reg:reg + 512],
                                         w2t[:, k % 2, c * 128:(c + 1) * 128],
                                         hT[:, c, :],
                                         start=(c == 0), stop=(c == HB - 1))
                p.mark(last, "t", P + "mm_%d" % k)

            if mc == 0:
                for k in range(DB):
                    trs(k)
                    if k >= 1:
                        mms(k - 1)
                mms(DB - 1)
            else:
                for k in range(DB):
                    mms(k)

        def sca_f(scalar):
            p.gate(scalar, FIN)

            def gev(k):
                p.wait(scalar, P + "mm_%d" % k)
                reg = (k % 2) * 512
                p.mark(scalar.activation(out=gT[:, k, :], in_=psA[:, reg:reg + 512],
                                         func=AF.Silu, bias=0.0, scale=1.0),
                       "s", P + "gev_%d" % k)

            for k in range(DB):
                if mc == 0:
                    for cc in range(2):
                        for r in range(6):
                            nb_r = min(8, 43 - r * 8)
                            p.wait(scalar, P + "wtr_%d_%d_%d" % (k, cc, r))
                            if k >= 2 and cc == 0 and r == 0:
                                p.wait(scalar, P + "mm_%d" % (k - 2))
                                p.wait(scalar, P + "gw_%d" % (k - 2))
                            kk = k * 12 + cc * 6 + r
                            p.mark(scalar.activation(
                                out=w2t[:, k % 2,
                                        (cc * 43 + r * 8) * 128:
                                        (cc * 43 + r * 8 + nb_r) * 128],
                                in_=psB[:, (kk % 2) * 1024:(kk % 2) * 1024 + nb_r * 128],
                                func=AF.Identity, bias=0.0, scale=1.0),
                                "s", P + "wev_%d_%d_%d" % (k, cc, r))
                    if k >= 1:
                        gev(k - 1)
                else:
                    gev(k)
            if mc == 0:
                gev(DB - 1)

        blk.sync(sync_f)
        blk.gpsimd(gpl_f)
        blk.vector(vec_f)
        blk.tensor(ten_f)
        blk.scalar(sca_f)

        fin = [P + "mm_31", P + "gev_31"]
        if mc == 0:
            fin += [P + "gw_31", P + "dq_31_1", P + "lds_31"]
        else:
            fin += [P + "ldt_31_0", P + "ldt_31_1"]
        return fin

    def emit_combine(self, mc, FIN, d_sT, d_xown, d_out, gT, ssl, xcf,
                     outst, psB, ident):
        p = self
        blk = self.blk
        taps = self.taps
        P = "cb%d_" % mc

        def sync_f(sync):
            p.gate(sync, FIN)
            p.mark(sync.dma_start(
                out=ssl[:],
                in_=d_sT.rearrange("p (b m) -> p b m", b=DB)[:, :, mc * 512:(mc + 1) * 512]),
                "q", P + "lds", 16)
            for t in range(4):
                if t >= 2:
                    p.wait(sync, P + "add_%d_7" % (t - 2))
                p.mark(sync.dma_start(
                    out=xcf[:, t % 2, :],
                    in_=d_xown[(mc * 4 + t) * 128:(mc * 4 + t + 1) * 128, :]),
                    "q", P + "ldx_%d" % t, 16)

        def vec_f(vector):
            p.gate(vector, FIN)
            p.wait(vector, P + "lds")
            for c in range(8):
                p.mark(vector.tensor_mul(out=gT[:, c * 4:(c + 1) * 4, :],
                                         in0=gT[:, c * 4:(c + 1) * 4, :],
                                         in1=ssl[:, c * 4:(c + 1) * 4, :]),
                       "v", P + "ptt_%d" % c)
            for t in range(4):
                p.wait(vector, P + "ldx_%d" % t)
                for rr in range(8):
                    p.wait(vector, P + "ctr_%d_%d" % (t, rr))
                    if rr >= 2:
                        p.wait(vector, P + "out_%d_%d" % (t, rr - 2))
                    elif t >= 1:
                        p.wait(vector, P + "out_%d_%d" % (t - 1, 6 + rr))
                    p.mark(vector.tensor_add(
                        out=outst[:, rr % 2, :],
                        in0=psB[:, (rr % 2) * 1024:(rr % 2) * 1024 + 512],
                        in1=xcf[:, t % 2, rr * 512:(rr + 1) * 512]),
                        "v", P + "add_%d_%d" % (t, rr))

        def ten_f(tensor):
            p.gate(tensor, FIN)
            for t in range(4):
                for rr in range(8):
                    p.wait(tensor, P + "ptt_%d" % rr)
                    rid = t * 8 + rr
                    if rid >= 2:
                        p.wait(tensor, P + "add_%d_%d" % ((rid - 2) // 8, (rid - 2) % 8))
                    last = None
                    for u in range(4):
                        k = rr * 4 + u
                        last = tensor.transpose(
                            psB[:, (rr % 2) * 1024 + u * 128:(rr % 2) * 1024 + (u + 1) * 128],
                            gT[:, k, t * 128:(t + 1) * 128], ident[:])
                    p.mark(last, "t", P + "ctr_%d_%d" % (t, rr))

        def gp_f(gp):
            p.gate(gp, FIN)
            for t in range(4):
                for rr in range(8):
                    p.wait(gp, P + "add_%d_%d" % (t, rr))
                    p.mark(gp.dma_start(
                        out=d_out[(mc * 4 + t) * 128:(mc * 4 + t + 1) * 128,
                                  rr * 512:(rr + 1) * 512],
                        in_=outst[:, rr % 2, :]),
                        "g", P + "out_%d_%d" % (t, rr), 16)
            if DEBUG_TAPS and mc == 0:
                p.wait(gp, "m20_gev_31")
                p.mark(gp.dma_start(out=taps["gT"],
                                    in_=gT[:].rearrange("p b m -> p (b m)")),
                       "g", None, 16)

        blk.sync(sync_f)
        blk.vector(vec_f)
        blk.tensor(ten_f)
        blk.gpsimd(gp_f)

        return [P + "ctr_3_7", P + "add_3_7", P + "out_3_7", P + "ldx_3"]


def build_program():
    b1 = Builder(events=None)
    b1.build()
    b2 = Builder(events=b1.rec)
    nc = b2.build()
    assert b2.cnt == b1.cnt, (b1.cnt, b2.cnt)
    return nc


_NC_CACHE = {}
_LAST_RES = {}


def _get_nc():
    if "nc" not in _NC_CACHE:
        _NC_CACHE["nc"] = build_program()
    return _NC_CACHE["nc"]


def _make_mask(par):
    q = np.arange(128)[:, None]
    k = np.arange(128)[None, :]
    tri = np.where(k <= q, 0.0, -30000.0).astype(np.float32)
    m = np.zeros((128, 512), np.float32)
    if par == 0:
        m[:, 256:384] = tri
        m[:, 384:512] = -30000.0
    else:
        m[:, 384:512] = tri
    return np.ascontiguousarray(m.astype(bfdt))


def kernel(L0_attn_x, L0_attn_wq_w, L0_attn_wq_b, L0_attn_wk_w, L0_attn_wk_b,
           L0_attn_wv_w, L0_attn_wv_b, L0_attn_wo_w, L0_attn_wo_b,
           L0_ffn_w1, L0_ffn_s1, L0_ffn_w2, L0_ffn_s2, L0_ffn_w3, L0_ffn_s3):
    x = np.ascontiguousarray(np.asarray(L0_attn_x, dtype=np.float32))
    wqT = np.ascontiguousarray(np.asarray(L0_attn_wq_w, np.float32).T)
    wkT = np.ascontiguousarray(np.asarray(L0_attn_wk_w, np.float32).T)
    wvT = np.ascontiguousarray(np.asarray(L0_attn_wv_w, np.float32).T)
    woT = np.ascontiguousarray(np.asarray(L0_attn_wo_w, np.float32).T)
    wqb = np.ascontiguousarray(np.asarray(L0_attn_wq_b, np.float32).reshape(32, 128).T)
    wob = np.ascontiguousarray(np.asarray(L0_attn_wo_b, np.float32).reshape(32, 128).T)
    wkb = np.ascontiguousarray(np.asarray(L0_attn_wk_b, np.float32).reshape(128, 1))
    wvb = np.ascontiguousarray(np.asarray(L0_attn_wv_b, np.float32).reshape(128, 1))
    w1 = np.ascontiguousarray(np.asarray(L0_ffn_w1, np.int32))
    w2 = np.ascontiguousarray(np.asarray(L0_ffn_w2, np.int32))
    w3 = np.ascontiguousarray(np.asarray(L0_ffn_w3, np.int32))
    s1 = np.ascontiguousarray(np.asarray(L0_ffn_s1, np.float32))
    s2 = np.ascontiguousarray(np.asarray(L0_ffn_s2, np.float32))
    s3 = np.ascontiguousarray(np.asarray(L0_ffn_s3, np.float32))

    masks = [_make_mask(0), _make_mask(1)]
    in_maps = []
    for c in range(8):
        b, par = c // 2, c % 2
        xb = np.ascontiguousarray(x[b])
        xown = np.ascontiguousarray(xb.reshape(16, 128, D)[par::2].reshape(M, D))
        in_maps.append(dict(
            xkv=xb, xown=xown, wqT=wqT, wkT=wkT, wvT=wvT, woT=woT,
            wqb=wqb, wob=wob, wkb=wkb, wvb=wvb,
            w1p=w1, w2p=w2, w3p=w3, s1=s1, s2=s2, s3=s3,
            mask=masks[par]))

    nc = _get_nc()
    res = run_bass_kernel_spmd(nc, in_maps, core_ids=list(range(8)))
    _LAST_RES["res"] = res

    out = np.empty((B, T, D), np.float32)
    for c in range(8):
        b, par = c // 2, c % 2
        oc = np.asarray(res.results[c]["out"]).reshape(NSLOT, 128, D)
        out.reshape(B, 16, 128, D)[b, par::2] = oc
    return out
